# version 01
# baseline (speedup 1.0000x reference)
import numpy as np
import contextlib
import concourse.bass as bass
import concourse.mybir as mybir

F32 = mybir.dt.float32
BF16 = mybir.dt.bfloat16
I32 = mybir.dt.int32
AF = mybir.ActivationFunctionType
ALU = mybir.AluOpType
AX = mybir.AxisListType

ENGINES = ("sync", "vector", "scalar", "tensor", "gpsimd")

D = 1024
NCH = 8
SEQ = 2048
NSEQ = 2
TOK = SEQ * NSEQ
NH = 16
DK = 64
TB = 512
NB = TOK // TB



def lhsT_pack(W):
    din, dout = W.shape
    k, m = din // 128, dout // 128
    A = W.reshape(k, 128, m, 128)
    A = np.ascontiguousarray(A.transpose(2, 1, 0, 3))
    return A.reshape(m, 128, din)


def col_pack(v):
    n = v.shape[0]
    return np.ascontiguousarray(v.reshape(n // 128, 128).T)



class Builder:
    def __init__(self, nc, name="prog"):
        self.nc = nc
        self.name = name
        self.ops = {e: [] for e in ENGINES}
        self.sem_handles = {}
        self.sem_cnt = {}
        self.es = contextlib.ExitStack()

    def sem(self, name):
        if name not in self.sem_handles:
            self.sem_handles[name] = self.es.enter_context(self.nc.semaphore(name))
            self.sem_cnt[name] = 0
        return self.sem_handles[name]

    def bump(self, name, n):
        self.sem(name)
        self.sem_cnt[name] += n
        assert self.sem_cnt[name] < 63000, f"sem {name} overflow"
        return self.sem_cnt[name]

    def val(self, name):
        self.sem(name)
        return self.sem_cnt[name]

    def emit(self, eng, maker, inc=None, n=1):
        if inc is not None:
            v = self.bump(inc, n)
            h = self.sem_handles[inc]
            def run(E, maker=maker, h=h, n=n):
                maker(E).then_inc(h, n)
            self.ops[eng].append(run)
            return v
        self.ops[eng].append(lambda E, maker=maker: maker(E))
        return None

    def wait(self, eng, sem, v=None):
        if v is None:
            v = self.val(sem)
        if v <= 0:
            return
        h = self.sem(sem)
        self.ops[eng].append(lambda E, h=h, v=v: E.wait_ge(h, v))

    def drain(self, eng):
        self.ops[eng].append(lambda E: E.drain())

    def grace(self, eng="tensor", cyc=3000):
        self.ops[eng].append(lambda E: E.nop(cycle_cnt=cyc))

    def sbuf(self, name, shape, dtype):
        return self.es.enter_context(self.nc.sbuf_tensor(name, list(shape), dtype))

    def psum(self, name, shape=(128, 512), dtype=F32):
        return self.es.enter_context(self.nc.psum_tensor(name, list(shape), dtype))

    def finish(self):
        block = self.es.enter_context(self.nc.Block())
        for e in ENGINES:
            ops = self.ops[e]
            getattr(block, e)(lambda E, ops=ops: [f(E) for f in ops])
        self.es.close()
        return self.nc


class Guard:
    def __init__(self, b):
        self.b = b
        self.d = {}

    def wait(self, eng, *key):
        e = self.d.get(key)
        if e:
            self.b.wait(eng, e[0], e[1])

    def set(self, key, sem, val):
        self.d[key] = (sem, val)



class Ctx:
    def __init__(self, b: Builder):
        self.b = b
        self.g = Guard(b)
        self.ident = b.sbuf("ident", (128, 128), BF16)
        self.identf = b.sbuf("identf", (128, 128), F32)
        self.onesb = b.sbuf("onesb", (128, 1), BF16)
        self.onesrow = b.sbuf("onesrow", (1, 128), F32)
        self.PSALL = b.psum("PSALL", (128, 4096))
        self.P = [self.PSALL[:, i * 512:(i + 1) * 512] for i in range(8)]
        self.xf = [[b.sbuf(f"xf{s}_{c}", (128, TB), F32) for c in range(NCH)]
                   for s in range(2)]
        self.xb = [b.sbuf(f"xbc{c}", (128, TB), BF16) for c in range(NCH)]
        self.x2 = [b.sbuf(f"x2_{i}", (128, TB), BF16) for i in range(2)]
        self.tmp = [b.sbuf(f"tmp{c}", (128, TB), F32) for c in range(4)]
        self.rows = b.sbuf("rows", (1, 4 * TB), F32)
        self.dmy = b.sbuf("dmy", (128, 64), BF16)
        self.prep_f = [b.sbuf(f"prf{i}", (128, 512), F32) for i in range(2)]
        self.prep_b = [b.sbuf(f"prb{i}", (128, 512), BF16) for i in range(2)]
        self.consts = {}

    def alloc_xnq(self, nsets=2):
        b = self.b
        self.ffn_xnq = [[b.sbuf(f"fxn{s}_{c}", (128, TB), BF16) for c in range(NCH)]
                        for s in range(nsets)]

    def alloc_ffn(self):
        b = self.b
        self.alloc_xnq()
        self.ffn_hA = [b.sbuf(f"fhA{m}", (128, TB), BF16) for m in range(32)]
        self.ffn_hB = [b.sbuf(f"fhB{m}", (128, TB), BF16) for m in range(32)]
        self.ffn_wbuf = [b.sbuf(f"fw{i}", (128, 1024), BF16) for i in range(8)]
        self.ffn_hwbuf = [b.sbuf(f"fhw{i}", (128, 4096), BF16) for i in range(4)]
        self.ffn_t12 = [b.sbuf(f"ft12_{i}", (128, TB), F32) for i in range(2)]
        self.ffn_t1b = [b.sbuf(f"ft1b_{i}", (128, TB), BF16) for i in range(2)]

    def load_consts(self, spec):
        b = self.b
        for name, (src, shape, dtype) in spec.items():
            t = b.sbuf("c_" + name, shape, dtype)
            self.consts[name] = t
            b.emit("sync", lambda E, t=t, src=src:
                   E.dma_start(out=t[:], in_=src), inc="ld", n=16)

    def init_const_tiles(self, idn_ap):
        b = self.b
        b.emit("sync", lambda E, s=idn_ap, t=self.identf:
               E.dma_start(out=t[:], in_=s[:, :]), inc="ld", n=16)
        b.wait("vector", "ld", b.val("ld"))
        b.emit("vector", lambda E, t=self.ident, s=self.identf:
               E.tensor_copy(t[:], s[:]))
        b.emit("vector", lambda E, t=self.onesb: E.memset(t[:], 1.0))
        b.emit("vector", lambda E, t=self.onesrow: E.memset(t[:], 1.0), inc="v")
        b.drain("vector")


def dummy_loads(ctx, src2d, n=2):
    b = ctx.b
    for _ in range(n):
        b.emit("sync", lambda E, d=ctx.dmy, s=src2d:
               E.dma_start(out=d[:, 0:64], in_=s[0:128, 0:64]), inc="ld", n=16)
    return b.val("ld")


def use_bank(ctx, i):
    ctx.g.wait("tensor", "bank", i)


def free_bank(ctx, i, sem, val):
    ctx.g.set(("bank", i), sem, val)


def barrier(ctx):
    b = ctx.b
    for eng in ENGINES:
        for s in ("v", "s", "t", "st", "ld"):
            b.wait(eng, s, b.val(s))



def ln_phase(ctx, x_dram, xn_dram, u_dram=None, aff=None):
    b = ctx.b
    g = ctx.g
    EPS1 = 1e-5
    st_after = {}
    for j in range(NB):
        buf = ctx.xf[j % 2]
        for c in range(NCH):
            g.wait("sync", "xf", j % 2, c)
            b.emit("sync", lambda E, d=buf[c], s_=x_dram, c=c, j=j:
                   E.dma_start(out=d[:], in_=s_[c * 128:(c + 1) * 128,
                                               j * TB:(j + 1) * TB]),
                   inc="ld", n=16)
        ld_done = b.val("ld")
        b.wait("vector", "ld", ld_done)
        cast_done = []
        for c in range(NCH):
            g.wait("vector", "xb", c)
            cast_done.append(b.emit("vector", lambda E, d=ctx.xb[c], s_=buf[c]:
                                    E.tensor_copy(d[:], s_[:]), inc="v"))
        use_bank(ctx, 0)
        use_bank(ctx, 1)
        for c in range(NCH):
            b.wait("scalar", "v", cast_done[c])
            g.wait("scalar", "x2", c % 2)
            sq = b.emit("scalar", lambda E, d=ctx.x2[c % 2], s_=ctx.xb[c]:
                        E.activation(d[:], s_[:], AF.Square), inc="s")
            b.wait("tensor", "v", cast_done[c])
            b.emit("tensor", lambda E, p=ctx.P[0], s_=ctx.xb[c], f=(c == 0),
                   l=(c == NCH - 1), o=ctx.onesb:
                   E.matmul(p[0:1, :], o[:], s_[:], start=f, stop=l))
            b.wait("tensor", "s", sq)
            tv = b.emit("tensor", lambda E, p=ctx.P[1], s_=ctx.x2[c % 2], f=(c == 0),
                        l=(c == NCH - 1), o=ctx.onesb:
                        E.matmul(p[0:1, :], o[:], s_[:], start=f, stop=l),
                        inc="t")
            g.set(("x2", c % 2), "t", tv)
        sumx_t = b.val("t")
        sumsq_t = b.val("t")
        R = ctx.rows
        mu = R[0:1, 0:TB]
        s2r = R[0:1, TB:2 * TB]
        rst = R[0:1, 2 * TB:3 * TB]
        w1 = R[0:1, 3 * TB:4 * TB]
        b.wait("vector", "t", sumsq_t)
        b.emit("vector", lambda E, mu=mu, p=ctx.P[0]:
               E.tensor_scalar_mul(mu[:, :], p[0:1, :], 1.0 / D))
        b.drain("vector")
        b.emit("vector", lambda E, s2r=s2r, mu=mu:
               E.tensor_tensor(s2r[:, :], mu[:, :], mu[:, :], op=ALU.mult))
        b.drain("vector")
        b.emit("vector", lambda E, w1=w1, p=ctx.P[1], s2r=s2r:
               E.scalar_tensor_tensor(w1[:, :], p[0:1, :], 1.0 / D, s2r[:, :],
                                      op0=ALU.mult, op1=ALU.subtract))
        b.drain("vector")
        vrow = b.emit("vector", lambda E, rst=rst, w1=w1:
                      E.tensor_scalar_add(rst[:, :], w1[:, :], EPS1), inc="v")
        free_bank(ctx, 0, "v", vrow)
        free_bank(ctx, 1, "v", vrow)
        if aff is not None:
            b.wait("vector", "v", vrow)
            g.wait("vector", "tmp", 0)
            t0row = ctx.tmp[0][0:1, 0:TB]
            b.emit("vector", lambda E, t0=t0row, rst=rst:
                   E.reciprocal(t0[:, :], rst[:, :]))
            b.drain("vector")
            b.emit("vector", lambda E, s2r=s2r, w1=w1, t0=t0row:
                   E.tensor_tensor(s2r[:, :], w1[:, :], t0[:, :],
                                   op=ALU.mult))
            b.drain("vector")
            v2 = b.emit("vector", lambda E, s2r=s2r, e=aff["eps2"]:
                        E.tensor_scalar_add(s2r[:, :], s2r[:, :], e), inc="v")
        b.wait("scalar", "v", b.val("v"))
        b.emit("scalar", lambda E, rst=rst:
               E.activation(rst[:, :], rst[:, :], AF.Ln))
        b.drain("scalar")
        rst_s = b.emit("scalar", lambda E, rst=rst:
                       E.activation(rst[:, :], rst[:, :], AF.Exp, scale=-0.5),
                       inc="s")
        if aff is not None:
            b.emit("scalar", lambda E, s2r=s2r:
                   E.activation(s2r[:, :], s2r[:, :], AF.Ln))
            b.drain("scalar")
            s2s = b.emit("scalar", lambda E, s2r=s2r:
                         E.activation(s2r[:, :], s2r[:, :], AF.Exp, scale=-0.5),
                         inc="s")
            b.wait("vector", "s", s2s)
            rows_v = b.emit("vector", lambda E, s2r=s2r, rst=rst:
                            E.tensor_tensor(s2r[:, :], s2r[:, :], rst[:, :],
                                            op=ALU.mult), inc="v")
            rows_sem, rows_val = "v", rows_v
        else:
            rows_sem, rows_val = "s", rst_s
        b.wait("tensor", rows_sem, rows_val)
        nbk = 3 if u_dram is not None else 2
        rowsrc = [mu, (s2r if aff is not None else rst), rst]
        for bi in range(nbk):
            use_bank(ctx, 2 + bi)
        for bi in range(nbk):
            b.emit("tensor", lambda E, p=ctx.P[2 + bi], o=ctx.onesrow,
                   r=rowsrc[bi]:
                   E.matmul(p[:, :], o[:, :], r[:, :], start=True, stop=True),
                   inc=("t" if bi == nbk - 1 else None))
        bc_done = b.val("t")
        b.wait("vector", "t", bc_done)
        if aff is not None:
            b.wait("scalar", "t", bc_done)
        for c in range(NCH):
            g.wait("vector", "tmp", c % 4)
            b.emit("vector", lambda E, d=ctx.tmp[c % 4], x=buf[c], p=ctx.P[2]:
                   E.tensor_tensor(d[:], x[:], p[:, :], op=ALU.subtract))
            b.drain("vector")
            if u_dram is not None:
                b.emit("vector", lambda E, d=buf[c], tt=ctx.tmp[c % 4], p=ctx.P[4]:
                       E.tensor_tensor(d[:], tt[:], p[:, :], op=ALU.mult),
                       inc="v")
                b.drain("vector")
            if aff is not None:
                gam = ctx.consts[aff["gamma"]]
                bet = ctx.consts[aff["beta"]]
                b.emit("vector", lambda E, d=ctx.tmp[c % 4], p=ctx.P[3]:
                       E.tensor_tensor(d[:], d[:], p[:, :], op=ALU.mult))
                b.drain("vector")
                sv = b.emit("vector", lambda E, d=ctx.xb[c], tt=ctx.tmp[c % 4],
                            gam=gam, bet=bet, c=c:
                            E.tensor_scalar(d[:], tt[:], gam[:, c:c + 1],
                                            bet[:, c:c + 1], op0=ALU.mult,
                                            op1=ALU.add), inc="v")
                g.set(("tmp", c % 4), "v", sv)
            else:
                av = b.emit("vector", lambda E, d=ctx.xb[c], tt=ctx.tmp[c % 4],
                            p=ctx.P[3]:
                            E.tensor_tensor(d[:], tt[:], p[:, :], op=ALU.mult),
                            inc="v")
                g.set(("tmp", c % 4), "v", av)
        apply_v, apply_s = b.val("v"), b.val("s")
        free_bank(ctx, 2, "v", apply_v)
        if aff is not None:
            free_bank(ctx, 3, "v", apply_v)
        else:
            free_bank(ctx, 3, "v", apply_v)
        if u_dram is not None:
            free_bank(ctx, 4, "v", apply_v)
        b.wait("gpsimd", "v", apply_v)
        if aff is not None:
            b.wait("gpsimd", "s", apply_s)
        for c in range(NCH):
            b.emit("gpsimd", lambda E, s_=ctx.xb[c], d=xn_dram, c=c, j=j:
                   E.dma_start(out=d[c * 128:(c + 1) * 128, j * TB:(j + 1) * TB],
                               in_=s_[:]), inc="st", n=16)
            if u_dram is not None:
                b.emit("gpsimd", lambda E, s_=buf[c], d=u_dram, c=c, j=j:
                       E.dma_start(out=d[c * 128:(c + 1) * 128,
                                        j * TB:(j + 1) * TB], in_=s_[:]),
                       inc="st", n=16)
        st_after[j] = b.val("st")
        for c in range(NCH):
            g.set(("xb", c), "st", st_after[j])
            g.set(("xf", j % 2, c), "st", st_after[j])
    return st_after



def prep_cast(ctx, src2d, dst2d, nrows, ncols, scale=None):
    b = ctx.b
    g = ctx.g
    nt = nrows // 128
    ns = (ncols + 511) // 512
    for i in range(nt):
        for cs in range(ns):
            c0 = cs * 512
            n = min(512, ncols - c0)
            p = (i * ns + cs) % 2
            g.wait("gpsimd", "prep", p)
            b.emit("gpsimd", lambda E, d=ctx.prep_f[p], s=src2d, i=i, c0=c0, n=n:
                   E.dma_start(out=d[:, 0:n],
                               in_=s[i * 128:(i + 1) * 128, c0:c0 + n]),
                   inc="pld", n=16)
            b.wait("gpsimd", "pld", b.val("pld"))
            cv = b.emit("gpsimd", lambda E, d=ctx.prep_b[p], s=ctx.prep_f[p], n=n:
                        E.tensor_copy(d[:, 0:n], s[:, 0:n]), inc="pv")
            b.wait("gpsimd", "pv", cv)
            b.emit("gpsimd", lambda E, s=ctx.prep_b[p], d=dst2d, i=i, c0=c0, n=n:
                   E.dma_start(out=d[i * 128:(i + 1) * 128, c0:c0 + n],
                               in_=s[:, 0:n]),
                   inc="pst", n=16)
            g.set(("prep", p), "pst", b.val("pst"))
    return b.val("pst")



def _hid_mm_phase(ctx, Wdram, hin, hout, sc, in_ready, nkin, act=AF.Gelu):
    b = ctx.b
    g = ctx.g
    hwbuf = ctx.ffn_hwbuf
    wlds = {}
    def _ld(m):
        p = m % 4
        g.wait("sync", "hw", p)
        b.emit("sync", lambda E, d=hwbuf[p], s=Wdram, m=m, n=nkin * 128:
               E.dma_start(out=d[:, 0:n], in_=s[m * 128:(m + 1) * 128, :]),
               inc="ld", n=16)
        wlds[m] = b.val("ld")
    for m in range(3):
        _ld(m)
    for m in range(32):
        p = m % 4
        if m + 3 < 32:
            _ld(m + 3)
        elif m + 3 == 32:
            wlds[32] = dummy_loads(ctx, Wdram, 1)
            wlds[33] = dummy_loads(ctx, Wdram, 1)
        b.wait("tensor", "ld", wlds[min(m + 2, 33)])
        if m == 0:
            b.grace()
            b.wait("tensor", in_ready[0], in_ready[1])
        bank = 4 + m % 2
        use_bank(ctx, bank)
        for k in range(nkin):
            b.emit("tensor", lambda E, P=ctx.P[bank], w=hwbuf[p], x=hin[k], k=k:
                   E.matmul(P[:, :], w[:, k * 128:(k + 1) * 128], x[:],
                            start=(k == 0), stop=(k == nkin - 1)),
                   inc=("t" if k == nkin - 1 else None))
        mmv = b.val("t")
        g.set(("hw", p), "t", mmv)
        if m == 0:
            g.wait("scalar", "hfree", id(hout))
            g.wait("scalar", "hfreeS", id(hout))
        b.wait("scalar", "t", mmv)
        gv = b.emit("scalar", lambda E, d=hout[m], P=ctx.P[bank], sc=sc, m=m:
                    E.activation(d[:], P[:, :], act, scale=sc[:, m:m + 1]),
                    inc="s")
        free_bank(ctx, bank, "s", gv)
    g.set(("hfree", id(hin)), "t", b.val("t"))
    return ("s", b.val("s"))


def l1ffn(ctx, x_dram, xn_dram, xout_dram, W, ln_st):
    b = ctx.b
    g = ctx.g
    s1 = ctx.consts["l6_s1"]; s2 = ctx.consts["l6_s2"]
    s1h = ctx.consts["l6_s1h"]; s2h = ctx.consts["l6_s2h"]
    s3 = ctx.consts["l6_s3"]
    wbuf, hwbuf = ctx.ffn_wbuf, ctx.ffn_hwbuf
    t12, t1b = ctx.ffn_t12, ctx.ffn_t1b
    hA, hB = ctx.ffn_hA, ctx.ffn_hB
    for j in range(NB):
        xnq = ctx.ffn_xnq[j % 2]
        b.wait("sync", "st", ln_st[j])
        for c in range(NCH):
            g.wait("sync", "xnq", j % 2, c)
            b.emit("sync", lambda E, d=xnq[c], s=xn_dram, c=c, j=j:
                   E.dma_start(out=d[:], in_=s[c * 128:(c + 1) * 128,
                                              j * TB:(j + 1) * TB]),
                   inc="ld", n=16)
        wlds = {}
        def _ld_w12(m):
            p = m % 4
            g.wait("sync", "wb", p)
            b.emit("sync", lambda E, d=wbuf[2 * p], s=W["w1"], m=m:
                   E.dma_start(out=d[:], in_=s[m * 128:(m + 1) * 128, :]),
                   inc="ld", n=16)
            b.emit("sync", lambda E, d=wbuf[2 * p + 1], s=W["w2"], m=m:
                   E.dma_start(out=d[:], in_=s[m * 128:(m + 1) * 128, :]),
                   inc="ld", n=16)
            wlds[m] = b.val("ld")
        for m in range(3):
            _ld_w12(m)
        first_write = True
        for m in range(32):
            p = m % 4
            if m + 3 < 32:
                _ld_w12(m + 3)
            elif m + 3 == 32:
                wlds[32] = dummy_loads(ctx, W["w1"], 1)
                wlds[33] = dummy_loads(ctx, W["w1"], 1)
            b.wait("tensor", "ld", wlds[min(m + 2, 33)])
            if m == 0:
                b.grace()
            bank1, bank2 = m % 2, 2 + m % 2
            use_bank(ctx, bank1)
            for k in range(NCH):
                b.emit("tensor", lambda E, P=ctx.P[bank1], w=wbuf[2 * p],
                       x=xnq[k], k=k:
                       E.matmul(P[:, :], w[:, k * 128:(k + 1) * 128], x[:],
                                start=(k == 0), stop=(k == NCH - 1)),
                       inc=("t" if k == NCH - 1 else None))
            mm1 = b.val("t")
            use_bank(ctx, bank2)
            for k in range(NCH):
                b.emit("tensor", lambda E, P=ctx.P[bank2], w=wbuf[2 * p + 1],
                       x=xnq[k], k=k:
                       E.matmul(P[:, :], w[:, k * 128:(k + 1) * 128], x[:],
                                start=(k == 0), stop=(k == NCH - 1)),
                       inc=("t" if k == NCH - 1 else None))
            mm2 = b.val("t")
            g.set(("wb", p), "t", mm2)
            b.wait("scalar", "t", mm1)
            g.wait("scalar", "t1b", m % 2)
            if first_write:
                g.wait("scalar", "hfree", id(hA))
                g.wait("scalar", "hfreeS", id(hA))
            sv = b.emit("scalar", lambda E, d=t1b[m % 2], P=ctx.P[bank1],
                        s1=s1, m=m:
                        E.activation(d[:], P[:, :], AF.Silu,
                                     scale=s1[:, m:m + 1]), inc="s")
            free_bank(ctx, bank1, "s", sv)
            b.wait("vector", "t", mm2)
            if first_write:
                g.wait("vector", "hfree", id(hA))
                g.wait("vector", "hfreeS", id(hA))
                first_write = False
            b.emit("vector", lambda E, d=t12[m % 2], P=ctx.P[bank2], s2=s2, m=m:
                   E.tensor_scalar_mul(d[:], P[:, :], s2[:, m:m + 1]))
            b.drain("vector")
            b.wait("vector", "s", sv)
            hv = b.emit("vector", lambda E, d=hA[m], a=t1b[m % 2], c_=t12[m % 2]:
                        E.tensor_tensor(d[:], a[:], c_[:], op=ALU.mult),
                        inc="v")
            free_bank(ctx, bank2, "v", hv)
            g.set(("t1b", m % 2), "v", hv)
        for c in range(NCH):
            g.set(("xnq", j % 2, c), "t", b.val("t"))
        r1 = _hid_mm_phase(ctx, W["h1"], hA, hB, s1h, ("v", b.val("v")), 32)
        r2 = _hid_mm_phase(ctx, W["h2"], hB, hA, s2h, r1, 32)
        xbuf = ctx.xf[j % 2]
        for c in range(NCH):
            g.wait("sync", "xf", j % 2, c)
            b.emit("sync", lambda E, d=xbuf[c], s=x_dram, c=c, j=j:
                   E.dma_start(out=d[:], in_=s[c * 128:(c + 1) * 128,
                                              j * TB:(j + 1) * TB]),
                   inc="ld", n=16)
        x_ld = b.val("ld")
        wlds3 = {}
        def _ld_w3(m):
            p = m % 4
            g.wait("sync", "hw", p)
            b.emit("sync", lambda E, d=hwbuf[p], s=W["w3"], m=m:
                   E.dma_start(out=d[:], in_=s[m * 128:(m + 1) * 128, :]),
                   inc="ld", n=16)
            wlds3[m] = b.val("ld")
        for m in range(3):
            _ld_w3(m)
        for m in range(NCH):
            p = m % 4
            if m + 3 < NCH:
                _ld_w3(m + 3)
            elif m + 3 == NCH:
                wlds3[NCH] = dummy_loads(ctx, W["w3"], 1)
                wlds3[NCH + 1] = dummy_loads(ctx, W["w3"], 1)
            b.wait("tensor", "ld", wlds3[min(m + 2, NCH + 1)])
            if m == 0:
                b.grace()
                b.wait("tensor", r2[0], r2[1])
            bank = 4 + m % 2
            use_bank(ctx, bank)
            for k in range(32):
                b.emit("tensor", lambda E, P=ctx.P[bank], w=hwbuf[p],
                       x=hA[k], k=k:
                       E.matmul(P[:, :], w[:, k * 128:(k + 1) * 128], x[:],
                                start=(k == 0), stop=(k == 31)),
                       inc=("t" if k == 31 else None))
            mmv = b.val("t")
            g.set(("hw", p), "t", mmv)
            b.wait("scalar", "t", mmv)
            g.wait("scalar", "tmp", m % 4)
            ov = b.emit("scalar", lambda E, d=ctx.tmp[m % 4], P=ctx.P[bank],
                        s3=s3, m=m:
                        E.activation(d[:], P[:, :], AF.Copy,
                                     scale=s3[:, m:m + 1]), inc="s")
            free_bank(ctx, bank, "s", ov)
            b.wait("vector", "s", ov)
            b.wait("vector", "ld", x_ld)
            rv = b.emit("vector", lambda E, d=xbuf[m], t=ctx.tmp[m % 4]:
                        E.tensor_tensor(d[:], d[:], t[:], op=ALU.add), inc="v")
            g.set(("tmp", m % 4), "v", rv)
            b.wait("gpsimd", "v", rv)
            stv = b.emit("gpsimd", lambda E, s=xbuf[m], d=xout_dram, m=m, j=j:
                         E.dma_start(out=d[m * 128:(m + 1) * 128,
                                          j * TB:(j + 1) * TB], in_=s[:]),
                         inc="st", n=16)
            g.set(("xf", j % 2, m), "st", stv)
        g.set(("hfree", id(hA)), "t", b.val("t"))
    return b.val("st")


def prep_dequant(ctx, q2d, scale2d, dst2d, nrows, ncols):
    b = ctx.b
    g = ctx.g
    nt = nrows // 128
    ns = (ncols + 511) // 512
    for i in range(nt):
        for cs in range(ns):
            c0 = cs * 512
            n = min(512, ncols - c0)
            p = (i * ns + cs) % 2
            g.wait("gpsimd", "prep", p)
            b.emit("gpsimd", lambda E, d=ctx.prep_q[p], s=q2d, i=i, c0=c0, n=n:
                   E.dma_start(out=d[:, 0:n],
                               in_=s[i * 128:(i + 1) * 128, c0:c0 + n]),
                   inc="pld", n=16)
            b.emit("gpsimd", lambda E, d=ctx.prep_f[p], s=scale2d, i=i, c0=c0, n=n:
                   E.dma_start(out=d[:, 0:n],
                               in_=s[i * 128:(i + 1) * 128, c0:c0 + n]),
                   inc="pld", n=16)
            b.wait("gpsimd", "pld", b.val("pld"))
            b.emit("gpsimd", lambda E, d=ctx.prep_f2[p], s=ctx.prep_q[p], n=n:
                   E.tensor_copy(d[:, 0:n], s[:, 0:n]))
            b.ops["gpsimd"].append(lambda E: E.drain())
            b.emit("gpsimd", lambda E, d=ctx.prep_f2[p], n=n:
                   E.tensor_scalar_sub(d[:, 0:n], d[:, 0:n], 128.0))
            b.ops["gpsimd"].append(lambda E: E.drain())
            cv = b.emit("gpsimd", lambda E, d=ctx.prep_b[p], t=ctx.prep_f2[p],
                        sc=ctx.prep_f[p], n=n:
                        E.tensor_mul(d[:, 0:n], t[:, 0:n], sc[:, 0:n]),
                        inc="pv")
            b.wait("gpsimd", "pv", cv)
            b.emit("gpsimd", lambda E, s=ctx.prep_b[p], d=dst2d, i=i, c0=c0, n=n:
                   E.dma_start(out=d[i * 128:(i + 1) * 128, c0:c0 + n],
                               in_=s[:, 0:n]),
                   inc="pst", n=16)
            g.set(("prep", p), "pst", b.val("pst"))
    return b.val("pst")



DEBUG_SERIALIZE = 0
DEBUG_PE_DELAY = 0

def l0ffn(ctx, x_dram, xn_dram, u_dram, xout_dram, W, ln_st):
    b = ctx.b
    g = ctx.g
    b1 = ctx.consts["l2_b1"]
    b2h = ctx.consts["l2_b2h"]
    wbuf, hwbuf = ctx.ffn_wbuf, ctx.ffn_hwbuf
    for j in range(NB):
        hA = ctx.ffn_hA if j % 2 == 0 else ctx.ffn_hB
        xnq = ctx.ffn_xnq[j % 2]
        b.wait("sync", "st", ln_st[j])
        for c in range(NCH):
            g.wait("sync", "xnq", j % 2, c)
            b.emit("sync", lambda E, d=xnq[c], s=xn_dram, c=c, j=j:
                   E.dma_start(out=d[:], in_=s[c * 128:(c + 1) * 128,
                                              j * TB:(j + 1) * TB]),
                   inc="ld", n=16)
        xn_ld = b.val("ld")
        b.wait("tensor", "ld", xn_ld)
        b.grace()
        first_write = True
        wlds = {}
        def _ld_fc1(m):
            p = m % 4
            g.wait("sync", "wb", p)
            b.emit("sync", lambda E, d=wbuf[p], s=W["fc1"], m=m:
                   E.dma_start(out=d[:], in_=s[m * 128:(m + 1) * 128, :]),
                   inc="ld", n=16)
            wlds[m] = b.val("ld")
        for m in range(4):
            _ld_fc1(m)
        for m in range(32):
            p = m % 4
            b.wait("tensor", "ld", wlds.get(m + 2, 0) or wlds[m])
            bank = m % 2
            use_bank(ctx, bank)
            for k in range(NCH):
                b.emit("tensor", lambda E, P=ctx.P[bank], w=wbuf[p],
                       x=xnq[k], k=k:
                       E.matmul(P[:, :], w[:, k * 128:(k + 1) * 128], x[:],
                                start=(k == 0), stop=(k == NCH - 1)),
                       inc=("t" if k == NCH - 1 else None))
            mmv = b.val("t")
            g.set(("wb", p), "t", mmv)
            if m + 4 < 32:
                _ld_fc1(m + 4)
            elif m == 28:
                wlds[32] = dummy_loads(ctx, W["fc1"], 1)
                wlds[33] = dummy_loads(ctx, W["fc1"], 1)
            b.wait("scalar", "t", mmv)
            if first_write:
                g.wait("scalar", "hfree", id(hA))
                g.wait("scalar", "hfreeS", id(hA))
                first_write = False
            sv = b.emit("scalar", lambda E, d=hA[m], P=ctx.P[bank], b1=b1, m=m:
                        E.activation(d[:], P[:, :], AF.Silu,
                                     bias=b1[:, m:m + 1]), inc="s")
            free_bank(ctx, bank, "s", sv)
        for c in range(NCH):
            g.set(("xnq", j % 2, c), "t", b.val("t"))
        h_ready = ("s", b.val("s"))
        if getattr(ctx, "dbg_h", None) is not None:
            b.wait("gpsimd", "s", h_ready[1])
            for m in range(32):
                b.emit("gpsimd", lambda E, s=hA[m], d=ctx.dbg_h, m=m, j=j:
                       E.dma_start(out=d[(j * 32 + m) * 128:(j * 32 + m + 1) * 128, :],
                                   in_=s[:]), inc="st", n=16)
            g.set(("hfreeS", id(hA)), "st", b.val("st"))
        xbuf = ctx.xf[j % 2]
        ubuf = ctx.xf[(j % 2) ^ 1]
        for c in range(NCH):
            g.wait("sync", "xf", j % 2, c)
            b.emit("sync", lambda E, d=xbuf[c], s=x_dram, c=c, j=j:
                   E.dma_start(out=d[:], in_=s[c * 128:(c + 1) * 128,
                                              j * TB:(j + 1) * TB]),
                   inc="ld", n=16)
            g.wait("sync", "xf", (j % 2) ^ 1, c)
            b.emit("sync", lambda E, d=ubuf[c], s=u_dram, c=c, j=j:
                   E.dma_start(out=d[:], in_=s[c * 128:(c + 1) * 128,
                                              j * TB:(j + 1) * TB]),
                   inc="ld", n=16)
        x_ld = b.val("ld")
        wlds2 = {}
        def _ld_fc2(m):
            p = m % 4
            g.wait("sync", "hw", p)
            b.emit("sync", lambda E, d=hwbuf[p], s=W["fc2"], m=m:
                   E.dma_start(out=d[:], in_=s[m * 128:(m + 1) * 128, :]),
                   inc="ld", n=16)
            wlds2[m] = b.val("ld")
        for m in range(3):
            _ld_fc2(m)
        for m in range(NCH):
            p = m % 4
            if m + 3 < NCH:
                _ld_fc2(m + 3)
            elif m + 3 == NCH:
                wlds2[NCH] = dummy_loads(ctx, W["fc2"], 1)
                wlds2[NCH + 1] = dummy_loads(ctx, W["fc2"], 1)
            b.wait("tensor", "ld", wlds2[min(m + 2, NCH + 1)])
            if m == 0:
                b.grace()
            if m == 0:
                b.wait("tensor", h_ready[0], h_ready[1])
            bank = 4 + m % 2
            use_bank(ctx, bank)
            for k in range(32):
                b.emit("tensor", lambda E, P=ctx.P[bank], w=hwbuf[p],
                       x=hA[k], k=k:
                       E.matmul(P[:, :], w[:, k * 128:(k + 1) * 128], x[:],
                                start=(k == 0), stop=(k == 31)),
                       inc=("t" if k == 31 else None))
            mmv = b.val("t")
            g.set(("hw", p), "t", mmv)
            b.wait("vector", "t", mmv)
            g.wait("vector", "tmp", m % 4)
            ov = b.emit("vector", lambda E, d=ctx.tmp[m % 4], P=ctx.P[bank],
                        bb=b2h, m=m:
                        E.tensor_scalar(d[:], P[:, :], 0.5, bb[:, m:m + 1],
                                        op0=ALU.mult, op1=ALU.add), inc="v")
            free_bank(ctx, bank, "v", ov)
            b.drain("vector")
            b.wait("vector", "ld", x_ld)
            b.emit("vector", lambda E, d=xbuf[m], u=ubuf[m]:
                   E.tensor_tensor(d[:], d[:], u[:], op=ALU.add))
            b.drain("vector")
            rv = b.emit("vector", lambda E, d=xbuf[m], t=ctx.tmp[m % 4]:
                        E.tensor_tensor(d[:], d[:], t[:], op=ALU.add), inc="v")
            g.set(("tmp", m % 4), "v", rv)
            b.wait("gpsimd", "v", rv)
            stv = b.emit("gpsimd", lambda E, s=xbuf[m], d=xout_dram, m=m, j=j:
                         E.dma_start(out=d[m * 128:(m + 1) * 128,
                                          j * TB:(j + 1) * TB], in_=s[:]),
                         inc="st", n=16)
            g.set(("xf", j % 2, m), "st", stv)
            g.set(("xf", (j % 2) ^ 1, m), "st", stv)
        g.set(("hfree", id(hA)), "t", b.val("t"))
    return b.val("st")


def alloc_attn(ctx, tables=False, need_ctx=True, n_aw=6):
    b = ctx.b
    ctx.at_qF = [b.sbuf(f"aqF{c}", (128, SEQ), BF16) for c in range(NCH)]
    ctx.at_kF = [b.sbuf(f"akF{c}", (128, SEQ), BF16) for c in range(NCH)]
    ctx.at_vT = [b.sbuf(f"avT{i}", (128, D), BF16) for i in range(SEQ // 128)]
    ctx.at_w = [b.sbuf(f"aw{i}", (128, 1024), BF16) for i in range(n_aw)]
    ctx.at_vw = [b.sbuf(f"avw{i}", (128, 1024), BF16) for i in range(4)]
    ctx.at_vf = ctx.prep_f[0]
    ctx.at_pkv = None
    ctx.at_rb = None
    ctx.at_q2 = ctx.x2
    ctx.at_rows = ctx.rows
    ctx.at_mq = b.sbuf("amq", (128, 32), F32)
    ctx.at_sc = b.sbuf("asc", (128, 4), F32)
    ctx.at_negM = b.sbuf("anegM", (128, NH), F32)
    ctx.at_P = ctx.x2
    if need_ctx:
        ctx.at_ctx = [[b.sbuf(f"actx{c}", (128, TB), BF16) for c in range(NCH)]] * 2
    if tables:
        ctx.at_tab = [b.sbuf(f"atab{i}", (128, 2432), BF16) for i in range(2)]
        ctx.at_bmax = b.sbuf("abmax", (1, NH), F32)


def alloc_l1aa(ctx):
    b = ctx.b
    ctx.at_pkv = b.sbuf("apkv", (128, 1024), BF16)
    ctx.at_rb = b.sbuf("arb", (128, TB), BF16)
    ctx.at_rl = [b.sbuf(f"arl{i}", (128, TB), BF16) for i in range(2)]
    ctx.at_zero = b.sbuf("azero", (128, TB), BF16)


def _attn_proj_phase(ctx, xn_dram, W, ln_st, s, rms, bias_c):
    b = ctx.b
    g = ctx.g
    qF, kF, vT, aw, vw = ctx.at_qF, ctx.at_kF, ctx.at_vT, ctx.at_w, ctx.at_vw
    EPS = 1e-6
    for jb in range(SEQ // TB):
        j = s * (SEQ // TB) + jb
        nset = len(ctx.ffn_xnq)
        xnq = ctx.ffn_xnq[j % nset]
        b.wait("sync", "st", ln_st[j])
        for c in range(NCH):
            g.wait("sync", "xnq", j % nset, c)
            b.emit("sync", lambda E, d=xnq[c], s_=xn_dram, c=c, j=j:
                   E.dma_start(out=d[:], in_=s_[c * 128:(c + 1) * 128,
                                               j * TB:(j + 1) * TB]),
                   inc="ld", n=16)
        lds = {}
        def _ldqk(c):
            pq, pk = c % 3, 3 + c % 3
            g.wait("sync", "aw", pq)
            b.emit("sync", lambda E, d=aw[pq], s_=W["wq"], c=c:
                   E.dma_start(out=d[:], in_=s_[c * 128:(c + 1) * 128, :]),
                   inc="ld", n=16)
            g.wait("sync", "aw", pk)
            b.emit("sync", lambda E, d=aw[pk], s_=W["wk"], c=c:
                   E.dma_start(out=d[:], in_=s_[c * 128:(c + 1) * 128, :]),
                   inc="ld", n=16)
            lds[c] = b.val("ld")
        vlds = {}
        def _ldv(c2):
            p = c2 % 4
            g.wait("sync", "avw", p)
            b.emit("sync", lambda E, d=vw[p], s_=W["wv"], c2=c2:
                   E.dma_start(out=d[:], in_=s_[c2 * 128:(c2 + 1) * 128, :]),
                   inc="ld", n=16)
            vlds[c2] = b.val("ld")
        for c2 in range(3):
            _ldv(c2)
        for c2 in range(NCH):
            if c2 + 3 < NCH:
                _ldv(c2 + 3)
            elif c2 + 3 == NCH:
                vlds[NCH] = dummy_loads(ctx, W["wv"], 1)
                vlds[NCH + 1] = dummy_loads(ctx, W["wv"], 1)
            b.wait("tensor", "ld", vlds[min(c2 + 2, NCH + 1)])
            if c2 == 0:
                b.grace()
            p = c2 % 4
            bank = c2 % 2
            use_bank(ctx, bank)
            for k in range(NCH):
                b.emit("tensor", lambda E, P=ctx.P[bank], w=vw[p], x=xnq[k], k=k:
                       E.matmul(P[:, :], w[:, k * 128:(k + 1) * 128], x[:],
                                start=(k == 0), stop=(k == NCH - 1)),
                       inc=("t" if k == NCH - 1 else None))
            vmm = b.val("t")
            g.set(("avw", p), "t", vmm)
            b.wait("vector", "t", vmm)
            g.wait("vector", "vfs", 0)
            g.wait("vector", "prep", 0)
            ev = b.emit("vector", lambda E, d=ctx.at_vf, P=ctx.P[bank]:
                        E.tensor_copy(d[:], P[:, :]), inc="v")
            free_bank(ctx, bank, "v", ev)
            b.wait("tensor", "v", ev)
            for ts in range(4):
                bank2 = 2 + ts % 2
                use_bank(ctx, bank2)
                tt = b.emit("tensor", lambda E, P=ctx.P[bank2], s_=ctx.at_vf,
                            idn=ctx.identf, ts=ts:
                            E.transpose(P[:, 0:128], s_[:, ts * 128:(ts + 1) * 128],
                                        idn[:]), inc="t")
                b.wait("vector", "t", tt)
                g.wait("vector", "avT", jb * 4 + ts)
                tv = b.emit("vector", lambda E, d=vT[jb * 4 + ts], P=ctx.P[bank2],
                            c2=c2, ts=ts:
                            E.tensor_copy(d[:, c2 * 128:(c2 + 1) * 128],
                                          P[:, 0:128]), inc="v")
                free_bank(ctx, bank2, "v", tv)
            g.set(("vfs", 0), "t", b.val("t"))
            g.set(("prep", 0), "t", b.val("t"))
        _ldqk(0)
        _ldqk(1)
        for c in range(NCH):
            if c + 2 < NCH:
                _ldqk(c + 2)
            else:
                lds[c + 2] = dummy_loads(ctx, W["wq"], 2)
            b.wait("tensor", "ld", lds[c + 2])
            if c == 0:
                b.grace()
            pq, pk = c % 3, 3 + c % 3
            bq, bk = 2, 3
            use_bank(ctx, bq)
            for k in range(NCH):
                b.emit("tensor", lambda E, P=ctx.P[bq], w=aw[pq], x=xnq[k], k=k:
                       E.matmul(P[:, :], w[:, k * 128:(k + 1) * 128], x[:],
                                start=(k == 0), stop=(k == NCH - 1)),
                       inc=("t" if k == NCH - 1 else None))
            mmq = b.val("t")
            g.set(("aw", pq), "t", mmq)
            use_bank(ctx, bk)
            for k in range(NCH):
                b.emit("tensor", lambda E, P=ctx.P[bk], w=aw[pk], x=xnq[k], k=k:
                       E.matmul(P[:, :], w[:, k * 128:(k + 1) * 128], x[:],
                                start=(k == 0), stop=(k == NCH - 1)),
                       inc=("t" if k == NCH - 1 else None))
            mmk = b.val("t")
            g.set(("aw", pk), "t", mmk)
            b.wait("vector", "t", mmk)
            g.wait("vector", "aqk", c)
            if bias_c is not None:
                bqkv = ctx.consts[bias_c]
                vq = b.emit("vector", lambda E, d=qF[c], P=ctx.P[bq], bb=bqkv,
                            c=c, jb=jb:
                            E.tensor_scalar_add(d[:, jb * TB:(jb + 1) * TB],
                                                P[:, :], bb[:, c:c + 1]),
                            inc="v")
                b.drain("vector")
                vk = b.emit("vector", lambda E, d=kF[c], P=ctx.P[bk], bb=bqkv,
                            c=c, jb=jb:
                            E.tensor_scalar_add(d[:, jb * TB:(jb + 1) * TB],
                                                P[:, :], bb[:, 8 + c:9 + c]),
                            inc="v")
            else:
                vq = b.emit("vector", lambda E, d=qF[c], P=ctx.P[bq], c=c, jb=jb:
                            E.tensor_copy(d[:, jb * TB:(jb + 1) * TB], P[:, :]),
                            inc="v")
                b.drain("vector")
                vk = b.emit("vector", lambda E, d=kF[c], P=ctx.P[bk], c=c, jb=jb:
                            E.tensor_copy(d[:, jb * TB:(jb + 1) * TB], P[:, :]),
                            inc="v")
            free_bank(ctx, bq, "v", vq)
            free_bank(ctx, bk, "v", vk)
            if rms:
                _rmsnorm_qk(ctx, c, jb, vq, vk)
            else:
                g.set(("aqk", c), "v", vk)
        for c2 in range(NCH):
            g.set(("xnq", j % nset, c2), "t", b.val("t"))
    return


def _rmsnorm_qk(ctx, c, jb, vq, vk):
    b = ctx.b
    g = ctx.g
    EPS = 1e-6
    qF, kF = ctx.at_qF, ctx.at_kF
    R = ctx.at_rows
    w2dd = ctx.consts["l3_w2dd"]
    b.wait("scalar", "v", vk)
    g.wait("scalar", "x2", 0)
    sq = b.emit("scalar", lambda E, d=ctx.at_q2[0], s_=qF[c], jb=jb:
                E.activation(d[:], s_[:, jb * TB:(jb + 1) * TB], AF.Square),
                inc="s")
    g.wait("scalar", "x2", 1)
    sk = b.emit("scalar", lambda E, d=ctx.at_q2[1], s_=kF[c], jb=jb:
                E.activation(d[:], s_[:, jb * TB:(jb + 1) * TB], AF.Square),
                inc="s")
    for bi in range(4, 8):
        use_bank(ctx, bi)
    b.wait("tensor", "s", sq)
    b.emit("tensor", lambda E, P=ctx.P[4], o=ctx.onesb, x=ctx.at_q2[0]:
           E.matmul(P[0:1, 0:TB], o[0:64, :], x[0:64, :], start=True,
                    stop=True, tile_position=(0, 0)))
    tq = b.emit("tensor", lambda E, P=ctx.P[5], o=ctx.onesb, x=ctx.at_q2[0]:
                E.matmul(P[0:1, 0:TB], o[64:128, :], x[64:128, :], start=True,
                         stop=True, tile_position=(64, 0)), inc="t")
    g.set(("x2", 0), "t", tq)
    b.wait("tensor", "s", sk)
    b.emit("tensor", lambda E, P=ctx.P[6], o=ctx.onesb, x=ctx.at_q2[1]:
           E.matmul(P[0:1, 0:TB], o[0:64, :], x[0:64, :], start=True,
                    stop=True, tile_position=(0, 0)))
    tk = b.emit("tensor", lambda E, P=ctx.P[7], o=ctx.onesb, x=ctx.at_q2[1]:
                E.matmul(P[0:1, 0:TB], o[64:128, :], x[64:128, :], start=True,
                         stop=True, tile_position=(64, 0)), inc="t")
    g.set(("x2", 1), "t", tk)
    b.wait("vector", "t", tk)
    for ri in range(4):
        b.emit("vector", lambda E, R=R, P=ctx.P[4 + ri], ri=ri:
               E.tensor_scalar(R[0:1, ri * TB:(ri + 1) * TB], P[0:1, 0:TB],
                               1.0 / 64.0, EPS, op0=ALU.mult, op1=ALU.add))
    b.drain("vector")
    rv = b.emit("vector", lambda E: E.engine_nop(), inc="v")
    b.wait("scalar", "v", rv)
    b.emit("scalar", lambda E, R=R:
           E.activation(R[0:1, 0:4 * TB], R[0:1, 0:4 * TB], AF.Ln))
    b.drain("scalar")
    rs = b.emit("scalar", lambda E, R=R:
                E.activation(R[0:1, 0:4 * TB], R[0:1, 0:4 * TB], AF.Exp,
                             scale=-0.5), inc="s")
    for bi in range(4, 8):
        free_bank(ctx, bi, "s", rs)
    b.wait("tensor", "s", rs)
    use_bank(ctx, 4)
    use_bank(ctx, 5)
    b.emit("tensor", lambda E, P=ctx.P[4], o=ctx.onesrow, R=R:
           E.matmul(P[0:64, :], o[0:1, 0:64], R[0:1, 0:TB], start=True,
                    stop=True, tile_position=(0, 0)))
    b.emit("tensor", lambda E, P=ctx.P[4], o=ctx.onesrow, R=R:
           E.matmul(P[64:128, :], o[0:1, 0:64], R[0:1, TB:2 * TB], start=True,
                    stop=True, tile_position=(0, 64)))
    b.emit("tensor", lambda E, P=ctx.P[5], o=ctx.onesrow, R=R:
           E.matmul(P[0:64, :], o[0:1, 0:64], R[0:1, 2 * TB:3 * TB], start=True,
                    stop=True, tile_position=(0, 0)))
    tb = b.emit("tensor", lambda E, P=ctx.P[5], o=ctx.onesrow, R=R:
                E.matmul(P[64:128, :], o[0:1, 0:64], R[0:1, 3 * TB:4 * TB],
                         start=True, stop=True, tile_position=(0, 64)), inc="t")
    b.wait("vector", "t", tb)
    b.emit("vector", lambda E, d=qF[c], P=ctx.P[4], jb=jb:
           E.scalar_tensor_tensor(d[:, jb * TB:(jb + 1) * TB],
                                  d[:, jb * TB:(jb + 1) * TB], 1.0, P[:, :],
                                  op0=ALU.mult, op1=ALU.mult))
    b.drain("vector")
    ak = b.emit("vector", lambda E, d=kF[c], P=ctx.P[5], w=w2dd, jb=jb:
                E.scalar_tensor_tensor(d[:, jb * TB:(jb + 1) * TB],
                                       d[:, jb * TB:(jb + 1) * TB], w[:, 0:1],
                                       P[:, :], op0=ALU.mult, op1=ALU.mult),
                inc="v")
    b.drain("vector")
    free_bank(ctx, 4, "v", ak)
    free_bank(ctx, 5, "v", ak)
    g.set(("aqk", c), "v", ak)

def _attn_passA(ctx, s, tables=False):
    b = ctx.b
    g = ctx.g
    qF, kF = ctx.at_qF, ctx.at_kF
    mq, sc, R = ctx.at_mq, ctx.at_sc, ctx.at_rows
    b.wait("tensor", "v", b.val("v"))
    for c in range(NCH):
        hA, hB = 2 * c, 2 * c + 1
        for qt in range(SEQ // 128):
            for bi in range(8):
                use_bank(ctx, bi)
            for kb in range(4):
                b.emit("tensor", lambda E, P=ctx.P[kb], q=qF[c], k=kF[c],
                       qt=qt, kb=kb:
                       E.matmul(P[:, :], q[0:64, qt * 128:(qt + 1) * 128],
                                k[0:64, kb * TB:(kb + 1) * TB],
                                start=True, stop=True, tile_position=(0, 0)))
                last = (kb == 3)
                b.emit("tensor", lambda E, P=ctx.P[4 + kb], q=qF[c], k=kF[c],
                       qt=qt, kb=kb:
                       E.matmul(P[:, :], q[64:128, qt * 128:(qt + 1) * 128],
                                k[64:128, kb * TB:(kb + 1) * TB],
                                start=True, stop=True, tile_position=(64, 0)),
                       inc=("t" if last else None))
            tmm = b.val("t")
            b.wait("vector", "t", tmm)
            b.emit("vector", lambda E, m=mq, PA=ctx.PSALL, qt=qt:
                   E.reduce_max(m[:, qt:qt + 1], PA[:, 0:SEQ], axis=AX.X))
            b.drain("vector")
            rm = b.emit("vector", lambda E, m=mq, PA=ctx.PSALL, qt=qt:
                        E.reduce_max(m[:, 16 + qt:17 + qt], PA[:, SEQ:2 * SEQ],
                                     axis=AX.X), inc="v")
            b.drain("vector")
            for bi in range(8):
                free_bank(ctx, bi, "v", rm)
        for hi, h in enumerate((hA, hB)):
            b.emit("vector", lambda E, sc=sc, m=mq, hi=hi:
                   E.reduce_max(sc[:, hi:hi + 1], m[:, hi * 16:(hi + 1) * 16],
                                axis=AX.X))
        b.drain("vector")
        fm = b.emit("vector", lambda E: E.engine_nop(), inc="v")
        for hi, h in enumerate((hA, hB)):
            b.wait("tensor", "v", fm if hi == 0 else b.val("v"))
            use_bank(ctx, 6)
            tb = b.emit("tensor", lambda E, P=ctx.P[6], sc=sc, idn=ctx.identf,
                        hi=hi:
                        E.transpose(P[0:1, 0:128], sc[:, hi:hi + 1], idn[:]),
                        inc="t")
            b.wait("vector", "t", tb)
            rv = b.emit("vector", lambda E, R=R, P=ctx.P[6], h=h:
                        E.reduce_max(R[0:1, h:h + 1], P[0:1, 0:128], axis=AX.X),
                        inc="v")
            b.drain("vector")
            free_bank(ctx, 6, "v", rv)
    return


def _attn_finalizeM(ctx, tables=False):
    b = ctx.b
    g = ctx.g
    R = ctx.at_rows
    if tables:
        b.emit("vector", lambda E, R=R, bm=ctx.at_bmax:
               E.tensor_tensor(R[0:1, 0:NH], R[0:1, 0:NH], bm[0:1, 0:NH],
                               op=ALU.add))
        b.drain("vector")
    nm = b.emit("vector", lambda E, R=R:
                E.tensor_scalar_mul(R[0:1, NH:2 * NH], R[0:1, 0:NH], -1.0),
                inc="v")
    b.wait("tensor", "v", nm)
    use_bank(ctx, 6)
    tb = b.emit("tensor", lambda E, P=ctx.P[6], o=ctx.onesrow, R=R:
                E.matmul(P[:, 0:NH], o[0:1, 0:128], R[0:1, NH:2 * NH],
                         start=True, stop=True), inc="t")
    b.wait("vector", "t", tb)
    ngm = b.emit("vector", lambda E, d=ctx.at_negM, P=ctx.P[6]:
                 E.tensor_copy(d[:, :], P[:, 0:NH]), inc="v")
    free_bank(ctx, 6, "v", ngm)
    return ngm

def _attn_passB_out(ctx, s, x_dram, xout_dram, W, bias_c, bproj_c,
                    tables=None, u_dram=None):
    b = ctx.b
    g = ctx.g
    qF, kF, vT, aw = ctx.at_qF, ctx.at_kF, ctx.at_vT, ctx.at_w
    negM = ctx.at_negM
    R = ctx.at_rows
    for qb in range(SEQ // TB):
        j = s * (SEQ // TB) + qb
        cbuf = ctx.at_ctx[qb % 2]
        xbuf = ctx.xf[j % 2]
        for c in range(NCH):
            g.wait("sync", "xf", j % 2, c)
            b.emit("sync", lambda E, d=xbuf[c], s_=x_dram, c=c, j=j:
                   E.dma_start(out=d[:], in_=s_[c * 128:(c + 1) * 128,
                                               j * TB:(j + 1) * TB]),
                   inc="ld", n=16)
            if u_dram is not None:
                g.wait("sync", "xf", (j % 2) ^ 1, c)
                b.emit("sync", lambda E, d=ctx.xf[(j % 2) ^ 1][c], s_=u_dram,
                       c=c, j=j:
                       E.dma_start(out=d[:], in_=s_[c * 128:(c + 1) * 128,
                                                   j * TB:(j + 1) * TB]),
                       inc="ld", n=16)
        x_ld = b.val("ld")
        for c in range(NCH):
            hA, hB = 2 * c, 2 * c + 1
            if tables is not None:
                for hi, h in enumerate((hA, hB)):
                    g.wait("sync", "atab", hi)
                    b.emit("sync", lambda E, d=ctx.at_tab[hi], s_=tables["a2"],
                           h=h, qb=qb:
                           E.dma_start(out=d[:],
                                       in_=s_[h * 128:(h + 1) * 128,
                                              qb * TB:qb * TB + 2432]),
                           inc="ld", n=16)
                tab_ld = dummy_loads(ctx, tables["a2"], 2)
                b.wait("tensor", "ld", tab_ld)
                b.grace()
            use_bank(ctx, 4)
            use_bank(ctx, 5)
            use_bank(ctx, 6)
            ngm_v = ctx._negM_ready
            b.wait("scalar", "v", ngm_v)
            for kt in range(SEQ // 128):
                bk = kt % 2
                bk2 = 2 + kt % 2
                use_bank(ctx, bk)
                mm1 = b.emit("tensor", lambda E, P=ctx.P[bk], k=kF[c], q=qF[c],
                             kt=kt, qb=qb:
                             E.matmul(P[:, :], k[0:64, kt * 128:(kt + 1) * 128],
                                      q[0:64, qb * TB:(qb + 1) * TB],
                                      start=True, stop=(tables is None),
                                      tile_position=(0, 0)),
                             inc=("t" if tables is None else None))
                if tables is not None:
                    n0 = 1920 - 128 * kt
                    mm1 = b.emit("tensor", lambda E, P=ctx.P[bk], idn=ctx.ident,
                                 tb_=ctx.at_tab[0], n0=n0:
                                 E.matmul(P[:, :], idn[:], tb_[:, n0:n0 + TB],
                                          start=False, stop=True), inc="t")
                use_bank(ctx, bk2)
                mm2 = b.emit("tensor", lambda E, P=ctx.P[bk2], k=kF[c], q=qF[c],
                             kt=kt, qb=qb:
                             E.matmul(P[:, :], k[64:128, kt * 128:(kt + 1) * 128],
                                      q[64:128, qb * TB:(qb + 1) * TB],
                                      start=True, stop=(tables is None),
                                      tile_position=(64, 0)),
                             inc=("t" if tables is None else None))
                if tables is not None:
                    n0 = 1920 - 128 * kt
                    mm2 = b.emit("tensor", lambda E, P=ctx.P[bk2], idn=ctx.ident,
                                 tb_=ctx.at_tab[1], n0=n0:
                                 E.matmul(P[:, :], idn[:], tb_[:, n0:n0 + TB],
                                          start=False, stop=True), inc="t")
                b.wait("scalar", "t", mm2)
                g.wait("scalar", "aPt", 0)
                e1 = b.emit("scalar", lambda E, d=ctx.at_P[0], P=ctx.P[bk],
                            nm=negM, h=hA:
                            E.activation(d[:], P[:, :], AF.Exp,
                                         bias=nm[:, h:h + 1]), inc="s")
                free_bank(ctx, bk, "s", e1)
                g.wait("scalar", "aPt", 1)
                e2 = b.emit("scalar", lambda E, d=ctx.at_P[1], P=ctx.P[bk2],
                            nm=negM, h=hB:
                            E.activation(d[:], P[:, :], AF.Exp,
                                         bias=nm[:, h:h + 1]), inc="s")
                free_bank(ctx, bk2, "s", e2)
                b.wait("tensor", "s", e2)
                b.emit("tensor", lambda E, P=ctx.P[4], o=ctx.onesb,
                       x=ctx.at_P[0], kt=kt:
                       E.matmul(P[0:1, 0:TB], o[:, :], x[:, :],
                                start=(kt == 0), stop=(kt == 15)))
                b.emit("tensor", lambda E, P=ctx.P[5], o=ctx.onesb,
                       x=ctx.at_P[1], kt=kt:
                       E.matmul(P[0:1, 0:TB], o[:, :], x[:, :],
                                start=(kt == 0), stop=(kt == 15)))
                b.emit("tensor", lambda E, P=ctx.P[6], v=vT[kt], x=ctx.at_P[0],
                       kt=kt, hA=hA:
                       E.matmul(P[0:64, :], v[:, hA * 64:(hA + 1) * 64], x[:, :],
                                start=(kt == 0), stop=(kt == 15),
                                tile_position=(0, 0)))
                av = b.emit("tensor", lambda E, P=ctx.P[6], v=vT[kt],
                            x=ctx.at_P[1], kt=kt, hB=hB:
                            E.matmul(P[64:128, :], v[:, hB * 64:(hB + 1) * 64],
                                     x[:, :], start=(kt == 0), stop=(kt == 15),
                                     tile_position=(0, 64)), inc="t")
                g.set(("aPt", 0), "t", av)
                g.set(("aPt", 1), "t", av)
            av_done = b.val("t")
            if tables is not None:
                g.set(("atab", 0), "t", av_done)
                g.set(("atab", 1), "t", av_done)
            b.wait("vector", "t", av_done)
            b.emit("vector", lambda E, R=R, P=ctx.P[4]:
                   E.reciprocal(R[0:1, 2 * TB:3 * TB], P[0:1, 0:TB]))
            b.drain("vector")
            rc = b.emit("vector", lambda E, R=R, P=ctx.P[5]:
                        E.reciprocal(R[0:1, 3 * TB:4 * TB], P[0:1, 0:TB]),
                        inc="v")
            b.drain("vector")
            free_bank(ctx, 4, "v", rc)
            free_bank(ctx, 5, "v", rc)
            b.wait("tensor", "v", rc)
            use_bank(ctx, 7)
            b.emit("tensor", lambda E, P=ctx.P[7], o=ctx.onesrow, R=R:
                   E.matmul(P[0:64, :], o[0:1, 0:64], R[0:1, 2 * TB:3 * TB],
                            start=True, stop=True, tile_position=(0, 0)))
            cb = b.emit("tensor", lambda E, P=ctx.P[7], o=ctx.onesrow, R=R:
                        E.matmul(P[64:128, :], o[0:1, 0:64],
                                 R[0:1, 3 * TB:4 * TB], start=True, stop=True,
                                 tile_position=(0, 64)), inc="t")
            b.wait("vector", "t", cb)
            g.wait("vector", "actx", qb % 2, c)
            b.emit("vector", lambda E, d=cbuf[c], P=ctx.P[6]:
                   E.tensor_copy(d[:], P[:, :]))
            b.drain("vector")
            b.emit("vector", lambda E, d=cbuf[c], P=ctx.P[7]:
                   E.scalar_tensor_tensor(d[:], d[:], 1.0, P[:, :],
                                          op0=ALU.mult, op1=ALU.mult))
            if bias_c is not None:
                b.drain("vector")
                cv = b.emit("vector", lambda E, d=cbuf[c], bb=ctx.consts[bias_c],
                            c=c:
                            E.tensor_scalar_add(d[:], d[:], bb[:, 16 + c:17 + c]),
                            inc="v")
            else:
                cv = b.emit("vector", lambda E: E.engine_nop(), inc="v")
            free_bank(ctx, 6, "v", cv)
            free_bank(ctx, 7, "v", cv)
            ctx._ctx_ready = cv
        lds = {}
        def _ldo(ms):
            p = ms % 6
            g.wait("sync", "aw", p)
            b.emit("sync", lambda E, d=aw[p], s_=W["wo"], ms=ms:
                   E.dma_start(out=d[:], in_=s_[ms * 128:(ms + 1) * 128, :]),
                   inc="ld", n=16)
            lds[ms] = b.val("ld")
        for ms in range(3):
            _ldo(ms)
        bpr = ctx.consts[bproj_c] if bproj_c is not None else None
        for ms in range(NCH):
            if ms + 3 < NCH:
                _ldo(ms + 3)
            elif ms + 3 == NCH:
                lds[NCH] = dummy_loads(ctx, W["wo"], 1)
                lds[NCH + 1] = dummy_loads(ctx, W["wo"], 1)
            b.wait("tensor", "ld", lds[min(ms + 2, NCH + 1)])
            if ms == 0:
                b.grace()
                b.wait("tensor", "v", ctx._ctx_ready)
            p = ms % 6
            bank = ms % 2
            use_bank(ctx, bank)
            for kc in range(NCH):
                b.emit("tensor", lambda E, P=ctx.P[bank], w=aw[p], x=cbuf[kc],
                       kc=kc:
                       E.matmul(P[:, :], w[:, kc * 128:(kc + 1) * 128], x[:],
                                start=(kc == 0), stop=(kc == NCH - 1)),
                       inc=("t" if kc == NCH - 1 else None))
            mmo = b.val("t")
            g.set(("aw", p), "t", mmo)
            b.wait("vector", "t", mmo)
            g.wait("vector", "tmp", ms % 4)
            if bpr is not None:
                ov = b.emit("vector", lambda E, d=ctx.tmp[ms % 4], P=ctx.P[bank],
                            bb=bpr, ms=ms:
                            E.tensor_scalar_add(d[:], P[:, :], bb[:, ms:ms + 1]),
                            inc="v")
            else:
                ov = b.emit("vector", lambda E, d=ctx.tmp[ms % 4], P=ctx.P[bank]:
                            E.tensor_copy(d[:], P[:, :]), inc="v")
            free_bank(ctx, bank, "v", ov)
            b.drain("vector")
            b.wait("vector", "ld", x_ld)
            if u_dram is not None:
                b.emit("vector", lambda E, d=xbuf[ms], u=ctx.xf[(j % 2) ^ 1][ms]:
                       E.tensor_tensor(d[:], d[:], u[:], op=ALU.add))
                b.drain("vector")
            rv = b.emit("vector", lambda E, d=xbuf[ms], t=ctx.tmp[ms % 4]:
                        E.tensor_tensor(d[:], d[:], t[:], op=ALU.add), inc="v")
            g.set(("tmp", ms % 4), "v", rv)
            b.wait("gpsimd", "v", rv)
            stv = b.emit("gpsimd", lambda E, s_=xbuf[ms], d=xout_dram, ms=ms, j=j:
                         E.dma_start(out=d[ms * 128:(ms + 1) * 128,
                                          j * TB:(j + 1) * TB], in_=s_[:]),
                         inc="st", n=16)
            g.set(("xf", j % 2, ms), "st", stv)
            if u_dram is not None:
                g.set(("xf", (j % 2) ^ 1, ms), "st", stv)
        for c in range(NCH):
            g.set(("actx", qb % 2, c), "t", b.val("t"))
    tend = b.val("t")
    for c in range(NCH):
        g.set(("aqk", c), "t", tend)
    for i in range(SEQ // 128):
        g.set(("avT", i), "t", tend)
    for i in range(NCH):
        g.set(("avw", i), "t", tend)


def l0attnb_run(ctx, x_dram, xn_dram, xout_dram, W, ln_st):
    b = ctx.b
    qkn = ctx.consts["l3_qkn"]
    if "l3_w2dd" not in ctx.consts:
        w2dd = b.sbuf("c_l3_w2dd", (128, 1), F32)
        ctx.consts["l3_w2dd"] = w2dd
        b.wait("vector", "ld", b.val("ld"))
        b.emit("vector", lambda E, d=ctx.at_sc, s=qkn:
               E.tensor_copy(d[0:64, 3:4], s[64:128, :]))
        b.drain("vector")
        b.emit("vector", lambda E, d=w2dd, s=qkn, t=ctx.at_sc:
               E.scalar_tensor_tensor(d[0:64, :], s[0:64, :], 0.125,
                                      t[0:64, 3:4], op0=ALU.mult, op1=ALU.mult))
        b.drain("vector")
        b.emit("vector", lambda E, d=w2dd:
               E.tensor_copy(d[64:128, :], d[0:64, :]), inc="v")
        b.drain("vector")
    for s in range(NSEQ):
        _attn_proj_phase(ctx, xn_dram, W, ln_st, s, rms=True, bias_c="l3_bqkv")
        _attn_passA(ctx, s)
        ctx._negM_ready = _attn_finalizeM(ctx)
        _attn_passB_out(ctx, s, x_dram, xout_dram, W, bias_c="l3_bqkv",
                        bproj_c="l3_bproj")
    return b.val("st")

def l0attna_run(ctx, x_dram, xn_dram, u_dram, xout_dram, W, ln_st, a2_cache):
    b = ctx.b
    g = ctx.g
    for h in range(NH):
        hi = h % 2
        g.wait("sync", "atab", hi)
        b.emit("sync", lambda E, d=ctx.at_tab[hi], s_=a2_cache, h=h:
               E.dma_start(out=d[:], in_=s_[h * 128:(h + 1) * 128, 0:2432]),
               inc="ld", n=16)
        b.emit("sync", lambda E, d=ctx.at_tab[hi], s_=a2_cache, h=h:
               E.dma_start(out=d[:, 0:1536],
                           in_=s_[h * 128:(h + 1) * 128, 2432:3968]),
               inc="ld", n=16) if False else None
        tl = dummy_loads(ctx, a2_cache, 2)
        b.wait("vector", "ld", tl)
        b.emit("vector", lambda E, d=ctx.at_sc, s_=ctx.at_tab[hi]:
               E.reduce_max(d[:, 2:3], s_[:], axis=AX.X))
        b.drain("vector")
        bm = b.emit("vector", lambda E: E.engine_nop(), inc="v")
        b.wait("tensor", "v", bm)
        use_bank(ctx, 6)
        tb = b.emit("tensor", lambda E, P=ctx.P[6], sc=ctx.at_sc, idn=ctx.identf:
                    E.transpose(P[0:1, 0:128], sc[:, 2:3], idn[:]), inc="t")
        b.wait("vector", "t", tb)
        rv = b.emit("vector", lambda E, d=ctx.at_bmax, P=ctx.P[6], h=h:
                    E.reduce_max(d[0:1, h:h + 1], P[0:1, 0:128], axis=AX.X),
                    inc="v")
        b.drain("vector")
        free_bank(ctx, 6, "v", rv)
        g.set(("atab", hi), "v", rv)
    for s in range(NSEQ):
        _attn_proj_phase(ctx, xn_dram, W, ln_st, s, rms=False, bias_c=None)
        _attn_passA(ctx, s)
        ctx._negM_ready = _attn_finalizeM(ctx, tables=True)
        _attn_passB_out(ctx, s, x_dram, xout_dram, W, bias_c=None,
                        bproj_c=None, tables=dict(a2=a2_cache), u_dram=u_dram)
    return b.val("st")


def l1aa_run(ctx, x_dram, xn_dram, xout_dram, W, ln_st, decay_cache, r_dram):
    b = ctx.b
    g = ctx.g
    qF, kF, vT = ctx.at_qF, ctx.at_kF, ctx.at_vT
    R = ctx.at_rows
    gnc = ctx.consts["l5_gnc"]
    gnbc = ctx.consts["l5_gnbc"]
    CNT = float(NH * DK * SEQ)
    b.emit("vector", lambda E, z=ctx.at_zero: E.memset(z[:], 0.0), inc="v")
    b.drain("vector")
    for s in range(NSEQ):
        _attn_proj_phase(ctx, xn_dram, W, ln_st, s, rms=False, bias_c=None)
        b.wait("tensor", "v", b.val("v"))
        g.wait("sync", "pkvb", 0)
        b.emit("sync", lambda E, d=ctx.at_pkv, s_=W["pkv"]:
               E.dma_start(out=d[:], in_=s_[0:128, :]), inc="ld", n=16)
        pkv_ld = dummy_loads(ctx, W["pkv"], 2)
        b.wait("tensor", "ld", pkv_ld)
        first_stat = True
        for qb in range(SEQ // TB):
            j = s * (SEQ // TB) + qb
            for c in range(NCH):
                hA, hB = 2 * c, 2 * c + 1
                for hi, h in enumerate((hA, hB)):
                    g.wait("sync", "atab", hi)
                    b.emit("sync", lambda E, d=ctx.at_tab[hi], s_=decay_cache,
                           h=h, qb=qb:
                           E.dma_start(out=d[:],
                                       in_=s_[h * 128:(h + 1) * 128,
                                              qb * TB:qb * TB + 2432]),
                           inc="ld", n=16)
                tab_ld = dummy_loads(ctx, decay_cache, 2)
                b.wait("tensor", "ld", tab_ld)
                b.grace()
                use_bank(ctx, 6)
                use_bank(ctx, 7)
                b.emit("tensor", lambda E, P=ctx.P[7], pk=ctx.at_pkv, q=qF[c],
                       hA=hA, qb=qb:
                       E.matmul(P[0:64, :], pk[0:64, hA * 64:(hA + 1) * 64],
                                q[0:64, qb * TB:(qb + 1) * TB], start=True,
                                stop=True, tile_position=(0, 0)))
                cr = b.emit("tensor", lambda E, P=ctx.P[7], pk=ctx.at_pkv,
                            q=qF[c], hB=hB, qb=qb:
                            E.matmul(P[64:128, :],
                                     pk[64:128, hB * 64:(hB + 1) * 64],
                                     q[64:128, qb * TB:(qb + 1) * TB],
                                     start=True, stop=True,
                                     tile_position=(64, 64)), inc="t")
                nkt = 4 * qb + 4
                for kt in range(nkt):
                    bk = kt % 2
                    bk2 = 2 + kt % 2
                    use_bank(ctx, bk)
                    mm1 = b.emit("tensor", lambda E, P=ctx.P[bk], k=kF[c],
                                 q=qF[c], kt=kt, qb=qb:
                                 E.matmul(P[:, :],
                                          k[0:64, kt * 128:(kt + 1) * 128],
                                          q[0:64, qb * TB:(qb + 1) * TB],
                                          start=True, stop=True,
                                          tile_position=(0, 0)), inc="t")
                    use_bank(ctx, bk2)
                    mm2 = b.emit("tensor", lambda E, P=ctx.P[bk2], k=kF[c],
                                 q=qF[c], kt=kt, qb=qb:
                                 E.matmul(P[:, :],
                                          k[64:128, kt * 128:(kt + 1) * 128],
                                          q[64:128, qb * TB:(qb + 1) * TB],
                                          start=True, stop=True,
                                          tile_position=(64, 0)), inc="t")
                    n0 = 1920 - 128 * kt
                    b.wait("vector", "t", mm2)
                    g.wait("vector", "aPt", 0)
                    e1 = b.emit("vector", lambda E, d=ctx.at_P[0], P=ctx.P[bk],
                                tb_=ctx.at_tab[0], n0=n0:
                                E.tensor_tensor(d[:], P[:, :],
                                                tb_[:, n0:n0 + TB],
                                                op=ALU.mult), inc="v")
                    free_bank(ctx, bk, "v", e1)
                    b.drain("vector")
                    g.wait("vector", "aPt", 1)
                    e2 = b.emit("vector", lambda E, d=ctx.at_P[1], P=ctx.P[bk2],
                                tb_=ctx.at_tab[1], n0=n0:
                                E.tensor_tensor(d[:], P[:, :],
                                                tb_[:, n0:n0 + TB],
                                                op=ALU.mult), inc="v")
                    free_bank(ctx, bk2, "v", e2)
                    b.wait("tensor", "v", e2)
                    av = b.emit("tensor", lambda E, P=ctx.P[6], v=vT[kt],
                                x=ctx.at_P[0], kt=kt, hA=hA, first=(kt == 0),
                                last=(kt == nkt - 1):
                                E.matmul(P[0:64, :],
                                         v[:, hA * 64:(hA + 1) * 64], x[:, :],
                                         start=first, stop=last,
                                         tile_position=(0, 0)))
                    av = b.emit("tensor", lambda E, P=ctx.P[6], v=vT[kt],
                                x=ctx.at_P[1], kt=kt, hB=hB, first=(kt == 0),
                                last=(kt == nkt - 1):
                                E.matmul(P[64:128, :],
                                         v[:, hB * 64:(hB + 1) * 64], x[:, :],
                                         start=first, stop=last,
                                         tile_position=(0, 64)), inc="t")
                    g.set(("aPt", 0), "t", b.val("t"))
                    g.set(("aPt", 1), "t", b.val("t"))
                g.set(("atab", 0), "t", b.val("t"))
                g.set(("atab", 1), "t", b.val("t"))
                av_done = b.val("t")
                b.wait("vector", "t", av_done)
                g.wait("vector", "arb", 0)
                b.emit("vector", lambda E, d=ctx.at_vf, P=ctx.P[6]:
                       E.tensor_copy(d[:], P[:, :]))
                b.drain("vector")
                rv = b.emit("vector", lambda E, d=ctx.at_rb, a=ctx.at_vf,
                            P=ctx.P[7]:
                            E.scalar_tensor_tensor(d[:], P[:, :], 1024.0, a[:],
                                                   op0=ALU.mult, op1=ALU.add),
                            inc="v")
                free_bank(ctx, 6, "v", rv)
                free_bank(ctx, 7, "v", rv)
                b.wait("scalar", "v", rv)
                g.wait("scalar", "x2", 0)
                sq = b.emit("scalar", lambda E, d=ctx.x2[0], s_=ctx.at_rb:
                            E.activation(d[:], s_[:], AF.Square), inc="s")
                if first_stat:
                    use_bank(ctx, 4)
                    use_bank(ctx, 5)
                b.wait("tensor", "v", rv)
                b.emit("tensor", lambda E, P=ctx.P[4], o=ctx.onesb, x=ctx.at_rb,
                       f=first_stat:
                       E.matmul(P[0:1, 0:TB], o[:, :], x[:, :], start=f,
                                stop=False))
                b.wait("tensor", "s", sq)
                st_mm = b.emit("tensor", lambda E, P=ctx.P[5], o=ctx.onesb,
                               x=ctx.x2[0], f=first_stat:
                               E.matmul(P[0:1, 0:TB], o[:, :], x[:, :], start=f,
                                        stop=False), inc="t")
                g.set(("x2", 0), "t", st_mm)
                g.set(("aPt", 0), "t", st_mm)
                first_stat = False
                b.wait("gpsimd", "v", rv)
                sr = b.emit("gpsimd", lambda E, s_=ctx.at_rb, d=r_dram, c=c, j=j:
                            E.dma_start(out=d[c * 128:(c + 1) * 128,
                                             j * TB:(j + 1) * TB], in_=s_[:]),
                            inc="st", n=16)
                g.set(("arb", 0), "st", sr)
        b.emit("tensor", lambda E, P=ctx.P[4], o=ctx.onesb, z=ctx.at_zero:
               E.matmul(P[0:1, 0:TB], o[:, :], z[:, :], start=False, stop=True))
        scl = b.emit("tensor", lambda E, P=ctx.P[5], o=ctx.onesb, z=ctx.at_zero:
                     E.matmul(P[0:1, 0:TB], o[:, :], z[:, :], start=False,
                              stop=True), inc="t")
        b.wait("vector", "t", scl)
        b.emit("vector", lambda E, R=R, P=ctx.P[4]:
               E.reduce_sum(R[0:1, 0:1], P[0:1, 0:TB], axis=AX.X))
        b.drain("vector")
        b.emit("vector", lambda E, R=R, P=ctx.P[5]:
               E.reduce_sum(R[0:1, 1:2], P[0:1, 0:TB], axis=AX.X))
        b.drain("vector")
        b.emit("vector", lambda E, R=R, CNT=CNT:
               E.tensor_scalar_mul(R[0:1, 2:3], R[0:1, 0:1], 1.0 / CNT))
        b.drain("vector")
        b.emit("vector", lambda E, R=R:
               E.tensor_tensor(R[0:1, 3:4], R[0:1, 2:3], R[0:1, 2:3],
                               op=ALU.mult))
        b.drain("vector")
        b.emit("vector", lambda E, R=R, CNT=CNT:
               E.scalar_tensor_tensor(R[0:1, 4:5], R[0:1, 1:2], 1.0 / CNT,
                                      R[0:1, 3:4], op0=ALU.mult,
                                      op1=ALU.subtract))
        b.drain("vector")
        sv2 = b.emit("vector", lambda E, R=R:
                     E.tensor_scalar_add(R[0:1, 5:6], R[0:1, 4:5], 1e-5),
                     inc="v")
        free_bank(ctx, 4, "v", sv2)
        free_bank(ctx, 5, "v", sv2)
        b.wait("scalar", "v", sv2)
        b.emit("scalar", lambda E, R=R:
               E.activation(R[0:1, 6:7], R[0:1, 5:6], AF.Ln))
        b.drain("scalar")
        rss = b.emit("scalar", lambda E, R=R:
                     E.activation(R[0:1, 6:7], R[0:1, 6:7], AF.Exp, scale=-0.5),
                     inc="s")
        b.wait("tensor", "s", rss)
        use_bank(ctx, 4)
        bc = b.emit("tensor", lambda E, P=ctx.P[4], o=ctx.onesrow, R=R:
                    E.matmul(P[:, 0:1], o[0:1, 0:128], R[0:1, 2:3],
                             start=True, stop=True), inc="t")
        use_bank(ctx, 5)
        bc2 = b.emit("tensor", lambda E, P=ctx.P[5], o=ctx.onesrow, R=R:
                     E.matmul(P[:, 0:1], o[0:1, 0:128], R[0:1, 6:7],
                              start=True, stop=True), inc="t")
        b.wait("vector", "t", bc2)
        b.emit("vector", lambda E, d=ctx.at_sc, P=ctx.P[4]:
               E.tensor_copy(d[:, 0:1], P[:, 0:1]))
        b.drain("vector")
        mb = b.emit("vector", lambda E, d=ctx.at_sc, P=ctx.P[5]:
                    E.tensor_copy(d[:, 1:2], P[:, 0:1]), inc="v")
        free_bank(ctx, 4, "v", mb)
        free_bank(ctx, 5, "v", mb)
        b.drain("vector")
        for qb in range(SEQ // TB):
            j = s * (SEQ // TB) + qb
            xbuf = ctx.xf[j % 2]
            for c in range(NCH):
                g.wait("sync", "xf", j % 2, c)
                b.emit("sync", lambda E, d=xbuf[c], s_=x_dram, c=c, j=j:
                       E.dma_start(out=d[:], in_=s_[c * 128:(c + 1) * 128,
                                                   j * TB:(j + 1) * TB]),
                       inc="ld", n=16)
            rb_lds = {}
            def _ldrl(c):
                g.wait("sync", "arl", c % 2)
                b.emit("sync", lambda E, d=ctx.at_rl[c % 2], s_=r_dram, c=c, j=j:
                       E.dma_start(out=d[:], in_=s_[c * 128:(c + 1) * 128,
                                                   j * TB:(j + 1) * TB]),
                       inc="ld", n=16)
                rb_lds[c] = b.val("ld")
            _ldrl(0)
            _ldrl(1)
            x_ld = b.val("ld")
            for c in range(NCH):
                b.wait("vector", "ld", rb_lds[min(c + 1, NCH + 1)])
                if c == 0:
                    b.wait("vector", "ld", x_ld)
                b.emit("vector", lambda E, d=ctx.at_sc, gn=gnc, c=c:
                       E.tensor_tensor(d[:, 2:3], d[:, 1:2], gn[:, c:c + 1],
                                       op=ALU.mult))
                b.drain("vector")
                b.emit("vector", lambda E, d=ctx.at_sc, gb=gnbc, c=c:
                       E.scalar_tensor_tensor(d[:, 3:4], d[:, 2:3], -1.0,
                                              d[:, 0:1], op0=ALU.mult,
                                              op1=ALU.mult))
                b.drain("vector")
                b.emit("vector", lambda E, d=ctx.at_sc, gb=gnbc, c=c:
                       E.tensor_tensor(d[:, 3:4], d[:, 3:4], gb[:, c:c + 1],
                                       op=ALU.add))
                b.drain("vector")
                g.wait("vector", "tmp", c % 4)
                b.emit("vector", lambda E, d=ctx.tmp[c % 4], s_=ctx.at_rl[c % 2],
                       sc=ctx.at_sc:
                       E.tensor_scalar(d[:], s_[:], sc[:, 2:3], sc[:, 3:4],
                                       op0=ALU.mult, op1=ALU.add))
                b.drain("vector")
                rv2 = b.emit("vector", lambda E, d=xbuf[c], t=ctx.tmp[c % 4]:
                             E.tensor_tensor(d[:], d[:], t[:], op=ALU.add),
                             inc="v")
                g.set(("tmp", c % 4), "v", rv2)
                g.set(("arl", c % 2), "v", rv2)
                if c + 2 < NCH:
                    _ldrl(c + 2)
                elif c + 2 == NCH:
                    rb_lds[NCH] = dummy_loads(ctx, r_dram, 1)
                    rb_lds[NCH + 1] = dummy_loads(ctx, r_dram, 1)
                b.wait("gpsimd", "v", rv2)
                sv3 = b.emit("gpsimd", lambda E, s_=xbuf[c], d=xout_dram,
                             c=c, j=j:
                             E.dma_start(out=d[c * 128:(c + 1) * 128,
                                              j * TB:(j + 1) * TB], in_=s_[:]),
                             inc="st", n=16)
                g.set(("xf", j % 2, c), "st", sv3)
        tend = b.val("t")
        for c in range(NCH):
            g.set(("aqk", c), "t", tend)
        for i in range(SEQ // 128):
            g.set(("avT", i), "t", tend)
        for i in range(NCH):
            g.set(("avw", i), "t", tend)
    return b.val("st")


def alloc_l1ab(ctx):
    b = ctx.b
    ctx.lb_kv = [b.sbuf(f"lkv{i}", (128, 1024), BF16) for i in range(NCH)]
    ctx.lb_kc = b.sbuf("lkc", (128, NCH), BF16)


def l1ab_run(ctx, x_dram, xn_dram, xout_dram, W, ln_st):
    b = ctx.b
    g = ctx.g
    EPS = 1e-6
    qF, aw, vw = ctx.at_qF, ctx.at_w, ctx.at_vw
    ktm = ctx.at_vT
    vtm = [ctx.at_kF[i // 2][:, (i % 2) * 1024:(i % 2 + 1) * 1024]
           for i in range(16)]
    for s in range(NSEQ):
        for jb in range(SEQ // TB):
            j = s * (SEQ // TB) + jb
            nset = len(ctx.ffn_xnq)
            xnq = ctx.ffn_xnq[j % nset]
            b.wait("sync", "st", ln_st[j])
            for c in range(NCH):
                g.wait("sync", "xnq", j % nset, c)
                b.emit("sync", lambda E, d=xnq[c], s_=xn_dram, c=c, j=j:
                       E.dma_start(out=d[:], in_=s_[c * 128:(c + 1) * 128,
                                                   j * TB:(j + 1) * TB]),
                       inc="ld", n=16)
            for wname, dst, gname, relu in (("wv", vtm, "akT", False),
                                            ("wk", ktm, "avT", True)):
                vlds = {}
                def _ldv(c2, wname=wname):
                    p = c2 % 4
                    g.wait("sync", "avw", p)
                    b.emit("sync", lambda E, d=vw[p], s_=W[wname], c2=c2:
                           E.dma_start(out=d[:],
                                       in_=s_[c2 * 128:(c2 + 1) * 128, :]),
                           inc="ld", n=16)
                    vlds[c2] = b.val("ld")
                for c2 in range(3):
                    _ldv(c2)
                for c2 in range(NCH):
                    if c2 + 3 < NCH:
                        _ldv(c2 + 3)
                    elif c2 + 3 == NCH:
                        vlds[NCH] = dummy_loads(ctx, W[wname], 1)
                        vlds[NCH + 1] = dummy_loads(ctx, W[wname], 1)
                    b.wait("tensor", "ld", vlds[min(c2 + 2, NCH + 1)])
                    if c2 == 0:
                        b.grace()
                    p = c2 % 4
                    bank = c2 % 2
                    use_bank(ctx, bank)
                    for k in range(NCH):
                        b.emit("tensor", lambda E, P=ctx.P[bank], w=vw[p],
                               x=xnq[k], k=k:
                               E.matmul(P[:, :], w[:, k * 128:(k + 1) * 128],
                                        x[:], start=(k == 0),
                                        stop=(k == NCH - 1)),
                               inc=("t" if k == NCH - 1 else None))
                    vmm = b.val("t")
                    g.set(("avw", p), "t", vmm)
                    b.wait("vector", "t", vmm)
                    g.wait("vector", "vfs", 0)
                    g.wait("vector", "prep", 0)
                    if relu:
                        ev = b.emit("vector", lambda E, d=ctx.at_vf,
                                    P=ctx.P[bank]:
                                    E.tensor_scalar(d[:], P[:, :], 0.0, EPS,
                                                    op0=ALU.max, op1=ALU.add),
                                    inc="v")
                    else:
                        ev = b.emit("vector", lambda E, d=ctx.at_vf,
                                    P=ctx.P[bank]:
                                    E.tensor_copy(d[:], P[:, :]), inc="v")
                    free_bank(ctx, bank, "v", ev)
                    b.wait("tensor", "v", ev)
                    for ts in range(4):
                        bank2 = 2 + ts % 2
                        use_bank(ctx, bank2)
                        tt = b.emit("tensor", lambda E, P=ctx.P[bank2],
                                    s_=ctx.at_vf, idn=ctx.identf, ts=ts:
                                    E.transpose(P[:, 0:128],
                                                s_[:, ts * 128:(ts + 1) * 128],
                                                idn[:]), inc="t")
                        b.wait("vector", "t", tt)
                        g.wait("vector", gname, jb * 4 + ts)
                        tv = b.emit("vector", lambda E, d=dst[jb * 4 + ts],
                                    P=ctx.P[bank2], c2=c2:
                                    E.tensor_copy(d[:, c2 * 128:(c2 + 1) * 128],
                                                  P[:, 0:128]), inc="v")
                        free_bank(ctx, bank2, "v", tv)
                    g.set(("vfs", 0), "t", b.val("t"))
                    g.set(("prep", 0), "t", b.val("t"))
            lds = {}
            def _ldq(c):
                p = c % 3
                g.wait("sync", "aw", p)
                b.emit("sync", lambda E, d=aw[p], s_=W["wq"], c=c:
                       E.dma_start(out=d[:], in_=s_[c * 128:(c + 1) * 128, :]),
                       inc="ld", n=16)
                lds[c] = b.val("ld")
            _ldq(0)
            _ldq(1)
            for c in range(NCH):
                if c + 2 < NCH:
                    _ldq(c + 2)
                else:
                    lds[c + 2] = dummy_loads(ctx, W["wq"], 2)
                b.wait("tensor", "ld", lds[c + 2])
                if c == 0:
                    b.grace()
                p = c % 3
                bank = c % 2
                use_bank(ctx, bank)
                for k in range(NCH):
                    b.emit("tensor", lambda E, P=ctx.P[bank], w=aw[p],
                           x=xnq[k], k=k:
                           E.matmul(P[:, :], w[:, k * 128:(k + 1) * 128], x[:],
                                    start=(k == 0), stop=(k == NCH - 1)),
                           inc=("t" if k == NCH - 1 else None))
                mmq = b.val("t")
                g.set(("aw", p), "t", mmq)
                b.wait("vector", "t", mmq)
                g.wait("vector", "aqk", c)
                vq = b.emit("vector", lambda E, d=qF[c], P=ctx.P[bank], c=c,
                            jb=jb:
                            E.tensor_scalar(d[:, jb * TB:(jb + 1) * TB],
                                            P[:, :], 0.0, EPS, op0=ALU.max,
                                            op1=ALU.add), inc="v")
                free_bank(ctx, bank, "v", vq)
                g.set(("aqk", c), "v", vq)
            for c2 in range(NCH):
                g.set(("xnq", j % nset, c2), "t", b.val("t"))
        b.wait("tensor", "v", b.val("v"))
        use_bank(ctx, 6)
        for ds in range(NCH):
            bank = (ds % 2) * 2
            use_bank(ctx, bank)
            use_bank(ctx, bank + 1)
            for kt in range(16):
                for half in range(2):
                    b.emit("tensor", lambda E, PA=ctx.PSALL, x=ktm[kt],
                           v=vtm[kt], ds=ds, half=half, bank=bank, kt=kt:
                           E.matmul(PA[:, (bank + half) * 512:
                                        (bank + half + 1) * 512],
                                    x[:, ds * 128:(ds + 1) * 128],
                                    v[:, half * 512:(half + 1) * 512],
                                    start=(kt == 0), stop=(kt == 15)))
                b.emit("tensor", lambda E, P=ctx.P[6], x=ktm[kt],
                       o=ctx.onesb, ds=ds, kt=kt:
                       E.matmul(P[:, ds:ds + 1], x[:, ds * 128:(ds + 1) * 128],
                                o[:, :], start=(kt == 0), stop=(kt == 15)),
                       inc=("t" if kt == 15 else None))
            kvm = b.val("t")
            b.wait("vector", "t", kvm)
            g.wait("vector", "lkv", ds)
            kve = b.emit("vector", lambda E, d=ctx.lb_kv[ds], PA=ctx.PSALL,
                         bank=bank:
                         E.tensor_copy(d[:, :],
                                       PA[:, bank * 512:(bank + 2) * 512]),
                         inc="v")
            free_bank(ctx, bank, "v", kve)
            free_bank(ctx, bank + 1, "v", kve)
        g.wait("vector", "lkc", 0)
        kce = b.emit("vector", lambda E, d=ctx.lb_kc, P=ctx.P[6]:
                     E.tensor_copy(d[:, :], P[:, 0:NCH]), inc="v")
        free_bank(ctx, 6, "v", kce)
        for jb in range(SEQ // TB):
            j = s * (SEQ // TB) + jb
            cbuf = ctx.at_ctx[jb % 2]
            xbuf = ctx.xf[j % 2]
            for c in range(NCH):
                g.wait("sync", "xf", j % 2, c)
                b.emit("sync", lambda E, d=xbuf[c], s_=x_dram, c=c, j=j:
                       E.dma_start(out=d[:], in_=s_[c * 128:(c + 1) * 128,
                                                   j * TB:(j + 1) * TB]),
                       inc="ld", n=16)
            x_ld = b.val("ld")
            b.wait("tensor", "v", kce)
            use_bank(ctx, 7)
            for dc in range(NCH):
                b.emit("tensor", lambda E, P=ctx.P[7], kc=ctx.lb_kc, q=qF[dc],
                       dc=dc, jb=jb:
                       E.matmul(P[0:1, 0:TB], kc[:, dc:dc + 1],
                                q[:, jb * TB:(jb + 1) * TB],
                                start=(dc == 0), stop=(dc == NCH - 1)),
                       inc=("t" if dc == NCH - 1 else None))
            zmm = b.val("t")
            R = ctx.at_rows
            b.wait("vector", "t", zmm)
            b.emit("vector", lambda E, R=R, P=ctx.P[7]:
                   E.tensor_scalar_add(R[0:1, 0:TB], P[0:1, 0:TB], EPS))
            b.drain("vector")
            zv = b.emit("vector", lambda E, R=R:
                        E.reciprocal(R[0:1, TB:2 * TB], R[0:1, 0:TB]),
                        inc="v")
            free_bank(ctx, 7, "v", zv)
            b.wait("tensor", "v", zv)
            use_bank(ctx, 6)
            b.emit("tensor", lambda E, P=ctx.P[6], o=ctx.onesrow, R=R:
                   E.matmul(P[0:64, :], o[0:1, 0:64], R[0:1, TB:2 * TB],
                            start=True, stop=True, tile_position=(0, 0)))
            zb = b.emit("tensor", lambda E, P=ctx.P[6], o=ctx.onesrow, R=R:
                        E.matmul(P[64:128, :], o[0:1, 0:64], R[0:1, TB:2 * TB],
                                 start=True, stop=True, tile_position=(0, 64)),
                        inc="t")
            for es in range(NCH):
                bank = es % 2
                use_bank(ctx, bank)
                for dc in range(NCH):
                    b.emit("tensor", lambda E, P=ctx.P[bank], kv=ctx.lb_kv[dc],
                           q=qF[dc], es=es, dc=dc, jb=jb:
                           E.matmul(P[:, :], kv[:, es * 128:(es + 1) * 128],
                                    q[:, jb * TB:(jb + 1) * TB],
                                    start=(dc == 0), stop=(dc == NCH - 1)),
                           inc=("t" if dc == NCH - 1 else None))
                amm = b.val("t")
                b.wait("vector", "t", amm)
                g.wait("vector", "actx", jb % 2, es)
                b.emit("vector", lambda E, d=cbuf[es], P=ctx.P[bank]:
                       E.tensor_copy(d[:], P[:, :]))
                b.drain("vector")
                av = b.emit("vector", lambda E, d=cbuf[es], P=ctx.P[6]:
                            E.scalar_tensor_tensor(d[:], d[:], 1.0, P[:, :],
                                                   op0=ALU.mult, op1=ALU.mult),
                            inc="v")
                free_bank(ctx, bank, "v", av)
            ctx_ready = b.val("v")
            free_bank(ctx, 6, "v", ctx_ready)
            lds = {}
            def _ldo(ms):
                p = ms % 3
                g.wait("sync", "aw", p)
                b.emit("sync", lambda E, d=aw[p], s_=W["wo"], ms=ms:
                       E.dma_start(out=d[:], in_=s_[ms * 128:(ms + 1) * 128, :]),
                       inc="ld", n=16)
                lds[ms] = b.val("ld")
            _ldo(0)
            _ldo(1)
            bpr = ctx.consts["l7_bout"]
            for ms in range(NCH):
                if ms + 2 < NCH:
                    _ldo(ms + 2)
                else:
                    lds[ms + 2] = dummy_loads(ctx, W["wo"], 2)
                b.wait("tensor", "ld", lds[ms + 2])
                if ms == 0:
                    b.grace()
                    b.wait("tensor", "v", ctx_ready)
                p = ms % 3
                bank = ms % 2
                use_bank(ctx, bank)
                for kc in range(NCH):
                    b.emit("tensor", lambda E, P=ctx.P[bank], w=aw[p],
                           x=cbuf[kc], kc=kc:
                           E.matmul(P[:, :], w[:, kc * 128:(kc + 1) * 128],
                                    x[:], start=(kc == 0),
                                    stop=(kc == NCH - 1)),
                           inc=("t" if kc == NCH - 1 else None))
                mmo = b.val("t")
                g.set(("aw", p), "t", mmo)
                b.wait("vector", "t", mmo)
                g.wait("vector", "tmp", ms % 4)
                ov = b.emit("vector", lambda E, d=ctx.tmp[ms % 4],
                            P=ctx.P[bank], bb=bpr, ms=ms:
                            E.tensor_scalar_add(d[:], P[:, :], bb[:, ms:ms + 1]),
                            inc="v")
                free_bank(ctx, bank, "v", ov)
                b.drain("vector")
                b.wait("vector", "ld", x_ld)
                rv = b.emit("vector", lambda E, d=xbuf[ms], t=ctx.tmp[ms % 4]:
                            E.tensor_tensor(d[:], d[:], t[:], op=ALU.add),
                            inc="v")
                g.set(("tmp", ms % 4), "v", rv)
                b.wait("gpsimd", "v", rv)
                stv = b.emit("gpsimd", lambda E, s_=xbuf[ms], d=xout_dram,
                             ms=ms, j=j:
                             E.dma_start(out=d[ms * 128:(ms + 1) * 128,
                                              j * TB:(j + 1) * TB], in_=s_[:]),
                             inc="st", n=16)
                g.set(("xf", j % 2, ms), "st", stv)
            for es in range(NCH):
                g.set(("actx", jb % 2, es), "t", b.val("t"))
        tend = b.val("t")
        for c in range(NCH):
            g.set(("aqk", c), "t", tend)
        for i in range(SEQ // 128):
            g.set(("avT", i), "t", tend)
            g.set(("akT", i), "t", tend)
        for i in range(4):
            g.set(("avw", i), "t", tend)
        for ds in range(NCH):
            g.set(("lkv", ds), "t", tend)
        g.set(("lkc", 0), "t", tend)
    return b.val("st")

from concourse.bass_utils import run_bass_kernel_spmd

NB_BUCKETS, MAX_DIST = 32, 128


def bucket_of(rel):
    nb = NB_BUCKETS // 2
    ret = (rel > 0).astype(np.int64) * nb
    n = np.abs(rel)
    max_exact = nb // 2
    is_small = n < max_exact
    n_safe = np.maximum(n, 1).astype(np.float64)
    val_large = max_exact + (np.log(n_safe / max_exact) /
                             np.log(MAX_DIST / max_exact)
                             * (nb - max_exact)).astype(np.int64)
    val_large = np.minimum(val_large, nb - 1)
    return ret + np.where(is_small, n, val_large)


def build_a2(rel_bias):
    idx = np.arange(4095) - 2047
    buck = bucket_of(idx)
    g = rel_bias[buck, :]
    p = np.arange(128)[:, None]
    n = np.arange(3968)[None, :]
    gather = 3967 + p - n
    A2 = g[gather]
    A2 = np.ascontiguousarray(A2.transpose(2, 0, 1)).reshape(16 * 128, 3968)
    return A2.astype(np.float32)


def build_decay():
    gamma = 1.0 - 2.0 ** (-5.0 - np.arange(16))
    i = np.arange(4095)
    out = np.zeros((16, 4095), np.float64)
    for h in range(16):
        rel = i - 2047.0
        out[h] = np.where(rel <= 0, gamma[h] ** np.maximum(-rel, 0), 0.0)
    p = np.arange(128)[:, None]
    n = np.arange(3968)[None, :]
    gather = 3967 + p - n
    A = out[:, gather]
    return np.ascontiguousarray(A.reshape(16 * 128, 3968)).astype(np.float32)


def build_l0attna(nc):
    b = Builder(nc)
    x_in = nc.dram_tensor("x", [D, TOK], F32, kind="ExternalInput").ap()
    idn = nc.dram_tensor("idn", [128, 128], F32, kind="ExternalInput").ap()
    x_out = nc.dram_tensor("y", [D, TOK], F32, kind="ExternalOutput").ap()
    wP = {}
    for nm in ("wq", "wk", "wv", "wo"):
        wP[nm] = nc.dram_tensor(nm + "P", [8, 128, 1024], F32,
                                kind="ExternalInput")
    a2f = nc.dram_tensor("a2f", [16 * 128, 3968], F32, kind="ExternalInput")
    sc = {}
    for nm in ("l1_g", "l1_b"):
        sc[nm] = nc.dram_tensor(nm, [128, 8], F32, kind="ExternalInput")
    xn = nc.dram_tensor("xn_scr", [D, TOK], BF16, kind="Internal").ap()
    U = nc.dram_tensor("u_scr", [D, TOK], F32, kind="Internal").ap()
    W = {}
    for nm in ("wq", "wk", "wv", "wo"):
        W[nm] = nc.dram_tensor(nm + "B", [1024, 1024], BF16,
                               kind="Internal").ap()
    A2 = nc.dram_tensor("a2B", [16 * 128, 3968], BF16, kind="Internal").ap()
    ctx = Ctx(b)
    ctx.alloc_xnq(nsets=1)
    alloc_attn(ctx, tables=True)
    ctx.init_const_tiles(idn)
    ctx.load_consts({nm: (sc[nm].ap()[:, :], [128, 8], F32) for nm in sc})
    for nm in ("wq", "wk", "wv", "wo"):
        pd = prep_cast(ctx, wP[nm].ap().rearrange("m p d -> (m p) d"), W[nm],
                       1024, 1024)
    pd = prep_cast(ctx, a2f.ap()[:, :], A2, 16 * 128, 3968)
    b.wait("sync", "pst", pd)
    ln_st = ln_phase(ctx, x_in, xn, u_dram=U,
                     aff=dict(gamma="l1_g", beta="l1_b", eps2=1e-6))
    l0attna_run(ctx, x_in, xn, U, x_out, W, ln_st, A2)
    b.wait("gpsimd", "st", b.val("st"))
    b.finish()
    return nc


def build_l0ffn(nc):
    b = Builder(nc)
    x_in = nc.dram_tensor("x", [D, TOK], F32, kind="ExternalInput").ap()
    idn = nc.dram_tensor("idn", [128, 128], F32, kind="ExternalInput").ap()
    x_out = nc.dram_tensor("y", [D, TOK], F32, kind="ExternalOutput").ap()
    fc1q = nc.dram_tensor("fc1q", [32, 128, 1024], I32, kind="ExternalInput")
    fc2q = nc.dram_tensor("fc2q", [8, 128, 4096], I32, kind="ExternalInput")
    fc1s = nc.dram_tensor("fc1s", [32, 128, 1024], F32, kind="ExternalInput")
    fc2s = nc.dram_tensor("fc2s", [8, 128, 4096], F32, kind="ExternalInput")
    sc = {}
    for nm, shp in [("l2_g", (128, 8)), ("l2_b", (128, 8)),
                    ("l2_b1", (128, 32)), ("l2_b2", (128, 8))]:
        sc[nm] = nc.dram_tensor(nm, list(shp), F32, kind="ExternalInput")
    xn = nc.dram_tensor("xn_scr", [D, TOK], BF16, kind="Internal").ap()
    U = nc.dram_tensor("u_scr", [D, TOK], F32, kind="Internal").ap()
    W = {
        "fc1": nc.dram_tensor("fc1B", [32 * 128, 1024], BF16,
                              kind="Internal").ap(),
        "fc2": nc.dram_tensor("fc2B", [8 * 128, 4096], BF16,
                              kind="Internal").ap(),
    }
    ctx = Ctx(b)
    ctx.prep_q = [b.sbuf(f"prq{i}", (128, 512), I32) for i in range(2)]
    ctx.prep_f2 = [b.sbuf(f"prf2{i}", (128, 512), F32) for i in range(2)]
    ctx.alloc_ffn()
    ctx.init_const_tiles(idn)
    ctx.load_consts({nm: (sc[nm].ap()[:, :], list(sc[nm].shape), F32)
                     for nm in sc})
    b.wait("vector", "ld", b.val("ld"))
    t = b.sbuf("c_l2_b2h", (128, 8), F32)
    ctx.consts["l2_b2h"] = t
    b.emit("vector", lambda E, d=t, s=ctx.consts["l2_b2"]:
           E.tensor_scalar_mul(d[:], s[:], 0.5), inc="v")
    for eng in ("scalar", "tensor"):
        b.wait(eng, "v", b.val("v"))
    prep_dequant(ctx, fc1q.ap().rearrange("m p d -> (m p) d"),
                 fc1s.ap().rearrange("m p d -> (m p) d"), W["fc1"],
                 32 * 128, 1024)
    pd = prep_dequant(ctx, fc2q.ap().rearrange("m p d -> (m p) d"),
                      fc2s.ap().rearrange("m p d -> (m p) d"), W["fc2"],
                      8 * 128, 4096)
    b.wait("sync", "pst", pd)
    ln_st = ln_phase(ctx, x_in, xn, u_dram=U,
                     aff=dict(gamma="l2_g", beta="l2_b", eps2=1e-5))
    l0ffn(ctx, x_in, xn, U, x_out, W, ln_st)
    b.wait("gpsimd", "st", b.val("st"))
    b.finish()
    return nc


def build_l0attnb(nc):
    b = Builder(nc)
    x_in = nc.dram_tensor("x", [D, TOK], F32, kind="ExternalInput").ap()
    idn = nc.dram_tensor("idn", [128, 128], F32, kind="ExternalInput").ap()
    x_out = nc.dram_tensor("y", [D, TOK], F32, kind="ExternalOutput").ap()
    wP = {}
    for nm in ("wq", "wk", "wo"):
        wP[nm] = nc.dram_tensor(nm + "P", [8, 128, 1024], F32,
                                kind="ExternalInput")
    wP["wv"] = nc.dram_tensor("wvN", [8, 128, 1024], F32,
                              kind="ExternalInput")
    sc = {}
    for nm, shp in [("l3_bqkv", (128, 24)), ("l3_qkn", (128, 1)),
                    ("l3_bproj", (128, 8))]:
        sc[nm] = nc.dram_tensor(nm, list(shp), F32, kind="ExternalInput")
    xn = nc.dram_tensor("xn_scr", [D, TOK], BF16, kind="Internal").ap()
    W = {}
    for nm in ("wq", "wk", "wv", "wo"):
        W[nm] = nc.dram_tensor(nm + "B", [1024, 1024], BF16,
                               kind="Internal").ap()
    ctx = Ctx(b)
    ctx.alloc_xnq()
    alloc_attn(ctx)
    ctx.init_const_tiles(idn)
    ctx.load_consts({nm: (sc[nm].ap()[:, :], list(sc[nm].shape), F32)
                     for nm in sc})
    for nm in ("wq", "wk", "wo"):
        pd = prep_cast(ctx, wP[nm].ap().rearrange("m p d -> (m p) d"), W[nm],
                       1024, 1024)
    pd = prep_cast(ctx, wP["wv"].ap().rearrange("m p d -> (m p) d"), W["wv"],
                   1024, 1024)
    b.wait("sync", "pst", pd)
    ln_st = ln_phase(ctx, x_in, xn)
    l0attnb_run(ctx, x_in, xn, x_out, W, ln_st)
    b.wait("gpsimd", "st", b.val("st"))
    b.finish()
    return nc


def build_l1aa(nc):
    b = Builder(nc)
    x_in = nc.dram_tensor("x", [D, TOK], F32, kind="ExternalInput").ap()
    idn = nc.dram_tensor("idn", [128, 128], F32, kind="ExternalInput").ap()
    x_out = nc.dram_tensor("y", [D, TOK], F32, kind="ExternalOutput").ap()
    wP = {}
    for nm in ("wq", "wk", "wv"):
        wP[nm] = nc.dram_tensor(nm + "P", [8, 128, 1024], F32,
                                kind="ExternalInput")
    dcf = nc.dram_tensor("dcf", [16 * 128, 3968], F32, kind="ExternalInput")
    pkvf = nc.dram_tensor("pkvf", [128, 1024], F32, kind="ExternalInput")
    sc = {}
    for nm in ("l5_gnc", "l5_gnbc"):
        sc[nm] = nc.dram_tensor(nm, [128, 8], F32, kind="ExternalInput")
    xn = nc.dram_tensor("xn_scr", [D, TOK], BF16, kind="Internal").ap()
    rscr = nc.dram_tensor("r_scr", [D, TOK], BF16, kind="Internal").ap()
    W = {}
    for nm in ("wq", "wk", "wv"):
        W[nm] = nc.dram_tensor(nm + "B", [1024, 1024], BF16,
                               kind="Internal").ap()
    W["pkv"] = nc.dram_tensor("pkvB", [128, 1024], BF16, kind="Internal").ap()
    DC = nc.dram_tensor("dcB", [16 * 128, 3968], BF16, kind="Internal").ap()
    ctx = Ctx(b)
    ctx.alloc_xnq(nsets=1)
    alloc_attn(ctx, tables=True, need_ctx=False)
    alloc_l1aa(ctx)
    ctx.init_const_tiles(idn)
    ctx.load_consts({nm: (sc[nm].ap()[:, :], [128, 8], F32) for nm in sc})
    for nm in ("wq", "wk", "wv"):
        pd = prep_cast(ctx, wP[nm].ap().rearrange("m p d -> (m p) d"), W[nm],
                       1024, 1024)
    pd = prep_cast(ctx, pkvf.ap()[:, :], W["pkv"], 128, 1024)
    pd = prep_cast(ctx, dcf.ap()[:, :], DC, 16 * 128, 3968)
    b.wait("sync", "pst", pd)
    ln_st = ln_phase(ctx, x_in, xn)
    l1aa_run(ctx, x_in, xn, x_out, W, ln_st, DC, rscr)
    b.wait("gpsimd", "st", b.val("st"))
    b.finish()
    return nc


def build_l1ffn(nc):
    b = Builder(nc)
    x_in = nc.dram_tensor("x", [D, TOK], F32, kind="ExternalInput").ap()
    idn = nc.dram_tensor("idn", [128, 128], F32, kind="ExternalInput").ap()
    x_out = nc.dram_tensor("y", [D, TOK], F32, kind="ExternalOutput").ap()
    wP = {}
    for nm, shp in [("w1", (32, 128, 1024)), ("w2", (32, 128, 1024)),
                    ("h1", (32, 128, 4096)), ("h2", (32, 128, 4096)),
                    ("w3", (8, 128, 4096))]:
        wP[nm] = nc.dram_tensor(nm + "P", list(shp), F32,
                                kind="ExternalInput")
    sc = {}
    for nm, shp in [("l6_s1", (128, 32)), ("l6_s2", (128, 32)),
                    ("l6_s1h", (128, 32)), ("l6_s2h", (128, 32)),
                    ("l6_s3", (128, 8))]:
        sc[nm] = nc.dram_tensor(nm, list(shp), F32, kind="ExternalInput")
    xn = nc.dram_tensor("xn_scr", [D, TOK], BF16, kind="Internal").ap()
    W = {}
    for nm, shp in [("w1", (32 * 128, 1024)), ("w2", (32 * 128, 1024)),
                    ("h1", (32 * 128, 4096)), ("h2", (32 * 128, 4096)),
                    ("w3", (8 * 128, 4096))]:
        W[nm] = nc.dram_tensor(nm + "B", list(shp), BF16,
                               kind="Internal").ap()
    ctx = Ctx(b)
    ctx.alloc_ffn()
    ctx.init_const_tiles(idn)
    ctx.load_consts({nm: (sc[nm].ap()[:, :], list(sc[nm].shape), F32)
                     for nm in sc})
    for nm in ("w1", "w2", "h1", "h2", "w3"):
        shp = wP[nm].shape
        src2d = wP[nm].ap().rearrange("m p d -> (m p) d")
        prep_done = prep_cast(ctx, src2d, W[nm], shp[0] * 128, shp[2])
    b.wait("sync", "pst", prep_done)
    ln_st = ln_phase(ctx, x_in, xn)
    l1ffn(ctx, x_in, xn, x_out, W, ln_st)
    b.wait("gpsimd", "st", b.val("st"))
    b.finish()
    return nc


def build_l1ab(nc):
    b = Builder(nc)
    x_in = nc.dram_tensor("x", [D, TOK], F32, kind="ExternalInput").ap()
    idn = nc.dram_tensor("idn", [128, 128], F32, kind="ExternalInput").ap()
    x_out = nc.dram_tensor("y", [D, TOK], F32, kind="ExternalOutput").ap()
    wP = {}
    for nm in ("wq", "wk", "wv", "wo"):
        wP[nm] = nc.dram_tensor(nm + "P", [8, 128, 1024], F32,
                                kind="ExternalInput")
    sc = {"l7_bout": nc.dram_tensor("l7_bout", [128, 8], F32,
                                    kind="ExternalInput")}
    xn = nc.dram_tensor("xn_scr", [D, TOK], BF16, kind="Internal").ap()
    W = {}
    for nm in ("wq", "wk", "wv", "wo"):
        W[nm] = nc.dram_tensor(nm + "B", [1024, 1024], BF16,
                               kind="Internal").ap()
    ctx = Ctx(b)
    ctx.alloc_xnq(nsets=1)
    alloc_attn(ctx, tables=False, need_ctx=True, n_aw=3)
    alloc_l1ab(ctx)
    ctx.init_const_tiles(idn)
    ctx.load_consts({nm: (sc[nm].ap()[:, :], [128, 8], F32) for nm in sc})
    for nm in ("wq", "wk", "wv", "wo"):
        pd = prep_cast(ctx, wP[nm].ap().rearrange("m p d -> (m p) d"), W[nm],
                       1024, 1024)
    b.wait("sync", "pst", pd)
    ln_st = ln_phase(ctx, x_in, xn)
    l1ab_run(ctx, x_in, xn, x_out, W, ln_st)
    b.wait("gpsimd", "st", b.val("st"))
    b.finish()
    return nc



_prog_cache = {}


def _get_prog(name, builder):
    if name not in _prog_cache:
        nc = bass.Bass("TRN2", target_bir_lowering=False)
        _prog_cache[name] = builder(nc)
    return _prog_cache[name]


def kernel(**inputs):
    x = inputs["L0_attn_a_hidden_states"]
    idn = np.eye(128, dtype=np.float32)
    xs = [np.ascontiguousarray(x[2 * c:2 * c + 2].reshape(TOK, D).T)
          for c in range(8)]

    l1_in = {
        "wqP": lhsT_pack(inputs["L0_attn_a_Wq"]),
        "wkP": lhsT_pack(inputs["L0_attn_a_Wk"]),
        "wvP": lhsT_pack(inputs["L0_attn_a_Wv"]),
        "woP": lhsT_pack(inputs["L0_attn_a_Wo"]),
        "a2f": build_a2(inputs["L0_attn_a_rel_bias"]),
        "l1_g": col_pack(inputs["L0_attn_a_ln_g"]),
        "l1_b": col_pack(inputs["L0_attn_a_ln_b"]),
    }
    l2_in = {
        "fc1q": lhsT_pack(inputs["L0_ffn_fc1_q"]),
        "fc2q": lhsT_pack(inputs["L0_ffn_fc2_q"]),
        "fc1s": lhsT_pack(np.repeat(inputs["L0_ffn_fc1_scales"], 128,
                                    axis=0)),
        "fc2s": lhsT_pack(np.repeat(inputs["L0_ffn_fc2_scales"], 128,
                                    axis=0)),
        "l2_g": col_pack(inputs["L0_ffn_ln_gamma"]),
        "l2_b": col_pack(inputs["L0_ffn_ln_beta"]),
        "l2_b1": col_pack(inputs["L0_ffn_fc1_bias"]),
        "l2_b2": col_pack(inputs["L0_ffn_fc2_bias"]),
    }
    Wqkv3 = inputs["L0_attn_b_W_qkv"]
    bq3 = inputs["L0_attn_b_b_qkv"]
    l3_in = {
        "wqP": lhsT_pack(Wqkv3[:, 0:1024]),
        "wkP": lhsT_pack(Wqkv3[:, 1024:2048]),
        "wvN": lhsT_pack(Wqkv3[:, 2048:3072]),
        "woP": lhsT_pack(inputs["L0_attn_b_W_proj"]),
        "l3_bqkv": np.concatenate([col_pack(bq3[0:1024]),
                                   col_pack(bq3[1024:2048]),
                                   col_pack(bq3[2048:3072])], axis=1),
        "l3_qkn": np.concatenate([inputs["L0_attn_b_qn_w"],
                                  inputs["L0_attn_b_kn_w"]]).reshape(128, 1),
        "l3_bproj": col_pack(inputs["L0_attn_b_b_proj"]),
    }
    Wqkv5 = inputs["L1_attn_a_W_qkv"]
    past_kv = inputs["L1_attn_a_past_kv"]
    pkvf = np.zeros((128, 1024), np.float32)
    for h in range(16):
        pkvf[0:64, h * 64:(h + 1) * 64] = past_kv[h]
        pkvf[64:128, h * 64:(h + 1) * 64] = past_kv[h]
    gn_w, gn_b = inputs["L1_attn_a_gn_weight"], inputs["L1_attn_a_gn_bias"]
    gnc = np.zeros((128, 8), np.float32)
    gnbc = np.zeros((128, 8), np.float32)
    for c in range(8):
        gnc[0:64, c] = gn_w[2 * c]
        gnc[64:128, c] = gn_w[2 * c + 1]
        gnbc[0:64, c] = gn_b[2 * c]
        gnbc[64:128, c] = gn_b[2 * c + 1]
    l5_in = {
        "wqP": lhsT_pack(Wqkv5[:, 0:1024]),
        "wkP": lhsT_pack(Wqkv5[:, 1024:2048]),
        "wvP": lhsT_pack(Wqkv5[:, 2048:3072]),
        "dcf": build_decay(), "pkvf": pkvf,
        "l5_gnc": gnc, "l5_gnbc": gnbc,
    }
    l6_in = {
        "w1P": lhsT_pack(inputs["L1_ffn_w1_w"].T),
        "w2P": lhsT_pack(inputs["L1_ffn_w2_w"].T),
        "h1P": lhsT_pack(inputs["L1_ffn_h1_w"].T),
        "h2P": lhsT_pack(inputs["L1_ffn_h2_w"].T),
        "w3P": lhsT_pack(inputs["L1_ffn_w3_w"].T),
        "l6_s1": col_pack(inputs["L1_ffn_w1_s"]),
        "l6_s2": col_pack(inputs["L1_ffn_w2_s"]),
        "l6_s1h": col_pack(inputs["L1_ffn_h1_s"]),
        "l6_s2h": col_pack(inputs["L1_ffn_h2_s"]),
        "l6_s3": col_pack(inputs["L1_ffn_w3_s"]),
    }
    Wqkv7 = inputs["L1_attn_b_W_qkv"]
    l7_in = {
        "wqP": lhsT_pack(Wqkv7[0:1024].T),
        "wkP": lhsT_pack(Wqkv7[1024:2048].T),
        "wvP": lhsT_pack(Wqkv7[2048:3072].T),
        "woP": lhsT_pack(inputs["L1_attn_b_W_out"].T),
        "l7_bout": col_pack(inputs["L1_attn_b_b_out"]),
    }
    l1_in = {k: np.ascontiguousarray(v) for k, v in l1_in.items()}
    l2_in = {k: np.ascontiguousarray(v) for k, v in l2_in.items()}
    l3_in = {k: np.ascontiguousarray(v) for k, v in l3_in.items()}
    l5_in = {k: np.ascontiguousarray(v) for k, v in l5_in.items()}
    l6_in = {k: np.ascontiguousarray(v) for k, v in l6_in.items()}
    l7_in = {k: np.ascontiguousarray(v) for k, v in l7_in.items()}

    seq = [("l0attna", build_l0attna, l1_in),
           ("l0ffn", build_l0ffn, l2_in),
           ("l0attnb", build_l0attnb, l3_in),
           ("l0ffn", build_l0ffn, l2_in),
           ("l1aa", build_l1aa, l5_in),
           ("l1ffn", build_l1ffn, l6_in),
           ("l1ab", build_l1ab, l7_in),
           ("l1ffn", build_l1ffn, l6_in)]
    for name, bf, win in seq:
        prog = _get_prog(name, bf)
        inms = [dict(win, idn=idn, x=xs[c]) for c in range(8)]
        res = run_bass_kernel_spmd(prog, inms, core_ids=list(range(8)))
        xs = [np.ascontiguousarray(res.results[c]["y"]) for c in range(8)]

    out = np.empty((16, 2048, 1024), np.float32)
    for c in range(8):
        out[2 * c:2 * c + 2] = xs[c].T.reshape(2, SEQ, D)
    return out


# version 02
# speedup vs baseline: 1.1252x; 1.1252x over previous
import numpy as np
import contextlib
import concourse.bass as bass
import concourse.mybir as mybir

F32 = mybir.dt.float32
BF16 = mybir.dt.bfloat16
I32 = mybir.dt.int32
AF = mybir.ActivationFunctionType
ALU = mybir.AluOpType
AX = mybir.AxisListType

ENGINES = ("sync", "vector", "scalar", "tensor", "gpsimd")

D = 1024
NCH = 8
SEQ = 2048
NSEQ = 2
TOK = SEQ * NSEQ
NH = 16
DK = 64
TB = 512
NB = TOK // TB



def lhsT_pack(W):
    din, dout = W.shape
    k, m = din // 128, dout // 128
    A = W.reshape(k, 128, m, 128)
    A = np.ascontiguousarray(A.transpose(2, 1, 0, 3))
    return A.reshape(m, 128, din)


def col_pack(v):
    n = v.shape[0]
    return np.ascontiguousarray(v.reshape(n // 128, 128).T)



class Builder:
    def __init__(self, nc, name="prog"):
        self.nc = nc
        self.name = name
        self.ops = {e: [] for e in ENGINES}
        self.sem_handles = {}
        self.sem_cnt = {}
        self.es = contextlib.ExitStack()

    def sem(self, name):
        if name not in self.sem_handles:
            self.sem_handles[name] = self.es.enter_context(self.nc.semaphore(name))
            self.sem_cnt[name] = 0
        return self.sem_handles[name]

    def bump(self, name, n):
        self.sem(name)
        self.sem_cnt[name] += n
        assert self.sem_cnt[name] < 2000000000, f"sem {name} overflow"
        return self.sem_cnt[name]

    def val(self, name):
        self.sem(name)
        return self.sem_cnt[name]

    def emit(self, eng, maker, inc=None, n=1):
        if inc is not None:
            v = self.bump(inc, n)
            h = self.sem_handles[inc]
            def run(E, maker=maker, h=h, n=n):
                maker(E).then_inc(h, n)
            self.ops[eng].append(run)
            return v
        self.ops[eng].append(lambda E, maker=maker: maker(E))
        return None

    def wait(self, eng, sem, v=None):
        if v is None:
            v = self.val(sem)
        if v <= 0:
            return
        h = self.sem(sem)
        self.ops[eng].append(lambda E, h=h, v=v: E.wait_ge(h, v))

    def drain(self, eng):
        self.ops[eng].append(lambda E: E.drain())

    def grace(self, eng="tensor", cyc=3000):
        self.ops[eng].append(lambda E: E.nop(cycle_cnt=cyc))

    def sbuf(self, name, shape, dtype):
        return self.es.enter_context(self.nc.sbuf_tensor(name, list(shape), dtype))

    def psum(self, name, shape=(128, 512), dtype=F32):
        return self.es.enter_context(self.nc.psum_tensor(name, list(shape), dtype))

    def finish(self):
        block = self.es.enter_context(self.nc.Block())
        for e in ENGINES:
            ops = self.ops[e]
            getattr(block, e)(lambda E, ops=ops: [f(E) for f in ops])
        self.es.close()
        return self.nc


class Guard:
    def __init__(self, b):
        self.b = b
        self.d = {}

    def wait(self, eng, *key):
        e = self.d.get(key)
        if e:
            self.b.wait(eng, e[0], e[1])

    def set(self, key, sem, val):
        self.d[key] = (sem, val)



class Ctx:
    def __init__(self, b: Builder, ntmp=4, with_prep_b=True):
        self.b = b
        self.g = Guard(b)
        self.NTMP = ntmp
        self.ident = b.sbuf("ident", (128, 128), BF16)
        self.identf = b.sbuf("identf", (128, 128), F32)
        self.onesb = b.sbuf("onesb", (128, 1), BF16)
        self.onesrow = b.sbuf("onesrow", (1, 128), F32)
        self.PSALL = b.psum("PSALL", (128, 4096))
        self.P = [self.PSALL[:, i * 512:(i + 1) * 512] for i in range(8)]
        self.xf = [[b.sbuf(f"xf{s}_{c}", (128, TB), F32) for c in range(NCH)]
                   for s in range(2)]
        self.xb = [b.sbuf(f"xbc{c}", (128, TB), BF16) for c in range(NCH)]
        self.x2 = [b.sbuf(f"x2_{i}", (128, TB), BF16) for i in range(2)]
        self.tmp = [b.sbuf(f"tmp{c}", (128, TB), F32) for c in range(ntmp)]
        self.rows = b.sbuf("rows", (1, 4 * TB), F32)
        self.dmy = b.sbuf("dmy", (128, 64), BF16)
        if with_prep_b:
            self.prep_f = [b.sbuf(f"prf{i}", (128, 512), F32)
                           for i in range(2)]
            self.prep_b = [b.sbuf(f"prb{i}", (128, 512), BF16)
                           for i in range(2)]
        self.consts = {}

    def alloc_xnq(self, nsets=2):
        b = self.b
        self.ffn_xnq = [[b.sbuf(f"fxn{s}_{c}", (128, TB), BF16) for c in range(NCH)]
                        for s in range(nsets)]

    def alloc_ffn(self):
        b = self.b
        self.alloc_xnq()
        self.ffn_hA = [b.sbuf(f"fhA{m}", (128, TB), BF16) for m in range(32)]
        self.ffn_hB = [b.sbuf(f"fhB{m}", (128, TB), BF16) for m in range(32)]
        self.ffn_wbuf = [b.sbuf(f"fw{i}", (128, 1024), BF16) for i in range(8)]
        self.ffn_hwbuf = [b.sbuf(f"fhw{i}", (128, 4096), BF16) for i in range(4)]
        self.ffn_t12 = [b.sbuf(f"ft12_{i}", (128, TB), F32) for i in range(2)]
        self.ffn_t1b = [b.sbuf(f"ft1b_{i}", (128, TB), BF16) for i in range(2)]

    def load_consts(self, spec):
        b = self.b
        for name, (src, shape, dtype) in spec.items():
            t = b.sbuf("c_" + name, shape, dtype)
            self.consts[name] = t
            b.emit("sync", lambda E, t=t, src=src:
                   E.dma_start(out=t[:], in_=src), inc="ld", n=16)

    def init_const_tiles(self, idn_ap):
        b = self.b
        b.emit("sync", lambda E, s=idn_ap, t=self.identf:
               E.dma_start(out=t[:], in_=s[:, :]), inc="ld", n=16)
        b.wait("vector", "ld", b.val("ld"))
        b.emit("vector", lambda E, t=self.ident, s=self.identf:
               E.tensor_copy(t[:], s[:]))
        b.emit("vector", lambda E, t=self.onesb: E.memset(t[:], 1.0))
        b.emit("vector", lambda E, t=self.onesrow: E.memset(t[:], 1.0), inc="v")
        b.drain("vector")


def dummy_loads(ctx, src2d, n=2):
    b = ctx.b
    for _ in range(n):
        b.emit("sync", lambda E, d=ctx.dmy, s=src2d:
               E.dma_start(out=d[:, 0:64], in_=s[0:128, 0:64]), inc="ld", n=16)
    return b.val("ld")


def use_bank(ctx, i):
    ctx.g.wait("tensor", "bank", i)


def free_bank(ctx, i, sem, val):
    ctx.g.set(("bank", i), sem, val)


def barrier(ctx):
    b = ctx.b
    for eng in ENGINES:
        for s in list(b.sem_cnt.keys()):
            b.wait(eng, s, b.val(s))


def sublayer_boundary(ctx):
    barrier(ctx)
    ctx.g.d.clear()
    ctx.b.grace()



def ln_phase(ctx, x_dram, xn_dram, u_dram=None, aff=None):
    b = ctx.b
    g = ctx.g
    EPS1 = 1e-5
    st_after = {}
    for j in range(NB):
        buf = ctx.xf[j % 2]
        for c in range(NCH):
            g.wait("sync", "xf", j % 2, c)
            b.emit("sync", lambda E, d=buf[c], s_=x_dram, c=c, j=j:
                   E.dma_start(out=d[:], in_=s_[c * 128:(c + 1) * 128,
                                               j * TB:(j + 1) * TB]),
                   inc="ld", n=16)
        ld_done = b.val("ld")
        b.wait("vector", "ld", ld_done)
        cast_done = []
        for c in range(NCH):
            g.wait("vector", "xb", c)
            cast_done.append(b.emit("vector", lambda E, d=ctx.xb[c], s_=buf[c]:
                                    E.tensor_copy(d[:], s_[:]), inc="v"))
        use_bank(ctx, 0)
        use_bank(ctx, 1)
        for c in range(NCH):
            b.wait("scalar", "v", cast_done[c])
            g.wait("scalar", "x2", c % 2)
            sq = b.emit("scalar", lambda E, d=ctx.x2[c % 2], s_=ctx.xb[c]:
                        E.activation(d[:], s_[:], AF.Square), inc="s")
            b.wait("tensor", "v", cast_done[c])
            b.emit("tensor", lambda E, p=ctx.P[0], s_=ctx.xb[c], f=(c == 0),
                   l=(c == NCH - 1), o=ctx.onesb:
                   E.matmul(p[0:1, :], o[:], s_[:], start=f, stop=l))
            b.wait("tensor", "s", sq)
            tv = b.emit("tensor", lambda E, p=ctx.P[1], s_=ctx.x2[c % 2], f=(c == 0),
                        l=(c == NCH - 1), o=ctx.onesb:
                        E.matmul(p[0:1, :], o[:], s_[:], start=f, stop=l),
                        inc="t")
            g.set(("x2", c % 2), "t", tv)
        sumx_t = b.val("t")
        sumsq_t = b.val("t")
        R = ctx.rows
        mu = R[0:1, 0:TB]
        s2r = R[0:1, TB:2 * TB]
        rst = R[0:1, 2 * TB:3 * TB]
        w1 = R[0:1, 3 * TB:4 * TB]
        b.wait("vector", "t", sumsq_t)
        b.emit("vector", lambda E, mu=mu, p=ctx.P[0]:
               E.tensor_scalar_mul(mu[:, :], p[0:1, :], 1.0 / D))
        b.drain("vector")
        b.emit("vector", lambda E, s2r=s2r, mu=mu:
               E.tensor_tensor(s2r[:, :], mu[:, :], mu[:, :], op=ALU.mult))
        b.drain("vector")
        b.emit("vector", lambda E, w1=w1, p=ctx.P[1], s2r=s2r:
               E.scalar_tensor_tensor(w1[:, :], p[0:1, :], 1.0 / D, s2r[:, :],
                                      op0=ALU.mult, op1=ALU.subtract))
        b.drain("vector")
        vrow = b.emit("vector", lambda E, rst=rst, w1=w1:
                      E.tensor_scalar_add(rst[:, :], w1[:, :], EPS1), inc="v")
        free_bank(ctx, 0, "v", vrow)
        free_bank(ctx, 1, "v", vrow)
        if aff is not None:
            b.wait("vector", "v", vrow)
            g.wait("vector", "tmp", 0)
            t0row = ctx.tmp[0][0:1, 0:TB]
            b.emit("vector", lambda E, t0=t0row, rst=rst:
                   E.reciprocal(t0[:, :], rst[:, :]))
            b.drain("vector")
            b.emit("vector", lambda E, s2r=s2r, w1=w1, t0=t0row:
                   E.tensor_tensor(s2r[:, :], w1[:, :], t0[:, :],
                                   op=ALU.mult))
            b.drain("vector")
            v2 = b.emit("vector", lambda E, s2r=s2r, e=aff["eps2"]:
                        E.tensor_scalar_add(s2r[:, :], s2r[:, :], e), inc="v")
        b.wait("scalar", "v", b.val("v"))
        b.emit("scalar", lambda E, rst=rst:
               E.activation(rst[:, :], rst[:, :], AF.Ln))
        b.drain("scalar")
        rst_s = b.emit("scalar", lambda E, rst=rst:
                       E.activation(rst[:, :], rst[:, :], AF.Exp, scale=-0.5),
                       inc="s")
        if aff is not None:
            b.emit("scalar", lambda E, s2r=s2r:
                   E.activation(s2r[:, :], s2r[:, :], AF.Ln))
            b.drain("scalar")
            s2s = b.emit("scalar", lambda E, s2r=s2r:
                         E.activation(s2r[:, :], s2r[:, :], AF.Exp, scale=-0.5),
                         inc="s")
            b.wait("vector", "s", s2s)
            rows_v = b.emit("vector", lambda E, s2r=s2r, rst=rst:
                            E.tensor_tensor(s2r[:, :], s2r[:, :], rst[:, :],
                                            op=ALU.mult), inc="v")
            rows_sem, rows_val = "v", rows_v
        else:
            rows_sem, rows_val = "s", rst_s
        b.wait("tensor", rows_sem, rows_val)
        nbk = 3 if u_dram is not None else 2
        rowsrc = [mu, (s2r if aff is not None else rst), rst]
        for bi in range(nbk):
            use_bank(ctx, 2 + bi)
        for bi in range(nbk):
            b.emit("tensor", lambda E, p=ctx.P[2 + bi], o=ctx.onesrow,
                   r=rowsrc[bi]:
                   E.matmul(p[:, :], o[:, :], r[:, :], start=True, stop=True),
                   inc=("t" if bi == nbk - 1 else None))
        bc_done = b.val("t")
        b.wait("vector", "t", bc_done)
        if aff is not None:
            b.wait("scalar", "t", bc_done)
        for c in range(NCH):
            g.wait("vector", "tmp", c % ctx.NTMP)
            b.emit("vector", lambda E, d=ctx.tmp[c % ctx.NTMP], x=buf[c], p=ctx.P[2]:
                   E.tensor_tensor(d[:], x[:], p[:, :], op=ALU.subtract))
            b.drain("vector")
            if u_dram is not None:
                b.emit("vector", lambda E, d=buf[c], tt=ctx.tmp[c % ctx.NTMP], p=ctx.P[4]:
                       E.tensor_tensor(d[:], tt[:], p[:, :], op=ALU.mult),
                       inc="v")
                b.drain("vector")
            if aff is not None:
                gam = ctx.consts[aff["gamma"]]
                bet = ctx.consts[aff["beta"]]
                b.emit("vector", lambda E, d=ctx.tmp[c % ctx.NTMP], p=ctx.P[3]:
                       E.tensor_tensor(d[:], d[:], p[:, :], op=ALU.mult))
                b.drain("vector")
                sv = b.emit("vector", lambda E, d=ctx.xb[c], tt=ctx.tmp[c % ctx.NTMP],
                            gam=gam, bet=bet, c=c:
                            E.tensor_scalar(d[:], tt[:], gam[:, c:c + 1],
                                            bet[:, c:c + 1], op0=ALU.mult,
                                            op1=ALU.add), inc="v")
                g.set(("tmp", c % ctx.NTMP), "v", sv)
            else:
                av = b.emit("vector", lambda E, d=ctx.xb[c], tt=ctx.tmp[c % ctx.NTMP],
                            p=ctx.P[3]:
                            E.tensor_tensor(d[:], tt[:], p[:, :], op=ALU.mult),
                            inc="v")
                g.set(("tmp", c % ctx.NTMP), "v", av)
        apply_v, apply_s = b.val("v"), b.val("s")
        free_bank(ctx, 2, "v", apply_v)
        if aff is not None:
            free_bank(ctx, 3, "v", apply_v)
        else:
            free_bank(ctx, 3, "v", apply_v)
        if u_dram is not None:
            free_bank(ctx, 4, "v", apply_v)
        b.wait("gpsimd", "v", apply_v)
        if aff is not None:
            b.wait("gpsimd", "s", apply_s)
        for c in range(NCH):
            b.emit("gpsimd", lambda E, s_=ctx.xb[c], d=xn_dram, c=c, j=j:
                   E.dma_start(out=d[c * 128:(c + 1) * 128, j * TB:(j + 1) * TB],
                               in_=s_[:]), inc="st", n=16)
            if u_dram is not None:
                b.emit("gpsimd", lambda E, s_=buf[c], d=u_dram, c=c, j=j:
                       E.dma_start(out=d[c * 128:(c + 1) * 128,
                                        j * TB:(j + 1) * TB], in_=s_[:]),
                       inc="st", n=16)
        st_after[j] = b.val("st")
        for c in range(NCH):
            g.set(("xb", c), "st", st_after[j])
            g.set(("xf", j % 2, c), "st", st_after[j])
    return st_after



def prep_cast(ctx, src2d, dst2d, nrows, ncols, scale=None):
    b = ctx.b
    g = ctx.g
    nt = nrows // 128
    ns = (ncols + 511) // 512
    for i in range(nt):
        for cs in range(ns):
            c0 = cs * 512
            n = min(512, ncols - c0)
            p = (i * ns + cs) % 2
            g.wait("gpsimd", "prep", p)
            b.emit("gpsimd", lambda E, d=ctx.prep_f[p], s=src2d, i=i, c0=c0, n=n:
                   E.dma_start(out=d[:, 0:n],
                               in_=s[i * 128:(i + 1) * 128, c0:c0 + n]),
                   inc="pld", n=16)
            b.wait("gpsimd", "pld", b.val("pld"))
            cv = b.emit("gpsimd", lambda E, d=ctx.prep_b[p], s=ctx.prep_f[p], n=n:
                        E.tensor_copy(d[:, 0:n], s[:, 0:n]), inc="pv")
            b.wait("gpsimd", "pv", cv)
            b.emit("gpsimd", lambda E, s=ctx.prep_b[p], d=dst2d, i=i, c0=c0, n=n:
                   E.dma_start(out=d[i * 128:(i + 1) * 128, c0:c0 + n],
                               in_=s[:, 0:n]),
                   inc="pst", n=16)
            g.set(("prep", p), "pst", b.val("pst"))
    return b.val("pst")



def _hid_mm_phase(ctx, Wdram, hin, hout, sc, in_ready, nkin, act=AF.Gelu):
    b = ctx.b
    g = ctx.g
    hwbuf = ctx.ffn_hwbuf
    wlds = {}
    def _ld(m):
        p = m % 4
        g.wait("sync", "hw", p)
        b.emit("sync", lambda E, d=hwbuf[p], s=Wdram, m=m, n=nkin * 128:
               E.dma_start(out=d[:, 0:n], in_=s[m * 128:(m + 1) * 128, :]),
               inc="ld", n=16)
        wlds[m] = b.val("ld")
    for m in range(3):
        _ld(m)
    for m in range(32):
        p = m % 4
        if m + 3 < 32:
            _ld(m + 3)
        elif m + 3 == 32:
            wlds[32] = dummy_loads(ctx, Wdram, 1)
            wlds[33] = dummy_loads(ctx, Wdram, 1)
        b.wait("tensor", "ld", wlds[min(m + 2, 33)])
        if m == 0:
            b.grace()
            b.wait("tensor", in_ready[0], in_ready[1])
        bank = 4 + m % 2
        use_bank(ctx, bank)
        for k in range(nkin):
            b.emit("tensor", lambda E, P=ctx.P[bank], w=hwbuf[p], x=hin[k], k=k:
                   E.matmul(P[:, :], w[:, k * 128:(k + 1) * 128], x[:],
                            start=(k == 0), stop=(k == nkin - 1)),
                   inc=("t" if k == nkin - 1 else None))
        mmv = b.val("t")
        g.set(("hw", p), "t", mmv)
        if m == 0:
            g.wait("scalar", "hfree", id(hout))
            g.wait("scalar", "hfreeS", id(hout))
        b.wait("scalar", "t", mmv)
        gv = b.emit("scalar", lambda E, d=hout[m], P=ctx.P[bank], sc=sc, m=m:
                    E.activation(d[:], P[:, :], act, scale=sc[:, m:m + 1]),
                    inc="s")
        free_bank(ctx, bank, "s", gv)
    g.set(("hfree", id(hin)), "t", b.val("t"))
    return ("s", b.val("s"))


def l1ffn(ctx, x_dram, xn_dram, xout_dram, W, ln_st):
    b = ctx.b
    g = ctx.g
    s1 = ctx.consts["l6_s1"]; s2 = ctx.consts["l6_s2"]
    s1h = ctx.consts["l6_s1h"]; s2h = ctx.consts["l6_s2h"]
    s3 = ctx.consts["l6_s3"]
    wbuf, hwbuf = ctx.ffn_wbuf, ctx.ffn_hwbuf
    t12, t1b = ctx.ffn_t12, ctx.ffn_t1b
    hA, hB = ctx.ffn_hA, ctx.ffn_hB
    for j in range(NB):
        xnq = ctx.ffn_xnq[j % len(ctx.ffn_xnq)]
        b.wait("sync", "st", ln_st[j])
        for c in range(NCH):
            g.wait("sync", "xnq", j % len(ctx.ffn_xnq), c)
            b.emit("sync", lambda E, d=xnq[c], s=xn_dram, c=c, j=j:
                   E.dma_start(out=d[:], in_=s[c * 128:(c + 1) * 128,
                                              j * TB:(j + 1) * TB]),
                   inc="ld", n=16)
        wlds = {}
        def _ld_w12(m):
            p = m % 4
            g.wait("sync", "wb", p)
            b.emit("sync", lambda E, d=wbuf[2 * p], s=W["w1"], m=m:
                   E.dma_start(out=d[:], in_=s[m * 128:(m + 1) * 128, :]),
                   inc="ld", n=16)
            b.emit("sync", lambda E, d=wbuf[2 * p + 1], s=W["w2"], m=m:
                   E.dma_start(out=d[:], in_=s[m * 128:(m + 1) * 128, :]),
                   inc="ld", n=16)
            wlds[m] = b.val("ld")
        for m in range(3):
            _ld_w12(m)
        first_write = True
        for m in range(32):
            p = m % 4
            if m + 3 < 32:
                _ld_w12(m + 3)
            elif m + 3 == 32:
                wlds[32] = dummy_loads(ctx, W["w1"], 1)
                wlds[33] = dummy_loads(ctx, W["w1"], 1)
            b.wait("tensor", "ld", wlds[min(m + 2, 33)])
            if m == 0:
                b.grace()
            bank1, bank2 = m % 2, 2 + m % 2
            use_bank(ctx, bank1)
            for k in range(NCH):
                b.emit("tensor", lambda E, P=ctx.P[bank1], w=wbuf[2 * p],
                       x=xnq[k], k=k:
                       E.matmul(P[:, :], w[:, k * 128:(k + 1) * 128], x[:],
                                start=(k == 0), stop=(k == NCH - 1)),
                       inc=("t" if k == NCH - 1 else None))
            mm1 = b.val("t")
            use_bank(ctx, bank2)
            for k in range(NCH):
                b.emit("tensor", lambda E, P=ctx.P[bank2], w=wbuf[2 * p + 1],
                       x=xnq[k], k=k:
                       E.matmul(P[:, :], w[:, k * 128:(k + 1) * 128], x[:],
                                start=(k == 0), stop=(k == NCH - 1)),
                       inc=("t" if k == NCH - 1 else None))
            mm2 = b.val("t")
            g.set(("wb", p), "t", mm2)
            b.wait("scalar", "t", mm1)
            g.wait("scalar", "t1b", m % 2)
            if first_write:
                g.wait("scalar", "hfree", id(hA))
                g.wait("scalar", "hfreeS", id(hA))
            sv = b.emit("scalar", lambda E, d=t1b[m % 2], P=ctx.P[bank1],
                        s1=s1, m=m:
                        E.activation(d[:], P[:, :], AF.Silu,
                                     scale=s1[:, m:m + 1]), inc="s")
            free_bank(ctx, bank1, "s", sv)
            b.wait("vector", "t", mm2)
            if first_write:
                g.wait("vector", "hfree", id(hA))
                g.wait("vector", "hfreeS", id(hA))
                first_write = False
            b.emit("vector", lambda E, d=t12[m % 2], P=ctx.P[bank2], s2=s2, m=m:
                   E.tensor_scalar_mul(d[:], P[:, :], s2[:, m:m + 1]))
            b.drain("vector")
            b.wait("vector", "s", sv)
            hv = b.emit("vector", lambda E, d=hA[m], a=t1b[m % 2], c_=t12[m % 2]:
                        E.tensor_tensor(d[:], a[:], c_[:], op=ALU.mult),
                        inc="v")
            free_bank(ctx, bank2, "v", hv)
            g.set(("t1b", m % 2), "v", hv)
        for c in range(NCH):
            g.set(("xnq", j % len(ctx.ffn_xnq), c), "t", b.val("t"))
        r1 = _hid_mm_phase(ctx, W["h1"], hA, hB, s1h, ("v", b.val("v")), 32)
        r2 = _hid_mm_phase(ctx, W["h2"], hB, hA, s2h, r1, 32)
        xbuf = ctx.xf[j % 2]
        for c in range(NCH):
            g.wait("sync", "xf", j % 2, c)
            b.emit("sync", lambda E, d=xbuf[c], s=x_dram, c=c, j=j:
                   E.dma_start(out=d[:], in_=s[c * 128:(c + 1) * 128,
                                              j * TB:(j + 1) * TB]),
                   inc="ld", n=16)
        x_ld = b.val("ld")
        wlds3 = {}
        def _ld_w3(m):
            p = m % 4
            g.wait("sync", "hw", p)
            b.emit("sync", lambda E, d=hwbuf[p], s=W["w3"], m=m:
                   E.dma_start(out=d[:], in_=s[m * 128:(m + 1) * 128, :]),
                   inc="ld", n=16)
            wlds3[m] = b.val("ld")
        for m in range(3):
            _ld_w3(m)
        for m in range(NCH):
            p = m % 4
            if m + 3 < NCH:
                _ld_w3(m + 3)
            elif m + 3 == NCH:
                wlds3[NCH] = dummy_loads(ctx, W["w3"], 1)
                wlds3[NCH + 1] = dummy_loads(ctx, W["w3"], 1)
            b.wait("tensor", "ld", wlds3[min(m + 2, NCH + 1)])
            if m == 0:
                b.grace()
                b.wait("tensor", r2[0], r2[1])
            bank = 4 + m % 2
            use_bank(ctx, bank)
            for k in range(32):
                b.emit("tensor", lambda E, P=ctx.P[bank], w=hwbuf[p],
                       x=hA[k], k=k:
                       E.matmul(P[:, :], w[:, k * 128:(k + 1) * 128], x[:],
                                start=(k == 0), stop=(k == 31)),
                       inc=("t" if k == 31 else None))
            mmv = b.val("t")
            g.set(("hw", p), "t", mmv)
            b.wait("scalar", "t", mmv)
            g.wait("scalar", "tmp", m % ctx.NTMP)
            ov = b.emit("scalar", lambda E, d=ctx.tmp[m % ctx.NTMP], P=ctx.P[bank],
                        s3=s3, m=m:
                        E.activation(d[:], P[:, :], AF.Copy,
                                     scale=s3[:, m:m + 1]), inc="s")
            free_bank(ctx, bank, "s", ov)
            b.wait("vector", "s", ov)
            b.wait("vector", "ld", x_ld)
            rv = b.emit("vector", lambda E, d=xbuf[m], t=ctx.tmp[m % ctx.NTMP]:
                        E.tensor_tensor(d[:], d[:], t[:], op=ALU.add), inc="v")
            g.set(("tmp", m % ctx.NTMP), "v", rv)
            b.wait("gpsimd", "v", rv)
            stv = b.emit("gpsimd", lambda E, s=xbuf[m], d=xout_dram, m=m, j=j:
                         E.dma_start(out=d[m * 128:(m + 1) * 128,
                                          j * TB:(j + 1) * TB], in_=s[:]),
                         inc="st", n=16)
            g.set(("xf", j % 2, m), "st", stv)
        g.set(("hfree", id(hA)), "t", b.val("t"))
    return b.val("st")


def prep_dequant(ctx, q2d, scale2d, dst2d, nrows, ncols):
    b = ctx.b
    g = ctx.g
    nt = nrows // 128
    ns = (ncols + 511) // 512
    for i in range(nt):
        for cs in range(ns):
            c0 = cs * 512
            n = min(512, ncols - c0)
            p = (i * ns + cs) % 2
            g.wait("gpsimd", "prep", p)
            b.emit("gpsimd", lambda E, d=ctx.prep_q[p], s=q2d, i=i, c0=c0, n=n:
                   E.dma_start(out=d[:, 0:n],
                               in_=s[i * 128:(i + 1) * 128, c0:c0 + n]),
                   inc="pld", n=16)
            b.emit("gpsimd", lambda E, d=ctx.prep_f[p], s=scale2d, i=i, c0=c0, n=n:
                   E.dma_start(out=d[:, 0:n],
                               in_=s[i * 128:(i + 1) * 128, c0:c0 + n]),
                   inc="pld", n=16)
            b.wait("gpsimd", "pld", b.val("pld"))
            b.emit("gpsimd", lambda E, d=ctx.prep_f2[p], s=ctx.prep_q[p], n=n:
                   E.tensor_copy(d[:, 0:n], s[:, 0:n]))
            b.ops["gpsimd"].append(lambda E: E.drain())
            b.emit("gpsimd", lambda E, d=ctx.prep_f2[p], n=n:
                   E.tensor_scalar_sub(d[:, 0:n], d[:, 0:n], 128.0))
            b.ops["gpsimd"].append(lambda E: E.drain())
            cv = b.emit("gpsimd", lambda E, d=ctx.prep_b[p], t=ctx.prep_f2[p],
                        sc=ctx.prep_f[p], n=n:
                        E.tensor_mul(d[:, 0:n], t[:, 0:n], sc[:, 0:n]),
                        inc="pv")
            b.wait("gpsimd", "pv", cv)
            b.emit("gpsimd", lambda E, s=ctx.prep_b[p], d=dst2d, i=i, c0=c0, n=n:
                   E.dma_start(out=d[i * 128:(i + 1) * 128, c0:c0 + n],
                               in_=s[:, 0:n]),
                   inc="pst", n=16)
            g.set(("prep", p), "pst", b.val("pst"))
    return b.val("pst")



DEBUG_SERIALIZE = 0
DEBUG_PE_DELAY = 0

def l0ffn(ctx, x_dram, xn_dram, u_dram, xout_dram, W, ln_st):
    b = ctx.b
    g = ctx.g
    b1 = ctx.consts["l2_b1"]
    b2h = ctx.consts["l2_b2h"]
    wbuf, hwbuf = ctx.ffn_wbuf, ctx.ffn_hwbuf
    for j in range(NB):
        hA = ctx.ffn_hA if j % 2 == 0 else ctx.ffn_hB
        xnq = ctx.ffn_xnq[j % len(ctx.ffn_xnq)]
        b.wait("sync", "st", ln_st[j])
        for c in range(NCH):
            g.wait("sync", "xnq", j % len(ctx.ffn_xnq), c)
            b.emit("sync", lambda E, d=xnq[c], s=xn_dram, c=c, j=j:
                   E.dma_start(out=d[:], in_=s[c * 128:(c + 1) * 128,
                                              j * TB:(j + 1) * TB]),
                   inc="ld", n=16)
        xn_ld = b.val("ld")
        b.wait("tensor", "ld", xn_ld)
        b.grace()
        first_write = True
        wlds = {}
        def _ld_fc1(m):
            p = m % 4
            g.wait("sync", "wb", p)
            b.emit("sync", lambda E, d=wbuf[p], s=W["fc1"], m=m:
                   E.dma_start(out=d[:], in_=s[m * 128:(m + 1) * 128, :]),
                   inc="ld", n=16)
            wlds[m] = b.val("ld")
        for m in range(4):
            _ld_fc1(m)
        for m in range(32):
            p = m % 4
            b.wait("tensor", "ld", wlds.get(m + 2, 0) or wlds[m])
            bank = m % 2
            use_bank(ctx, bank)
            for k in range(NCH):
                b.emit("tensor", lambda E, P=ctx.P[bank], w=wbuf[p],
                       x=xnq[k], k=k:
                       E.matmul(P[:, :], w[:, k * 128:(k + 1) * 128], x[:],
                                start=(k == 0), stop=(k == NCH - 1)),
                       inc=("t" if k == NCH - 1 else None))
            mmv = b.val("t")
            g.set(("wb", p), "t", mmv)
            if m + 4 < 32:
                _ld_fc1(m + 4)
            elif m == 28:
                wlds[32] = dummy_loads(ctx, W["fc1"], 1)
                wlds[33] = dummy_loads(ctx, W["fc1"], 1)
            b.wait("scalar", "t", mmv)
            if first_write:
                g.wait("scalar", "hfree", id(hA))
                g.wait("scalar", "hfreeS", id(hA))
                first_write = False
            sv = b.emit("scalar", lambda E, d=hA[m], P=ctx.P[bank], b1=b1, m=m:
                        E.activation(d[:], P[:, :], AF.Silu,
                                     bias=b1[:, m:m + 1]), inc="s")
            free_bank(ctx, bank, "s", sv)
        for c in range(NCH):
            g.set(("xnq", j % len(ctx.ffn_xnq), c), "t", b.val("t"))
        h_ready = ("s", b.val("s"))
        if getattr(ctx, "dbg_h", None) is not None:
            b.wait("gpsimd", "s", h_ready[1])
            for m in range(32):
                b.emit("gpsimd", lambda E, s=hA[m], d=ctx.dbg_h, m=m, j=j:
                       E.dma_start(out=d[(j * 32 + m) * 128:(j * 32 + m + 1) * 128, :],
                                   in_=s[:]), inc="st", n=16)
            g.set(("hfreeS", id(hA)), "st", b.val("st"))
        xbuf = ctx.xf[j % 2]
        ubuf = ctx.xf[(j % 2) ^ 1]
        for c in range(NCH):
            g.wait("sync", "xf", j % 2, c)
            b.emit("sync", lambda E, d=xbuf[c], s=x_dram, c=c, j=j:
                   E.dma_start(out=d[:], in_=s[c * 128:(c + 1) * 128,
                                              j * TB:(j + 1) * TB]),
                   inc="ld", n=16)
            g.wait("sync", "xf", (j % 2) ^ 1, c)
            b.emit("sync", lambda E, d=ubuf[c], s=u_dram, c=c, j=j:
                   E.dma_start(out=d[:], in_=s[c * 128:(c + 1) * 128,
                                              j * TB:(j + 1) * TB]),
                   inc="ld", n=16)
        x_ld = b.val("ld")
        wlds2 = {}
        def _ld_fc2(m):
            p = m % 4
            g.wait("sync", "hw", p)
            b.emit("sync", lambda E, d=hwbuf[p], s=W["fc2"], m=m:
                   E.dma_start(out=d[:], in_=s[m * 128:(m + 1) * 128, :]),
                   inc="ld", n=16)
            wlds2[m] = b.val("ld")
        for m in range(3):
            _ld_fc2(m)
        for m in range(NCH):
            p = m % 4
            if m + 3 < NCH:
                _ld_fc2(m + 3)
            elif m + 3 == NCH:
                wlds2[NCH] = dummy_loads(ctx, W["fc2"], 1)
                wlds2[NCH + 1] = dummy_loads(ctx, W["fc2"], 1)
            b.wait("tensor", "ld", wlds2[min(m + 2, NCH + 1)])
            if m == 0:
                b.grace()
            if m == 0:
                b.wait("tensor", h_ready[0], h_ready[1])
            bank = 4 + m % 2
            use_bank(ctx, bank)
            for k in range(32):
                b.emit("tensor", lambda E, P=ctx.P[bank], w=hwbuf[p],
                       x=hA[k], k=k:
                       E.matmul(P[:, :], w[:, k * 128:(k + 1) * 128], x[:],
                                start=(k == 0), stop=(k == 31)),
                       inc=("t" if k == 31 else None))
            mmv = b.val("t")
            g.set(("hw", p), "t", mmv)
            b.wait("vector", "t", mmv)
            g.wait("vector", "tmp", m % ctx.NTMP)
            ov = b.emit("vector", lambda E, d=ctx.tmp[m % ctx.NTMP], P=ctx.P[bank],
                        bb=b2h, m=m:
                        E.tensor_scalar(d[:], P[:, :], 0.5, bb[:, m:m + 1],
                                        op0=ALU.mult, op1=ALU.add), inc="v")
            free_bank(ctx, bank, "v", ov)
            b.drain("vector")
            b.wait("vector", "ld", x_ld)
            b.emit("vector", lambda E, d=xbuf[m], u=ubuf[m]:
                   E.tensor_tensor(d[:], d[:], u[:], op=ALU.add))
            b.drain("vector")
            rv = b.emit("vector", lambda E, d=xbuf[m], t=ctx.tmp[m % ctx.NTMP]:
                        E.tensor_tensor(d[:], d[:], t[:], op=ALU.add), inc="v")
            g.set(("tmp", m % ctx.NTMP), "v", rv)
            b.wait("gpsimd", "v", rv)
            stv = b.emit("gpsimd", lambda E, s=xbuf[m], d=xout_dram, m=m, j=j:
                         E.dma_start(out=d[m * 128:(m + 1) * 128,
                                          j * TB:(j + 1) * TB], in_=s[:]),
                         inc="st", n=16)
            g.set(("xf", j % 2, m), "st", stv)
            g.set(("xf", (j % 2) ^ 1, m), "st", stv)
        g.set(("hfree", id(hA)), "t", b.val("t"))
    return b.val("st")


def alloc_attn(ctx, tables=False, need_ctx=True, n_aw=6):
    b = ctx.b
    ctx.at_qF = [b.sbuf(f"aqF{c}", (128, SEQ), BF16) for c in range(NCH)]
    ctx.at_kF = [b.sbuf(f"akF{c}", (128, SEQ), BF16) for c in range(NCH)]
    ctx.at_vT = [b.sbuf(f"avT{i}", (128, D), BF16) for i in range(SEQ // 128)]
    ctx.at_w = [b.sbuf(f"aw{i}", (128, 1024), BF16) for i in range(n_aw)]
    ctx.at_vw = [b.sbuf(f"avw{i}", (128, 1024), BF16) for i in range(4)]
    ctx.at_vf = ctx.prep_f[0]
    ctx.at_pkv = None
    ctx.at_rb = None
    ctx.at_q2 = ctx.x2
    ctx.at_rows = ctx.rows
    ctx.at_mq = b.sbuf("amq", (128, 32), F32)
    ctx.at_sc = b.sbuf("asc", (128, 4), F32)
    ctx.at_negM = b.sbuf("anegM", (128, NH), F32)
    ctx.at_P = ctx.x2
    if need_ctx:
        ctx.at_ctx = [[b.sbuf(f"actx{c}", (128, TB), BF16) for c in range(NCH)]] * 2
    if tables:
        ctx.at_tab = [b.sbuf(f"atab{i}", (128, 2432), BF16) for i in range(2)]
        ctx.at_bmax = b.sbuf("abmax", (1, NH), F32)


def alloc_l1aa(ctx):
    b = ctx.b
    ctx.at_pkv = b.sbuf("apkv", (128, 1024), BF16)
    ctx.at_rb = b.sbuf("arb", (128, TB), BF16)
    ctx.at_rl = [b.sbuf(f"arl{i}", (128, TB), BF16) for i in range(2)]
    ctx.at_zero = b.sbuf("azero", (128, TB), BF16)


def _attn_proj_phase(ctx, xn_dram, W, ln_st, s, rms, bias_c):
    b = ctx.b
    g = ctx.g
    qF, kF, vT, aw, vw = ctx.at_qF, ctx.at_kF, ctx.at_vT, ctx.at_w, ctx.at_vw
    EPS = 1e-6
    for jb in range(SEQ // TB):
        j = s * (SEQ // TB) + jb
        nset = len(ctx.ffn_xnq)
        xnq = ctx.ffn_xnq[j % nset]
        b.wait("sync", "st", ln_st[j])
        for c in range(NCH):
            g.wait("sync", "xnq", j % nset, c)
            b.emit("sync", lambda E, d=xnq[c], s_=xn_dram, c=c, j=j:
                   E.dma_start(out=d[:], in_=s_[c * 128:(c + 1) * 128,
                                               j * TB:(j + 1) * TB]),
                   inc="ld", n=16)
        lds = {}
        def _ldqk(c):
            pq, pk = c % 3, c % 3
            g.wait("sync", "aw", pq)
            b.emit("sync", lambda E, d=aw[pq], s_=W["wq"], c=c:
                   E.dma_start(out=d[:], in_=s_[c * 128:(c + 1) * 128, :]),
                   inc="ld", n=16)
            g.wait("sync", "avw", pk)
            b.emit("sync", lambda E, d=vw[pk], s_=W["wk"], c=c:
                   E.dma_start(out=d[:], in_=s_[c * 128:(c + 1) * 128, :]),
                   inc="ld", n=16)
            lds[c] = b.val("ld")
        vlds = {}
        def _ldv(c2):
            p = c2 % 4
            g.wait("sync", "avw", p)
            b.emit("sync", lambda E, d=vw[p], s_=W["wv"], c2=c2:
                   E.dma_start(out=d[:], in_=s_[c2 * 128:(c2 + 1) * 128, :]),
                   inc="ld", n=16)
            vlds[c2] = b.val("ld")
        for c2 in range(3):
            _ldv(c2)
        for c2 in range(NCH):
            if c2 + 3 < NCH:
                _ldv(c2 + 3)
            elif c2 + 3 == NCH:
                vlds[NCH] = dummy_loads(ctx, W["wv"], 1)
                vlds[NCH + 1] = dummy_loads(ctx, W["wv"], 1)
            b.wait("tensor", "ld", vlds[min(c2 + 2, NCH + 1)])
            if c2 == 0:
                b.grace()
            p = c2 % 4
            bank = c2 % 2
            use_bank(ctx, bank)
            for k in range(NCH):
                b.emit("tensor", lambda E, P=ctx.P[bank], w=vw[p], x=xnq[k], k=k:
                       E.matmul(P[:, :], w[:, k * 128:(k + 1) * 128], x[:],
                                start=(k == 0), stop=(k == NCH - 1)),
                       inc=("t" if k == NCH - 1 else None))
            vmm = b.val("t")
            g.set(("avw", p), "t", vmm)
            b.wait("vector", "t", vmm)
            g.wait("vector", "vfs", 0)
            g.wait("vector", "prep", 0)
            ev = b.emit("vector", lambda E, d=ctx.at_vf, P=ctx.P[bank]:
                        E.tensor_copy(d[:], P[:, :]), inc="v")
            free_bank(ctx, bank, "v", ev)
            b.wait("tensor", "v", ev)
            for ts in range(4):
                bank2 = 2 + ts % 2
                use_bank(ctx, bank2)
                tt = b.emit("tensor", lambda E, P=ctx.P[bank2], s_=ctx.at_vf,
                            idn=ctx.identf, ts=ts:
                            E.transpose(P[:, 0:128], s_[:, ts * 128:(ts + 1) * 128],
                                        idn[:]), inc="t")
                b.wait("vector", "t", tt)
                g.wait("vector", "avT", jb * 4 + ts)
                tv = b.emit("vector", lambda E, d=vT[jb * 4 + ts], P=ctx.P[bank2],
                            c2=c2, ts=ts:
                            E.tensor_copy(d[:, c2 * 128:(c2 + 1) * 128],
                                          P[:, 0:128]), inc="v")
                free_bank(ctx, bank2, "v", tv)
            g.set(("vfs", 0), "t", b.val("t"))
            g.set(("prep", 0), "t", b.val("t"))
        _ldqk(0)
        _ldqk(1)
        for c in range(NCH):
            if c + 2 < NCH:
                _ldqk(c + 2)
            else:
                lds[c + 2] = dummy_loads(ctx, W["wq"], 2)
            b.wait("tensor", "ld", lds[c + 2])
            if c == 0:
                b.grace()
            pq, pk = c % 3, c % 3
            bq, bk = 2, 3
            use_bank(ctx, bq)
            for k in range(NCH):
                b.emit("tensor", lambda E, P=ctx.P[bq], w=aw[pq], x=xnq[k], k=k:
                       E.matmul(P[:, :], w[:, k * 128:(k + 1) * 128], x[:],
                                start=(k == 0), stop=(k == NCH - 1)),
                       inc=("t" if k == NCH - 1 else None))
            mmq = b.val("t")
            g.set(("aw", pq), "t", mmq)
            use_bank(ctx, bk)
            for k in range(NCH):
                b.emit("tensor", lambda E, P=ctx.P[bk], w=vw[pk], x=xnq[k], k=k:
                       E.matmul(P[:, :], w[:, k * 128:(k + 1) * 128], x[:],
                                start=(k == 0), stop=(k == NCH - 1)),
                       inc=("t" if k == NCH - 1 else None))
            mmk = b.val("t")
            g.set(("avw", pk), "t", mmk)
            b.wait("vector", "t", mmk)
            g.wait("vector", "aqk", c)
            if bias_c is not None:
                bqkv = ctx.consts[bias_c]
                vq = b.emit("vector", lambda E, d=qF[c], P=ctx.P[bq], bb=bqkv,
                            c=c, jb=jb:
                            E.tensor_scalar_add(d[:, jb * TB:(jb + 1) * TB],
                                                P[:, :], bb[:, c:c + 1]),
                            inc="v")
                b.drain("vector")
                vk = b.emit("vector", lambda E, d=kF[c], P=ctx.P[bk], bb=bqkv,
                            c=c, jb=jb:
                            E.tensor_scalar_add(d[:, jb * TB:(jb + 1) * TB],
                                                P[:, :], bb[:, 8 + c:9 + c]),
                            inc="v")
            else:
                vq = b.emit("vector", lambda E, d=qF[c], P=ctx.P[bq], c=c, jb=jb:
                            E.tensor_copy(d[:, jb * TB:(jb + 1) * TB], P[:, :]),
                            inc="v")
                b.drain("vector")
                vk = b.emit("vector", lambda E, d=kF[c], P=ctx.P[bk], c=c, jb=jb:
                            E.tensor_copy(d[:, jb * TB:(jb + 1) * TB], P[:, :]),
                            inc="v")
            free_bank(ctx, bq, "v", vq)
            free_bank(ctx, bk, "v", vk)
            if rms:
                _rmsnorm_qk(ctx, c, jb, vq, vk)
            else:
                g.set(("aqk", c), "v", vk)
        for c2 in range(NCH):
            g.set(("xnq", j % nset, c2), "t", b.val("t"))
    return


def _rmsnorm_qk(ctx, c, jb, vq, vk):
    b = ctx.b
    g = ctx.g
    EPS = 1e-6
    qF, kF = ctx.at_qF, ctx.at_kF
    R = ctx.at_rows
    w2dd = ctx.consts["l3_w2dd"]
    b.wait("scalar", "v", vk)
    g.wait("scalar", "x2", 0)
    sq = b.emit("scalar", lambda E, d=ctx.at_q2[0], s_=qF[c], jb=jb:
                E.activation(d[:], s_[:, jb * TB:(jb + 1) * TB], AF.Square),
                inc="s")
    g.wait("scalar", "x2", 1)
    sk = b.emit("scalar", lambda E, d=ctx.at_q2[1], s_=kF[c], jb=jb:
                E.activation(d[:], s_[:, jb * TB:(jb + 1) * TB], AF.Square),
                inc="s")
    for bi in range(4, 8):
        use_bank(ctx, bi)
    b.wait("tensor", "s", sq)
    b.emit("tensor", lambda E, P=ctx.P[4], o=ctx.onesb, x=ctx.at_q2[0]:
           E.matmul(P[0:1, 0:TB], o[0:64, :], x[0:64, :], start=True,
                    stop=True, tile_position=(0, 0)))
    tq = b.emit("tensor", lambda E, P=ctx.P[5], o=ctx.onesb, x=ctx.at_q2[0]:
                E.matmul(P[0:1, 0:TB], o[64:128, :], x[64:128, :], start=True,
                         stop=True, tile_position=(64, 0)), inc="t")
    g.set(("x2", 0), "t", tq)
    b.wait("tensor", "s", sk)
    b.emit("tensor", lambda E, P=ctx.P[6], o=ctx.onesb, x=ctx.at_q2[1]:
           E.matmul(P[0:1, 0:TB], o[0:64, :], x[0:64, :], start=True,
                    stop=True, tile_position=(0, 0)))
    tk = b.emit("tensor", lambda E, P=ctx.P[7], o=ctx.onesb, x=ctx.at_q2[1]:
                E.matmul(P[0:1, 0:TB], o[64:128, :], x[64:128, :], start=True,
                         stop=True, tile_position=(64, 0)), inc="t")
    g.set(("x2", 1), "t", tk)
    b.wait("vector", "t", tk)
    for ri in range(4):
        b.emit("vector", lambda E, R=R, P=ctx.P[4 + ri], ri=ri:
               E.tensor_scalar(R[0:1, ri * TB:(ri + 1) * TB], P[0:1, 0:TB],
                               1.0 / 64.0, EPS, op0=ALU.mult, op1=ALU.add))
    b.drain("vector")
    rv = b.emit("vector", lambda E: E.engine_nop(), inc="v")
    b.wait("scalar", "v", rv)
    b.emit("scalar", lambda E, R=R:
           E.activation(R[0:1, 0:4 * TB], R[0:1, 0:4 * TB], AF.Ln))
    b.drain("scalar")
    rs = b.emit("scalar", lambda E, R=R:
                E.activation(R[0:1, 0:4 * TB], R[0:1, 0:4 * TB], AF.Exp,
                             scale=-0.5), inc="s")
    for bi in range(4, 8):
        free_bank(ctx, bi, "s", rs)
    b.wait("tensor", "s", rs)
    use_bank(ctx, 4)
    use_bank(ctx, 5)
    b.emit("tensor", lambda E, P=ctx.P[4], o=ctx.onesrow, R=R:
           E.matmul(P[0:64, :], o[0:1, 0:64], R[0:1, 0:TB], start=True,
                    stop=True, tile_position=(0, 0)))
    b.emit("tensor", lambda E, P=ctx.P[4], o=ctx.onesrow, R=R:
           E.matmul(P[64:128, :], o[0:1, 0:64], R[0:1, TB:2 * TB], start=True,
                    stop=True, tile_position=(0, 64)))
    b.emit("tensor", lambda E, P=ctx.P[5], o=ctx.onesrow, R=R:
           E.matmul(P[0:64, :], o[0:1, 0:64], R[0:1, 2 * TB:3 * TB], start=True,
                    stop=True, tile_position=(0, 0)))
    tb = b.emit("tensor", lambda E, P=ctx.P[5], o=ctx.onesrow, R=R:
                E.matmul(P[64:128, :], o[0:1, 0:64], R[0:1, 3 * TB:4 * TB],
                         start=True, stop=True, tile_position=(0, 64)), inc="t")
    b.wait("vector", "t", tb)
    b.emit("vector", lambda E, d=qF[c], P=ctx.P[4], jb=jb:
           E.scalar_tensor_tensor(d[:, jb * TB:(jb + 1) * TB],
                                  d[:, jb * TB:(jb + 1) * TB], 1.0, P[:, :],
                                  op0=ALU.mult, op1=ALU.mult))
    b.drain("vector")
    ak = b.emit("vector", lambda E, d=kF[c], P=ctx.P[5], w=w2dd, jb=jb:
                E.scalar_tensor_tensor(d[:, jb * TB:(jb + 1) * TB],
                                       d[:, jb * TB:(jb + 1) * TB], w[:, 0:1],
                                       P[:, :], op0=ALU.mult, op1=ALU.mult),
                inc="v")
    b.drain("vector")
    free_bank(ctx, 4, "v", ak)
    free_bank(ctx, 5, "v", ak)
    g.set(("aqk", c), "v", ak)

def _attn_passA(ctx, s, tables=False):
    b = ctx.b
    g = ctx.g
    qF, kF = ctx.at_qF, ctx.at_kF
    mq, sc, R = ctx.at_mq, ctx.at_sc, ctx.at_rows
    b.wait("tensor", "v", b.val("v"))
    for c in range(NCH):
        hA, hB = 2 * c, 2 * c + 1
        for qt in range(SEQ // 128):
            for bi in range(8):
                use_bank(ctx, bi)
            for kb in range(4):
                b.emit("tensor", lambda E, P=ctx.P[kb], q=qF[c], k=kF[c],
                       qt=qt, kb=kb:
                       E.matmul(P[:, :], q[0:64, qt * 128:(qt + 1) * 128],
                                k[0:64, kb * TB:(kb + 1) * TB],
                                start=True, stop=True, tile_position=(0, 0)))
                last = (kb == 3)
                b.emit("tensor", lambda E, P=ctx.P[4 + kb], q=qF[c], k=kF[c],
                       qt=qt, kb=kb:
                       E.matmul(P[:, :], q[64:128, qt * 128:(qt + 1) * 128],
                                k[64:128, kb * TB:(kb + 1) * TB],
                                start=True, stop=True, tile_position=(64, 0)),
                       inc=("t" if last else None))
            tmm = b.val("t")
            b.wait("vector", "t", tmm)
            b.emit("vector", lambda E, m=mq, PA=ctx.PSALL, qt=qt:
                   E.reduce_max(m[:, qt:qt + 1], PA[:, 0:SEQ], axis=AX.X))
            b.drain("vector")
            rm = b.emit("vector", lambda E, m=mq, PA=ctx.PSALL, qt=qt:
                        E.reduce_max(m[:, 16 + qt:17 + qt], PA[:, SEQ:2 * SEQ],
                                     axis=AX.X), inc="v")
            b.drain("vector")
            for bi in range(8):
                free_bank(ctx, bi, "v", rm)
        for hi, h in enumerate((hA, hB)):
            b.emit("vector", lambda E, sc=sc, m=mq, hi=hi:
                   E.reduce_max(sc[:, hi:hi + 1], m[:, hi * 16:(hi + 1) * 16],
                                axis=AX.X))
        b.drain("vector")
        fm = b.emit("vector", lambda E: E.engine_nop(), inc="v")
        for hi, h in enumerate((hA, hB)):
            b.wait("tensor", "v", fm if hi == 0 else b.val("v"))
            use_bank(ctx, 6)
            tb = b.emit("tensor", lambda E, P=ctx.P[6], sc=sc, idn=ctx.identf,
                        hi=hi:
                        E.transpose(P[0:1, 0:128], sc[:, hi:hi + 1], idn[:]),
                        inc="t")
            b.wait("vector", "t", tb)
            rv = b.emit("vector", lambda E, R=R, P=ctx.P[6], h=h:
                        E.reduce_max(R[0:1, h:h + 1], P[0:1, 0:128], axis=AX.X),
                        inc="v")
            b.drain("vector")
            free_bank(ctx, 6, "v", rv)
    return


def _attn_finalizeM(ctx, tables=False):
    b = ctx.b
    g = ctx.g
    R = ctx.at_rows
    if tables:
        b.emit("vector", lambda E, R=R, bm=ctx.at_bmax:
               E.tensor_tensor(R[0:1, 0:NH], R[0:1, 0:NH], bm[0:1, 0:NH],
                               op=ALU.add))
        b.drain("vector")
    nm = b.emit("vector", lambda E, R=R:
                E.tensor_scalar_mul(R[0:1, NH:2 * NH], R[0:1, 0:NH], -1.0),
                inc="v")
    b.wait("tensor", "v", nm)
    use_bank(ctx, 6)
    tb = b.emit("tensor", lambda E, P=ctx.P[6], o=ctx.onesrow, R=R:
                E.matmul(P[:, 0:NH], o[0:1, 0:128], R[0:1, NH:2 * NH],
                         start=True, stop=True), inc="t")
    b.wait("vector", "t", tb)
    ngm = b.emit("vector", lambda E, d=ctx.at_negM, P=ctx.P[6]:
                 E.tensor_copy(d[:, :], P[:, 0:NH]), inc="v")
    free_bank(ctx, 6, "v", ngm)
    return ngm

def _attn_passB_out(ctx, s, x_dram, xout_dram, W, bias_c, bproj_c,
                    tables=None, u_dram=None):
    b = ctx.b
    g = ctx.g
    qF, kF, vT, aw = ctx.at_qF, ctx.at_kF, ctx.at_vT, ctx.at_w
    negM = ctx.at_negM
    R = ctx.at_rows
    for qb in range(SEQ // TB):
        j = s * (SEQ // TB) + qb
        cbuf = ctx.at_ctx[qb % 2]
        xbuf = ctx.xf[j % 2]
        for c in range(NCH):
            g.wait("sync", "xf", j % 2, c)
            b.emit("sync", lambda E, d=xbuf[c], s_=x_dram, c=c, j=j:
                   E.dma_start(out=d[:], in_=s_[c * 128:(c + 1) * 128,
                                               j * TB:(j + 1) * TB]),
                   inc="ld", n=16)
            if u_dram is not None:
                g.wait("sync", "xf", (j % 2) ^ 1, c)
                b.emit("sync", lambda E, d=ctx.xf[(j % 2) ^ 1][c], s_=u_dram,
                       c=c, j=j:
                       E.dma_start(out=d[:], in_=s_[c * 128:(c + 1) * 128,
                                                   j * TB:(j + 1) * TB]),
                       inc="ld", n=16)
        x_ld = b.val("ld")
        for c in range(NCH):
            hA, hB = 2 * c, 2 * c + 1
            if tables is not None:
                for hi, h in enumerate((hA, hB)):
                    g.wait("sync", "atab", hi)
                    b.emit("sync", lambda E, d=ctx.at_tab[hi], s_=tables["a2"],
                           h=h, qb=qb:
                           E.dma_start(out=d[:],
                                       in_=s_[h * 128:(h + 1) * 128,
                                              qb * TB:qb * TB + 2432]),
                           inc="ld", n=16)
                tab_ld = dummy_loads(ctx, tables["a2"], 2)
                b.wait("tensor", "ld", tab_ld)
                b.grace()
            use_bank(ctx, 4)
            use_bank(ctx, 5)
            use_bank(ctx, 6)
            ngm_v = ctx._negM_ready
            b.wait("scalar", "v", ngm_v)
            for kt in range(SEQ // 128):
                bk = kt % 2
                bk2 = 2 + kt % 2
                use_bank(ctx, bk)
                mm1 = b.emit("tensor", lambda E, P=ctx.P[bk], k=kF[c], q=qF[c],
                             kt=kt, qb=qb:
                             E.matmul(P[:, :], k[0:64, kt * 128:(kt + 1) * 128],
                                      q[0:64, qb * TB:(qb + 1) * TB],
                                      start=True, stop=(tables is None),
                                      tile_position=(0, 0)),
                             inc=("t" if tables is None else None))
                if tables is not None:
                    n0 = 1920 - 128 * kt
                    mm1 = b.emit("tensor", lambda E, P=ctx.P[bk], idn=ctx.ident,
                                 tb_=ctx.at_tab[0], n0=n0:
                                 E.matmul(P[:, :], idn[:], tb_[:, n0:n0 + TB],
                                          start=False, stop=True), inc="t")
                use_bank(ctx, bk2)
                mm2 = b.emit("tensor", lambda E, P=ctx.P[bk2], k=kF[c], q=qF[c],
                             kt=kt, qb=qb:
                             E.matmul(P[:, :], k[64:128, kt * 128:(kt + 1) * 128],
                                      q[64:128, qb * TB:(qb + 1) * TB],
                                      start=True, stop=(tables is None),
                                      tile_position=(64, 0)),
                             inc=("t" if tables is None else None))
                if tables is not None:
                    n0 = 1920 - 128 * kt
                    mm2 = b.emit("tensor", lambda E, P=ctx.P[bk2], idn=ctx.ident,
                                 tb_=ctx.at_tab[1], n0=n0:
                                 E.matmul(P[:, :], idn[:], tb_[:, n0:n0 + TB],
                                          start=False, stop=True), inc="t")
                b.wait("scalar", "t", mm2)
                g.wait("scalar", "aPt", 0)
                e1 = b.emit("scalar", lambda E, d=ctx.at_P[0], P=ctx.P[bk],
                            nm=negM, h=hA:
                            E.activation(d[:], P[:, :], AF.Exp,
                                         bias=nm[:, h:h + 1]), inc="s")
                free_bank(ctx, bk, "s", e1)
                g.wait("scalar", "aPt", 1)
                e2 = b.emit("scalar", lambda E, d=ctx.at_P[1], P=ctx.P[bk2],
                            nm=negM, h=hB:
                            E.activation(d[:], P[:, :], AF.Exp,
                                         bias=nm[:, h:h + 1]), inc="s")
                free_bank(ctx, bk2, "s", e2)
                b.wait("tensor", "s", e2)
                b.emit("tensor", lambda E, P=ctx.P[4], o=ctx.onesb,
                       x=ctx.at_P[0], kt=kt:
                       E.matmul(P[0:1, 0:TB], o[:, :], x[:, :],
                                start=(kt == 0), stop=(kt == 15)))
                b.emit("tensor", lambda E, P=ctx.P[5], o=ctx.onesb,
                       x=ctx.at_P[1], kt=kt:
                       E.matmul(P[0:1, 0:TB], o[:, :], x[:, :],
                                start=(kt == 0), stop=(kt == 15)))
                b.emit("tensor", lambda E, P=ctx.P[6], v=vT[kt], x=ctx.at_P[0],
                       kt=kt, hA=hA:
                       E.matmul(P[0:64, :], v[:, hA * 64:(hA + 1) * 64], x[:, :],
                                start=(kt == 0), stop=(kt == 15),
                                tile_position=(0, 0)))
                av = b.emit("tensor", lambda E, P=ctx.P[6], v=vT[kt],
                            x=ctx.at_P[1], kt=kt, hB=hB:
                            E.matmul(P[64:128, :], v[:, hB * 64:(hB + 1) * 64],
                                     x[:, :], start=(kt == 0), stop=(kt == 15),
                                     tile_position=(0, 64)), inc="t")
                g.set(("aPt", 0), "t", av)
                g.set(("aPt", 1), "t", av)
            av_done = b.val("t")
            if tables is not None:
                g.set(("atab", 0), "t", av_done)
                g.set(("atab", 1), "t", av_done)
            b.wait("vector", "t", av_done)
            b.emit("vector", lambda E, R=R, P=ctx.P[4]:
                   E.reciprocal(R[0:1, 2 * TB:3 * TB], P[0:1, 0:TB]))
            b.drain("vector")
            rc = b.emit("vector", lambda E, R=R, P=ctx.P[5]:
                        E.reciprocal(R[0:1, 3 * TB:4 * TB], P[0:1, 0:TB]),
                        inc="v")
            b.drain("vector")
            free_bank(ctx, 4, "v", rc)
            free_bank(ctx, 5, "v", rc)
            b.wait("tensor", "v", rc)
            use_bank(ctx, 7)
            b.emit("tensor", lambda E, P=ctx.P[7], o=ctx.onesrow, R=R:
                   E.matmul(P[0:64, :], o[0:1, 0:64], R[0:1, 2 * TB:3 * TB],
                            start=True, stop=True, tile_position=(0, 0)))
            cb = b.emit("tensor", lambda E, P=ctx.P[7], o=ctx.onesrow, R=R:
                        E.matmul(P[64:128, :], o[0:1, 0:64],
                                 R[0:1, 3 * TB:4 * TB], start=True, stop=True,
                                 tile_position=(0, 64)), inc="t")
            b.wait("vector", "t", cb)
            g.wait("vector", "actx", qb % 2, c)
            b.emit("vector", lambda E, d=cbuf[c], P=ctx.P[6]:
                   E.tensor_copy(d[:], P[:, :]))
            b.drain("vector")
            b.emit("vector", lambda E, d=cbuf[c], P=ctx.P[7]:
                   E.scalar_tensor_tensor(d[:], d[:], 1.0, P[:, :],
                                          op0=ALU.mult, op1=ALU.mult))
            if bias_c is not None:
                b.drain("vector")
                cv = b.emit("vector", lambda E, d=cbuf[c], bb=ctx.consts[bias_c],
                            c=c:
                            E.tensor_scalar_add(d[:], d[:], bb[:, 16 + c:17 + c]),
                            inc="v")
            else:
                cv = b.emit("vector", lambda E: E.engine_nop(), inc="v")
            free_bank(ctx, 6, "v", cv)
            free_bank(ctx, 7, "v", cv)
            ctx._ctx_ready = cv
        lds = {}
        def _ldo(ms):
            p = ms % 3
            g.wait("sync", "aw", p)
            b.emit("sync", lambda E, d=aw[p], s_=W["wo"], ms=ms:
                   E.dma_start(out=d[:], in_=s_[ms * 128:(ms + 1) * 128, :]),
                   inc="ld", n=16)
            lds[ms] = b.val("ld")
        _ldo(0)
        _ldo(1)
        bpr = ctx.consts[bproj_c] if bproj_c is not None else None
        for ms in range(NCH):
            if ms + 2 < NCH:
                _ldo(ms + 2)
            else:
                lds[ms + 2] = dummy_loads(ctx, W["wo"], 2)
            b.wait("tensor", "ld", lds[ms + 2])
            if ms == 0:
                b.grace()
                b.wait("tensor", "v", ctx._ctx_ready)
            p = ms % 3
            bank = ms % 2
            use_bank(ctx, bank)
            for kc in range(NCH):
                b.emit("tensor", lambda E, P=ctx.P[bank], w=aw[p], x=cbuf[kc],
                       kc=kc:
                       E.matmul(P[:, :], w[:, kc * 128:(kc + 1) * 128], x[:],
                                start=(kc == 0), stop=(kc == NCH - 1)),
                       inc=("t" if kc == NCH - 1 else None))
            mmo = b.val("t")
            g.set(("aw", p), "t", mmo)
            b.wait("vector", "t", mmo)
            g.wait("vector", "tmp", ms % ctx.NTMP)
            if bpr is not None:
                ov = b.emit("vector", lambda E, d=ctx.tmp[ms % ctx.NTMP], P=ctx.P[bank],
                            bb=bpr, ms=ms:
                            E.tensor_scalar_add(d[:], P[:, :], bb[:, ms:ms + 1]),
                            inc="v")
            else:
                ov = b.emit("vector", lambda E, d=ctx.tmp[ms % ctx.NTMP], P=ctx.P[bank]:
                            E.tensor_copy(d[:], P[:, :]), inc="v")
            free_bank(ctx, bank, "v", ov)
            b.drain("vector")
            b.wait("vector", "ld", x_ld)
            if u_dram is not None:
                b.emit("vector", lambda E, d=xbuf[ms], u=ctx.xf[(j % 2) ^ 1][ms]:
                       E.tensor_tensor(d[:], d[:], u[:], op=ALU.add))
                b.drain("vector")
            rv = b.emit("vector", lambda E, d=xbuf[ms], t=ctx.tmp[ms % ctx.NTMP]:
                        E.tensor_tensor(d[:], d[:], t[:], op=ALU.add), inc="v")
            g.set(("tmp", ms % ctx.NTMP), "v", rv)
            b.wait("gpsimd", "v", rv)
            stv = b.emit("gpsimd", lambda E, s_=xbuf[ms], d=xout_dram, ms=ms, j=j:
                         E.dma_start(out=d[ms * 128:(ms + 1) * 128,
                                          j * TB:(j + 1) * TB], in_=s_[:]),
                         inc="st", n=16)
            g.set(("xf", j % 2, ms), "st", stv)
            if u_dram is not None:
                g.set(("xf", (j % 2) ^ 1, ms), "st", stv)
        for c in range(NCH):
            g.set(("actx", qb % 2, c), "t", b.val("t"))
    tend = b.val("t")
    for c in range(NCH):
        g.set(("aqk", c), "t", tend)
    for i in range(SEQ // 128):
        g.set(("avT", i), "t", tend)
    for i in range(NCH):
        g.set(("avw", i), "t", tend)


def l0attnb_run(ctx, x_dram, xn_dram, xout_dram, W, ln_st):
    b = ctx.b
    qkn = ctx.consts["l3_qkn"]
    if "l3_w2dd" not in ctx.consts:
        w2dd = b.sbuf("c_l3_w2dd", (128, 1), F32)
        ctx.consts["l3_w2dd"] = w2dd
        b.wait("vector", "ld", b.val("ld"))
        b.emit("vector", lambda E, d=ctx.at_sc, s=qkn:
               E.tensor_copy(d[0:64, 3:4], s[64:128, :]))
        b.drain("vector")
        b.emit("vector", lambda E, d=w2dd, s=qkn, t=ctx.at_sc:
               E.scalar_tensor_tensor(d[0:64, :], s[0:64, :], 0.125,
                                      t[0:64, 3:4], op0=ALU.mult, op1=ALU.mult))
        b.drain("vector")
        b.emit("vector", lambda E, d=w2dd:
               E.tensor_copy(d[64:128, :], d[0:64, :]), inc="v")
        b.drain("vector")
    for s in range(NSEQ):
        _attn_proj_phase(ctx, xn_dram, W, ln_st, s, rms=True, bias_c="l3_bqkv")
        _attn_passA(ctx, s)
        ctx._negM_ready = _attn_finalizeM(ctx)
        _attn_passB_out(ctx, s, x_dram, xout_dram, W, bias_c="l3_bqkv",
                        bproj_c="l3_bproj")
    return b.val("st")

def l0attna_run(ctx, x_dram, xn_dram, u_dram, xout_dram, W, ln_st, a2_cache):
    b = ctx.b
    g = ctx.g
    for h in range(NH):
        hi = h % 2
        g.wait("sync", "atab", hi)
        b.emit("sync", lambda E, d=ctx.at_tab[hi], s_=a2_cache, h=h:
               E.dma_start(out=d[:], in_=s_[h * 128:(h + 1) * 128, 0:2432]),
               inc="ld", n=16)
        b.emit("sync", lambda E, d=ctx.at_tab[hi], s_=a2_cache, h=h:
               E.dma_start(out=d[:, 0:1536],
                           in_=s_[h * 128:(h + 1) * 128, 2432:3968]),
               inc="ld", n=16) if False else None
        tl = dummy_loads(ctx, a2_cache, 2)
        b.wait("vector", "ld", tl)
        b.emit("vector", lambda E, d=ctx.at_sc, s_=ctx.at_tab[hi]:
               E.reduce_max(d[:, 2:3], s_[:], axis=AX.X))
        b.drain("vector")
        bm = b.emit("vector", lambda E: E.engine_nop(), inc="v")
        b.wait("tensor", "v", bm)
        use_bank(ctx, 6)
        tb = b.emit("tensor", lambda E, P=ctx.P[6], sc=ctx.at_sc, idn=ctx.identf:
                    E.transpose(P[0:1, 0:128], sc[:, 2:3], idn[:]), inc="t")
        b.wait("vector", "t", tb)
        rv = b.emit("vector", lambda E, d=ctx.at_bmax, P=ctx.P[6], h=h:
                    E.reduce_max(d[0:1, h:h + 1], P[0:1, 0:128], axis=AX.X),
                    inc="v")
        b.drain("vector")
        free_bank(ctx, 6, "v", rv)
        g.set(("atab", hi), "v", rv)
    for s in range(NSEQ):
        _attn_proj_phase(ctx, xn_dram, W, ln_st, s, rms=False, bias_c=None)
        _attn_passA(ctx, s)
        ctx._negM_ready = _attn_finalizeM(ctx, tables=True)
        _attn_passB_out(ctx, s, x_dram, xout_dram, W, bias_c=None,
                        bproj_c=None, tables=dict(a2=a2_cache), u_dram=u_dram)
    return b.val("st")


def l1aa_run(ctx, x_dram, xn_dram, xout_dram, W, ln_st, decay_cache, r_dram):
    b = ctx.b
    g = ctx.g
    qF, kF, vT = ctx.at_qF, ctx.at_kF, ctx.at_vT
    R = ctx.at_rows
    gnc = ctx.consts["l5_gnc"]
    gnbc = ctx.consts["l5_gnbc"]
    CNT = float(NH * DK * SEQ)
    b.emit("vector", lambda E, z=ctx.at_zero: E.memset(z[:], 0.0), inc="v")
    b.drain("vector")
    for s in range(NSEQ):
        _attn_proj_phase(ctx, xn_dram, W, ln_st, s, rms=False, bias_c=None)
        b.wait("tensor", "v", b.val("v"))
        g.wait("sync", "pkvb", 0)
        b.emit("sync", lambda E, d=ctx.at_pkv, s_=W["pkv"]:
               E.dma_start(out=d[:], in_=s_[0:128, :]), inc="ld", n=16)
        pkv_ld = dummy_loads(ctx, W["pkv"], 2)
        b.wait("tensor", "ld", pkv_ld)
        first_stat = True
        for qb in range(SEQ // TB):
            j = s * (SEQ // TB) + qb
            for c in range(NCH):
                hA, hB = 2 * c, 2 * c + 1
                for hi, h in enumerate((hA, hB)):
                    g.wait("sync", "atab", hi)
                    b.emit("sync", lambda E, d=ctx.at_tab[hi], s_=decay_cache,
                           h=h, qb=qb:
                           E.dma_start(out=d[:],
                                       in_=s_[h * 128:(h + 1) * 128,
                                              qb * TB:qb * TB + 2432]),
                           inc="ld", n=16)
                tab_ld = dummy_loads(ctx, decay_cache, 2)
                b.wait("tensor", "ld", tab_ld)
                b.grace()
                use_bank(ctx, 6)
                use_bank(ctx, 7)
                b.emit("tensor", lambda E, P=ctx.P[7], pk=ctx.at_pkv, q=qF[c],
                       hA=hA, qb=qb:
                       E.matmul(P[0:64, :], pk[0:64, hA * 64:(hA + 1) * 64],
                                q[0:64, qb * TB:(qb + 1) * TB], start=True,
                                stop=True, tile_position=(0, 0)))
                cr = b.emit("tensor", lambda E, P=ctx.P[7], pk=ctx.at_pkv,
                            q=qF[c], hB=hB, qb=qb:
                            E.matmul(P[64:128, :],
                                     pk[64:128, hB * 64:(hB + 1) * 64],
                                     q[64:128, qb * TB:(qb + 1) * TB],
                                     start=True, stop=True,
                                     tile_position=(64, 64)), inc="t")
                nkt = 4 * qb + 4
                for kt in range(nkt):
                    bk = kt % 2
                    bk2 = 2 + kt % 2
                    use_bank(ctx, bk)
                    mm1 = b.emit("tensor", lambda E, P=ctx.P[bk], k=kF[c],
                                 q=qF[c], kt=kt, qb=qb:
                                 E.matmul(P[:, :],
                                          k[0:64, kt * 128:(kt + 1) * 128],
                                          q[0:64, qb * TB:(qb + 1) * TB],
                                          start=True, stop=True,
                                          tile_position=(0, 0)), inc="t")
                    use_bank(ctx, bk2)
                    mm2 = b.emit("tensor", lambda E, P=ctx.P[bk2], k=kF[c],
                                 q=qF[c], kt=kt, qb=qb:
                                 E.matmul(P[:, :],
                                          k[64:128, kt * 128:(kt + 1) * 128],
                                          q[64:128, qb * TB:(qb + 1) * TB],
                                          start=True, stop=True,
                                          tile_position=(64, 0)), inc="t")
                    n0 = 1920 - 128 * kt
                    b.wait("vector", "t", mm2)
                    g.wait("vector", "aPt", 0)
                    e1 = b.emit("vector", lambda E, d=ctx.at_P[0], P=ctx.P[bk],
                                tb_=ctx.at_tab[0], n0=n0:
                                E.tensor_tensor(d[:], P[:, :],
                                                tb_[:, n0:n0 + TB],
                                                op=ALU.mult), inc="v")
                    free_bank(ctx, bk, "v", e1)
                    b.drain("vector")
                    g.wait("vector", "aPt", 1)
                    e2 = b.emit("vector", lambda E, d=ctx.at_P[1], P=ctx.P[bk2],
                                tb_=ctx.at_tab[1], n0=n0:
                                E.tensor_tensor(d[:], P[:, :],
                                                tb_[:, n0:n0 + TB],
                                                op=ALU.mult), inc="v")
                    free_bank(ctx, bk2, "v", e2)
                    b.wait("tensor", "v", e2)
                    av = b.emit("tensor", lambda E, P=ctx.P[6], v=vT[kt],
                                x=ctx.at_P[0], kt=kt, hA=hA, first=(kt == 0),
                                last=(kt == nkt - 1):
                                E.matmul(P[0:64, :],
                                         v[:, hA * 64:(hA + 1) * 64], x[:, :],
                                         start=first, stop=last,
                                         tile_position=(0, 0)))
                    av = b.emit("tensor", lambda E, P=ctx.P[6], v=vT[kt],
                                x=ctx.at_P[1], kt=kt, hB=hB, first=(kt == 0),
                                last=(kt == nkt - 1):
                                E.matmul(P[64:128, :],
                                         v[:, hB * 64:(hB + 1) * 64], x[:, :],
                                         start=first, stop=last,
                                         tile_position=(0, 64)), inc="t")
                    g.set(("aPt", 0), "t", b.val("t"))
                    g.set(("aPt", 1), "t", b.val("t"))
                g.set(("atab", 0), "t", b.val("t"))
                g.set(("atab", 1), "t", b.val("t"))
                av_done = b.val("t")
                b.wait("vector", "t", av_done)
                g.wait("vector", "arb", 0)
                b.emit("vector", lambda E, d=ctx.at_vf, P=ctx.P[6]:
                       E.tensor_copy(d[:], P[:, :]))
                b.drain("vector")
                rv = b.emit("vector", lambda E, d=ctx.at_rb, a=ctx.at_vf,
                            P=ctx.P[7]:
                            E.scalar_tensor_tensor(d[:], P[:, :], 1024.0, a[:],
                                                   op0=ALU.mult, op1=ALU.add),
                            inc="v")
                free_bank(ctx, 6, "v", rv)
                free_bank(ctx, 7, "v", rv)
                b.wait("scalar", "v", rv)
                g.wait("scalar", "x2", 0)
                sq = b.emit("scalar", lambda E, d=ctx.x2[0], s_=ctx.at_rb:
                            E.activation(d[:], s_[:], AF.Square), inc="s")
                if first_stat:
                    use_bank(ctx, 4)
                    use_bank(ctx, 5)
                b.wait("tensor", "v", rv)
                b.emit("tensor", lambda E, P=ctx.P[4], o=ctx.onesb, x=ctx.at_rb,
                       f=first_stat:
                       E.matmul(P[0:1, 0:TB], o[:, :], x[:, :], start=f,
                                stop=False))
                b.wait("tensor", "s", sq)
                st_mm = b.emit("tensor", lambda E, P=ctx.P[5], o=ctx.onesb,
                               x=ctx.x2[0], f=first_stat:
                               E.matmul(P[0:1, 0:TB], o[:, :], x[:, :], start=f,
                                        stop=False), inc="t")
                g.set(("x2", 0), "t", st_mm)
                g.set(("aPt", 0), "t", st_mm)
                first_stat = False
                b.wait("gpsimd", "v", rv)
                sr = b.emit("gpsimd", lambda E, s_=ctx.at_rb, d=r_dram, c=c, j=j:
                            E.dma_start(out=d[c * 128:(c + 1) * 128,
                                             j * TB:(j + 1) * TB], in_=s_[:]),
                            inc="st", n=16)
                g.set(("arb", 0), "st", sr)
        b.emit("tensor", lambda E, P=ctx.P[4], o=ctx.onesb, z=ctx.at_zero:
               E.matmul(P[0:1, 0:TB], o[:, :], z[:, :], start=False, stop=True))
        scl = b.emit("tensor", lambda E, P=ctx.P[5], o=ctx.onesb, z=ctx.at_zero:
                     E.matmul(P[0:1, 0:TB], o[:, :], z[:, :], start=False,
                              stop=True), inc="t")
        b.wait("vector", "t", scl)
        b.emit("vector", lambda E, R=R, P=ctx.P[4]:
               E.reduce_sum(R[0:1, 0:1], P[0:1, 0:TB], axis=AX.X))
        b.drain("vector")
        b.emit("vector", lambda E, R=R, P=ctx.P[5]:
               E.reduce_sum(R[0:1, 1:2], P[0:1, 0:TB], axis=AX.X))
        b.drain("vector")
        b.emit("vector", lambda E, R=R, CNT=CNT:
               E.tensor_scalar_mul(R[0:1, 2:3], R[0:1, 0:1], 1.0 / CNT))
        b.drain("vector")
        b.emit("vector", lambda E, R=R:
               E.tensor_tensor(R[0:1, 3:4], R[0:1, 2:3], R[0:1, 2:3],
                               op=ALU.mult))
        b.drain("vector")
        b.emit("vector", lambda E, R=R, CNT=CNT:
               E.scalar_tensor_tensor(R[0:1, 4:5], R[0:1, 1:2], 1.0 / CNT,
                                      R[0:1, 3:4], op0=ALU.mult,
                                      op1=ALU.subtract))
        b.drain("vector")
        sv2 = b.emit("vector", lambda E, R=R:
                     E.tensor_scalar_add(R[0:1, 5:6], R[0:1, 4:5], 1e-5),
                     inc="v")
        free_bank(ctx, 4, "v", sv2)
        free_bank(ctx, 5, "v", sv2)
        b.wait("scalar", "v", sv2)
        b.emit("scalar", lambda E, R=R:
               E.activation(R[0:1, 6:7], R[0:1, 5:6], AF.Ln))
        b.drain("scalar")
        rss = b.emit("scalar", lambda E, R=R:
                     E.activation(R[0:1, 6:7], R[0:1, 6:7], AF.Exp, scale=-0.5),
                     inc="s")
        b.wait("tensor", "s", rss)
        use_bank(ctx, 4)
        bc = b.emit("tensor", lambda E, P=ctx.P[4], o=ctx.onesrow, R=R:
                    E.matmul(P[:, 0:1], o[0:1, 0:128], R[0:1, 2:3],
                             start=True, stop=True), inc="t")
        use_bank(ctx, 5)
        bc2 = b.emit("tensor", lambda E, P=ctx.P[5], o=ctx.onesrow, R=R:
                     E.matmul(P[:, 0:1], o[0:1, 0:128], R[0:1, 6:7],
                              start=True, stop=True), inc="t")
        b.wait("vector", "t", bc2)
        b.emit("vector", lambda E, d=ctx.at_sc, P=ctx.P[4]:
               E.tensor_copy(d[:, 0:1], P[:, 0:1]))
        b.drain("vector")
        mb = b.emit("vector", lambda E, d=ctx.at_sc, P=ctx.P[5]:
                    E.tensor_copy(d[:, 1:2], P[:, 0:1]), inc="v")
        free_bank(ctx, 4, "v", mb)
        free_bank(ctx, 5, "v", mb)
        b.drain("vector")
        for qb in range(SEQ // TB):
            j = s * (SEQ // TB) + qb
            xbuf = ctx.xf[j % 2]
            for c in range(NCH):
                g.wait("sync", "xf", j % 2, c)
                b.emit("sync", lambda E, d=xbuf[c], s_=x_dram, c=c, j=j:
                       E.dma_start(out=d[:], in_=s_[c * 128:(c + 1) * 128,
                                                   j * TB:(j + 1) * TB]),
                       inc="ld", n=16)
            rb_lds = {}
            def _ldrl(c):
                g.wait("sync", "arl", c % 2)
                b.emit("sync", lambda E, d=ctx.at_rl[c % 2], s_=r_dram, c=c, j=j:
                       E.dma_start(out=d[:], in_=s_[c * 128:(c + 1) * 128,
                                                   j * TB:(j + 1) * TB]),
                       inc="ld", n=16)
                rb_lds[c] = b.val("ld")
            _ldrl(0)
            _ldrl(1)
            x_ld = b.val("ld")
            for c in range(NCH):
                b.wait("vector", "ld", rb_lds[min(c + 1, NCH + 1)])
                if c == 0:
                    b.wait("vector", "ld", x_ld)
                b.emit("vector", lambda E, d=ctx.at_sc, gn=gnc, c=c:
                       E.tensor_tensor(d[:, 2:3], d[:, 1:2], gn[:, c:c + 1],
                                       op=ALU.mult))
                b.drain("vector")
                b.emit("vector", lambda E, d=ctx.at_sc, gb=gnbc, c=c:
                       E.scalar_tensor_tensor(d[:, 3:4], d[:, 2:3], -1.0,
                                              d[:, 0:1], op0=ALU.mult,
                                              op1=ALU.mult))
                b.drain("vector")
                b.emit("vector", lambda E, d=ctx.at_sc, gb=gnbc, c=c:
                       E.tensor_tensor(d[:, 3:4], d[:, 3:4], gb[:, c:c + 1],
                                       op=ALU.add))
                b.drain("vector")
                g.wait("vector", "tmp", c % ctx.NTMP)
                b.emit("vector", lambda E, d=ctx.tmp[c % ctx.NTMP], s_=ctx.at_rl[c % 2],
                       sc=ctx.at_sc:
                       E.tensor_scalar(d[:], s_[:], sc[:, 2:3], sc[:, 3:4],
                                       op0=ALU.mult, op1=ALU.add))
                b.drain("vector")
                rv2 = b.emit("vector", lambda E, d=xbuf[c], t=ctx.tmp[c % ctx.NTMP]:
                             E.tensor_tensor(d[:], d[:], t[:], op=ALU.add),
                             inc="v")
                g.set(("tmp", c % ctx.NTMP), "v", rv2)
                g.set(("arl", c % 2), "v", rv2)
                if c + 2 < NCH:
                    _ldrl(c + 2)
                elif c + 2 == NCH:
                    rb_lds[NCH] = dummy_loads(ctx, r_dram, 1)
                    rb_lds[NCH + 1] = dummy_loads(ctx, r_dram, 1)
                b.wait("gpsimd", "v", rv2)
                sv3 = b.emit("gpsimd", lambda E, s_=xbuf[c], d=xout_dram,
                             c=c, j=j:
                             E.dma_start(out=d[c * 128:(c + 1) * 128,
                                              j * TB:(j + 1) * TB], in_=s_[:]),
                             inc="st", n=16)
                g.set(("xf", j % 2, c), "st", sv3)
        tend = b.val("t")
        for c in range(NCH):
            g.set(("aqk", c), "t", tend)
        for i in range(SEQ // 128):
            g.set(("avT", i), "t", tend)
        for i in range(NCH):
            g.set(("avw", i), "t", tend)
    return b.val("st")


def alloc_l1ab(ctx):
    b = ctx.b
    ctx.lb_kv = [b.sbuf(f"lkv{i}", (128, 1024), BF16) for i in range(NCH)]
    ctx.lb_kc = b.sbuf("lkc", (128, NCH), BF16)


def l1ab_run(ctx, x_dram, xn_dram, xout_dram, W, ln_st):
    b = ctx.b
    g = ctx.g
    EPS = 1e-6
    qF, aw, vw = ctx.at_qF, ctx.at_w, ctx.at_vw
    ktm = ctx.at_vT
    vtm = [ctx.at_kF[i // 2][:, (i % 2) * 1024:(i % 2 + 1) * 1024]
           for i in range(16)]
    for s in range(NSEQ):
        for jb in range(SEQ // TB):
            j = s * (SEQ // TB) + jb
            nset = len(ctx.ffn_xnq)
            xnq = ctx.ffn_xnq[j % nset]
            b.wait("sync", "st", ln_st[j])
            for c in range(NCH):
                g.wait("sync", "xnq", j % nset, c)
                b.emit("sync", lambda E, d=xnq[c], s_=xn_dram, c=c, j=j:
                       E.dma_start(out=d[:], in_=s_[c * 128:(c + 1) * 128,
                                                   j * TB:(j + 1) * TB]),
                       inc="ld", n=16)
            for wname, dst, gname, relu in (("wv", vtm, "akT", False),
                                            ("wk", ktm, "avT", True)):
                vlds = {}
                def _ldv(c2, wname=wname):
                    p = c2 % 4
                    g.wait("sync", "avw", p)
                    b.emit("sync", lambda E, d=vw[p], s_=W[wname], c2=c2:
                           E.dma_start(out=d[:],
                                       in_=s_[c2 * 128:(c2 + 1) * 128, :]),
                           inc="ld", n=16)
                    vlds[c2] = b.val("ld")
                for c2 in range(3):
                    _ldv(c2)
                for c2 in range(NCH):
                    if c2 + 3 < NCH:
                        _ldv(c2 + 3)
                    elif c2 + 3 == NCH:
                        vlds[NCH] = dummy_loads(ctx, W[wname], 1)
                        vlds[NCH + 1] = dummy_loads(ctx, W[wname], 1)
                    b.wait("tensor", "ld", vlds[min(c2 + 2, NCH + 1)])
                    if c2 == 0:
                        b.grace()
                    p = c2 % 4
                    bank = c2 % 2
                    use_bank(ctx, bank)
                    for k in range(NCH):
                        b.emit("tensor", lambda E, P=ctx.P[bank], w=vw[p],
                               x=xnq[k], k=k:
                               E.matmul(P[:, :], w[:, k * 128:(k + 1) * 128],
                                        x[:], start=(k == 0),
                                        stop=(k == NCH - 1)),
                               inc=("t" if k == NCH - 1 else None))
                    vmm = b.val("t")
                    g.set(("avw", p), "t", vmm)
                    b.wait("vector", "t", vmm)
                    g.wait("vector", "vfs", 0)
                    g.wait("vector", "prep", 0)
                    if relu:
                        ev = b.emit("vector", lambda E, d=ctx.at_vf,
                                    P=ctx.P[bank]:
                                    E.tensor_scalar(d[:], P[:, :], 0.0, EPS,
                                                    op0=ALU.max, op1=ALU.add),
                                    inc="v")
                    else:
                        ev = b.emit("vector", lambda E, d=ctx.at_vf,
                                    P=ctx.P[bank]:
                                    E.tensor_copy(d[:], P[:, :]), inc="v")
                    free_bank(ctx, bank, "v", ev)
                    b.wait("tensor", "v", ev)
                    for ts in range(4):
                        bank2 = 2 + ts % 2
                        use_bank(ctx, bank2)
                        tt = b.emit("tensor", lambda E, P=ctx.P[bank2],
                                    s_=ctx.at_vf, idn=ctx.identf, ts=ts:
                                    E.transpose(P[:, 0:128],
                                                s_[:, ts * 128:(ts + 1) * 128],
                                                idn[:]), inc="t")
                        b.wait("vector", "t", tt)
                        g.wait("vector", gname, jb * 4 + ts)
                        tv = b.emit("vector", lambda E, d=dst[jb * 4 + ts],
                                    P=ctx.P[bank2], c2=c2:
                                    E.tensor_copy(d[:, c2 * 128:(c2 + 1) * 128],
                                                  P[:, 0:128]), inc="v")
                        free_bank(ctx, bank2, "v", tv)
                    g.set(("vfs", 0), "t", b.val("t"))
                    g.set(("prep", 0), "t", b.val("t"))
            lds = {}
            def _ldq(c):
                p = c % 3
                g.wait("sync", "aw", p)
                b.emit("sync", lambda E, d=aw[p], s_=W["wq"], c=c:
                       E.dma_start(out=d[:], in_=s_[c * 128:(c + 1) * 128, :]),
                       inc="ld", n=16)
                lds[c] = b.val("ld")
            _ldq(0)
            _ldq(1)
            for c in range(NCH):
                if c + 2 < NCH:
                    _ldq(c + 2)
                else:
                    lds[c + 2] = dummy_loads(ctx, W["wq"], 2)
                b.wait("tensor", "ld", lds[c + 2])
                if c == 0:
                    b.grace()
                p = c % 3
                bank = c % 2
                use_bank(ctx, bank)
                for k in range(NCH):
                    b.emit("tensor", lambda E, P=ctx.P[bank], w=aw[p],
                           x=xnq[k], k=k:
                           E.matmul(P[:, :], w[:, k * 128:(k + 1) * 128], x[:],
                                    start=(k == 0), stop=(k == NCH - 1)),
                           inc=("t" if k == NCH - 1 else None))
                mmq = b.val("t")
                g.set(("aw", p), "t", mmq)
                b.wait("vector", "t", mmq)
                g.wait("vector", "aqk", c)
                vq = b.emit("vector", lambda E, d=qF[c], P=ctx.P[bank], c=c,
                            jb=jb:
                            E.tensor_scalar(d[:, jb * TB:(jb + 1) * TB],
                                            P[:, :], 0.0, EPS, op0=ALU.max,
                                            op1=ALU.add), inc="v")
                free_bank(ctx, bank, "v", vq)
                g.set(("aqk", c), "v", vq)
            for c2 in range(NCH):
                g.set(("xnq", j % nset, c2), "t", b.val("t"))
        b.wait("tensor", "v", b.val("v"))
        use_bank(ctx, 6)
        for ds in range(NCH):
            bank = (ds % 2) * 2
            use_bank(ctx, bank)
            use_bank(ctx, bank + 1)
            for kt in range(16):
                for half in range(2):
                    b.emit("tensor", lambda E, PA=ctx.PSALL, x=ktm[kt],
                           v=vtm[kt], ds=ds, half=half, bank=bank, kt=kt:
                           E.matmul(PA[:, (bank + half) * 512:
                                        (bank + half + 1) * 512],
                                    x[:, ds * 128:(ds + 1) * 128],
                                    v[:, half * 512:(half + 1) * 512],
                                    start=(kt == 0), stop=(kt == 15)))
                b.emit("tensor", lambda E, P=ctx.P[6], x=ktm[kt],
                       o=ctx.onesb, ds=ds, kt=kt:
                       E.matmul(P[:, ds:ds + 1], x[:, ds * 128:(ds + 1) * 128],
                                o[:, :], start=(kt == 0), stop=(kt == 15)),
                       inc=("t" if kt == 15 else None))
            kvm = b.val("t")
            b.wait("vector", "t", kvm)
            g.wait("vector", "lkv", ds)
            kve = b.emit("vector", lambda E, d=ctx.lb_kv[ds], PA=ctx.PSALL,
                         bank=bank:
                         E.tensor_copy(d[:, :],
                                       PA[:, bank * 512:(bank + 2) * 512]),
                         inc="v")
            free_bank(ctx, bank, "v", kve)
            free_bank(ctx, bank + 1, "v", kve)
        g.wait("vector", "lkc", 0)
        kce = b.emit("vector", lambda E, d=ctx.lb_kc, P=ctx.P[6]:
                     E.tensor_copy(d[:, :], P[:, 0:NCH]), inc="v")
        free_bank(ctx, 6, "v", kce)
        for jb in range(SEQ // TB):
            j = s * (SEQ // TB) + jb
            cbuf = ctx.at_ctx[jb % 2]
            xbuf = ctx.xf[j % 2]
            for c in range(NCH):
                g.wait("sync", "xf", j % 2, c)
                b.emit("sync", lambda E, d=xbuf[c], s_=x_dram, c=c, j=j:
                       E.dma_start(out=d[:], in_=s_[c * 128:(c + 1) * 128,
                                                   j * TB:(j + 1) * TB]),
                       inc="ld", n=16)
            x_ld = b.val("ld")
            b.wait("tensor", "v", kce)
            use_bank(ctx, 7)
            for dc in range(NCH):
                b.emit("tensor", lambda E, P=ctx.P[7], kc=ctx.lb_kc, q=qF[dc],
                       dc=dc, jb=jb:
                       E.matmul(P[0:1, 0:TB], kc[:, dc:dc + 1],
                                q[:, jb * TB:(jb + 1) * TB],
                                start=(dc == 0), stop=(dc == NCH - 1)),
                       inc=("t" if dc == NCH - 1 else None))
            zmm = b.val("t")
            R = ctx.at_rows
            b.wait("vector", "t", zmm)
            b.emit("vector", lambda E, R=R, P=ctx.P[7]:
                   E.tensor_scalar_add(R[0:1, 0:TB], P[0:1, 0:TB], EPS))
            b.drain("vector")
            zv = b.emit("vector", lambda E, R=R:
                        E.reciprocal(R[0:1, TB:2 * TB], R[0:1, 0:TB]),
                        inc="v")
            free_bank(ctx, 7, "v", zv)
            b.wait("tensor", "v", zv)
            use_bank(ctx, 6)
            b.emit("tensor", lambda E, P=ctx.P[6], o=ctx.onesrow, R=R:
                   E.matmul(P[0:64, :], o[0:1, 0:64], R[0:1, TB:2 * TB],
                            start=True, stop=True, tile_position=(0, 0)))
            zb = b.emit("tensor", lambda E, P=ctx.P[6], o=ctx.onesrow, R=R:
                        E.matmul(P[64:128, :], o[0:1, 0:64], R[0:1, TB:2 * TB],
                                 start=True, stop=True, tile_position=(0, 64)),
                        inc="t")
            for es in range(NCH):
                bank = es % 2
                use_bank(ctx, bank)
                for dc in range(NCH):
                    b.emit("tensor", lambda E, P=ctx.P[bank], kv=ctx.lb_kv[dc],
                           q=qF[dc], es=es, dc=dc, jb=jb:
                           E.matmul(P[:, :], kv[:, es * 128:(es + 1) * 128],
                                    q[:, jb * TB:(jb + 1) * TB],
                                    start=(dc == 0), stop=(dc == NCH - 1)),
                           inc=("t" if dc == NCH - 1 else None))
                amm = b.val("t")
                b.wait("vector", "t", amm)
                g.wait("vector", "actx", jb % 2, es)
                b.emit("vector", lambda E, d=cbuf[es], P=ctx.P[bank]:
                       E.tensor_copy(d[:], P[:, :]))
                b.drain("vector")
                av = b.emit("vector", lambda E, d=cbuf[es], P=ctx.P[6]:
                            E.scalar_tensor_tensor(d[:], d[:], 1.0, P[:, :],
                                                   op0=ALU.mult, op1=ALU.mult),
                            inc="v")
                free_bank(ctx, bank, "v", av)
            ctx_ready = b.val("v")
            free_bank(ctx, 6, "v", ctx_ready)
            lds = {}
            def _ldo(ms):
                p = ms % 3
                g.wait("sync", "aw", p)
                b.emit("sync", lambda E, d=aw[p], s_=W["wo"], ms=ms:
                       E.dma_start(out=d[:], in_=s_[ms * 128:(ms + 1) * 128, :]),
                       inc="ld", n=16)
                lds[ms] = b.val("ld")
            _ldo(0)
            _ldo(1)
            bpr = ctx.consts["l7_bout"]
            for ms in range(NCH):
                if ms + 2 < NCH:
                    _ldo(ms + 2)
                else:
                    lds[ms + 2] = dummy_loads(ctx, W["wo"], 2)
                b.wait("tensor", "ld", lds[ms + 2])
                if ms == 0:
                    b.grace()
                    b.wait("tensor", "v", ctx_ready)
                p = ms % 3
                bank = ms % 2
                use_bank(ctx, bank)
                for kc in range(NCH):
                    b.emit("tensor", lambda E, P=ctx.P[bank], w=aw[p],
                           x=cbuf[kc], kc=kc:
                           E.matmul(P[:, :], w[:, kc * 128:(kc + 1) * 128],
                                    x[:], start=(kc == 0),
                                    stop=(kc == NCH - 1)),
                           inc=("t" if kc == NCH - 1 else None))
                mmo = b.val("t")
                g.set(("aw", p), "t", mmo)
                b.wait("vector", "t", mmo)
                g.wait("vector", "tmp", ms % ctx.NTMP)
                ov = b.emit("vector", lambda E, d=ctx.tmp[ms % ctx.NTMP],
                            P=ctx.P[bank], bb=bpr, ms=ms:
                            E.tensor_scalar_add(d[:], P[:, :], bb[:, ms:ms + 1]),
                            inc="v")
                free_bank(ctx, bank, "v", ov)
                b.drain("vector")
                b.wait("vector", "ld", x_ld)
                rv = b.emit("vector", lambda E, d=xbuf[ms], t=ctx.tmp[ms % ctx.NTMP]:
                            E.tensor_tensor(d[:], d[:], t[:], op=ALU.add),
                            inc="v")
                g.set(("tmp", ms % ctx.NTMP), "v", rv)
                b.wait("gpsimd", "v", rv)
                stv = b.emit("gpsimd", lambda E, s_=xbuf[ms], d=xout_dram,
                             ms=ms, j=j:
                             E.dma_start(out=d[ms * 128:(ms + 1) * 128,
                                              j * TB:(j + 1) * TB], in_=s_[:]),
                             inc="st", n=16)
                g.set(("xf", j % 2, ms), "st", stv)
            for es in range(NCH):
                g.set(("actx", jb % 2, es), "t", b.val("t"))
        tend = b.val("t")
        for c in range(NCH):
            g.set(("aqk", c), "t", tend)
        for i in range(SEQ // 128):
            g.set(("avT", i), "t", tend)
            g.set(("akT", i), "t", tend)
        for i in range(4):
            g.set(("avw", i), "t", tend)
        for ds in range(NCH):
            g.set(("lkv", ds), "t", tend)
        g.set(("lkc", 0), "t", tend)
    return b.val("st")


def alloc_fused(ctx):
    b = ctx.b
    ctx.ffn_t12 = [b.sbuf(f"ft12_{i}", (128, TB), F32) for i in range(2)]
    ctx.ffn_t1b = [b.sbuf(f"ft1b_{i}", (128, TB), BF16) for i in range(2)]
    ctx.prep_f = ctx.ffn_t12
    ctx.prep_b = ctx.ffn_t1b
    A = [b.sbuf(f"plA{i}", (128, SEQ), BF16) for i in range(NCH)]
    B = [b.sbuf(f"plB{i}", (128, SEQ), BF16) for i in range(NCH)]
    C = [b.sbuf(f"plC{i}", (128, 4096), BF16) for i in range(4)]
    Dp = [b.sbuf(f"plD{i}", (128, 1024), BF16) for i in range(8)]
    E = b.sbuf("plE", (128, 8960), BF16)
    ctx.plE = E
    ctx.at_qF = A
    ctx.at_kF = B
    ctx.at_vT = [C[i // 4][:, (i % 4) * 1024:(i % 4 + 1) * 1024]
                 for i in range(16)]
    ctx.at_w = Dp[0:3]
    ctx.at_vw = Dp[3:7]
    ctx.at_vf = ctx.prep_f[0]
    ctx.at_q2 = ctx.x2
    ctx.at_rows = ctx.rows
    ctx.at_mq = b.sbuf("amq", (128, 32), F32)
    ctx.at_sc = b.sbuf("asc", (128, 4), F32)
    ctx.at_negM = b.sbuf("anegM", (128, NH), F32)
    ctx.at_P = ctx.x2
    ctx.at_tab = [E[:, 0:2432], E[:, 2432:4864]]
    ctx.at_bmax = b.sbuf("abmax", (1, NH), F32)
    ctx.at_pkv = E[:, 4864:5888]
    ctx.at_rb = E[:, 5888:6400]
    ctx.at_rl = [E[:, 6400:6912], E[:, 6912:7424]]
    ctx.at_zero = E[:, 7424:7936]
    ctx.lb_kv = [E[:, i * 1024:(i + 1) * 1024] for i in range(NCH)]
    ctx.lb_kc = b.sbuf("lkc", (128, NCH), BF16)
    ctx.alloc_xnq(nsets=1)
    ctx.ffn_hA = [A[m // 4][:, (m % 4) * TB:(m % 4 + 1) * TB]
                  for m in range(32)]
    ctx.ffn_hB = [B[m // 4][:, (m % 4) * TB:(m % 4 + 1) * TB]
                  for m in range(32)]
    ctx.ffn_wbuf = Dp[0:8]
    ctx.ffn_hwbuf = C
    ctx.prep_q = [b.sbuf(f"prq{i}", (128, 512), I32) for i in range(2)]
    ctx.prep_f2 = [ctx.xf[0][0], ctx.xf[0][1]]


def set_ctx_chunks(ctx, mode):
    E = ctx.plE
    if mode == "tabs":
        cb = [E[:, 4864 + c * TB:4864 + (c + 1) * TB] for c in range(NCH)]
    elif mode == "lo":
        cb = [E[:, c * TB:(c + 1) * TB] for c in range(NCH)]
    else:
        cb = [ctx.at_kF[c][:, 0:TB] for c in range(NCH)]
    ctx.at_ctx = [cb, cb]


from concourse.bass_utils import run_bass_kernel_spmd

NB_BUCKETS, MAX_DIST = 32, 128


def bucket_of(rel):
    nb = NB_BUCKETS // 2
    ret = (rel > 0).astype(np.int64) * nb
    n = np.abs(rel)
    max_exact = nb // 2
    is_small = n < max_exact
    n_safe = np.maximum(n, 1).astype(np.float64)
    val_large = max_exact + (np.log(n_safe / max_exact) /
                             np.log(MAX_DIST / max_exact)
                             * (nb - max_exact)).astype(np.int64)
    val_large = np.minimum(val_large, nb - 1)
    return ret + np.where(is_small, n, val_large)


def build_a2(rel_bias):
    idx = np.arange(4095) - 2047
    buck = bucket_of(idx)
    g = rel_bias[buck, :]
    p = np.arange(128)[:, None]
    n = np.arange(3968)[None, :]
    gather = 3967 + p - n
    A2 = g[gather]
    A2 = np.ascontiguousarray(A2.transpose(2, 0, 1)).reshape(16 * 128, 3968)
    return A2.astype(np.float32)


def build_decay():
    gamma = 1.0 - 2.0 ** (-5.0 - np.arange(16))
    i = np.arange(4095)
    out = np.zeros((16, 4095), np.float64)
    for h in range(16):
        rel = i - 2047.0
        out[h] = np.where(rel <= 0, gamma[h] ** np.maximum(-rel, 0), 0.0)
    p = np.arange(128)[:, None]
    n = np.arange(3968)[None, :]
    gather = 3967 + p - n
    A = out[:, gather]
    return np.ascontiguousarray(A.reshape(16 * 128, 3968)).astype(np.float32)


def build_fused(nc):
    b = Builder(nc)
    x_in = nc.dram_tensor("x", [D, TOK], F32, kind="ExternalInput").ap()
    idn = nc.dram_tensor("idn", [128, 128], F32, kind="ExternalInput").ap()
    y_out = nc.dram_tensor("y", [D, TOK], F32, kind="ExternalOutput").ap()
    ext = {}
    for nm in ("l1wq", "l1wk", "l1wv", "l1wo", "l3wq", "l3wk", "l3wv",
               "l3wo", "l5wq", "l5wk", "l5wv", "l7wq", "l7wk", "l7wv",
               "l7wo"):
        ext[nm] = nc.dram_tensor(nm + "P", [8, 128, 1024], F32,
                                 kind="ExternalInput")
    ext["a2f"] = nc.dram_tensor("a2f", [16 * 128, 3968], F32,
                                kind="ExternalInput")
    ext["dcf"] = nc.dram_tensor("dcf", [16 * 128, 3968], F32,
                                kind="ExternalInput")
    ext["pkvf"] = nc.dram_tensor("pkvf", [128, 1024], F32,
                                 kind="ExternalInput")
    ext["fc1q"] = nc.dram_tensor("fc1q", [32, 128, 1024], I32,
                                 kind="ExternalInput")
    ext["fc2q"] = nc.dram_tensor("fc2q", [8, 128, 4096], I32,
                                 kind="ExternalInput")
    ext["fc1s"] = nc.dram_tensor("fc1s", [32, 128, 1024], F32,
                                 kind="ExternalInput")
    ext["fc2s"] = nc.dram_tensor("fc2s", [8, 128, 4096], F32,
                                 kind="ExternalInput")
    for nm, shp in [("w1", (32, 128, 1024)), ("w2", (32, 128, 1024)),
                    ("h1", (32, 128, 4096)), ("h2", (32, 128, 4096)),
                    ("w3", (8, 128, 4096))]:
        ext[nm] = nc.dram_tensor(nm + "P", list(shp), F32,
                                 kind="ExternalInput")
    sc = {}
    for nm, shp in [("l1_g", (128, 8)), ("l1_b", (128, 8)),
                    ("l2_g", (128, 8)), ("l2_b", (128, 8)),
                    ("l2_b1", (128, 32)), ("l2_b2", (128, 8)),
                    ("l3_bqkv", (128, 24)), ("l3_qkn", (128, 1)),
                    ("l3_bproj", (128, 8)),
                    ("l5_gnc", (128, 8)), ("l5_gnbc", (128, 8)),
                    ("l6_s1", (128, 32)), ("l6_s2", (128, 32)),
                    ("l6_s1h", (128, 32)), ("l6_s2h", (128, 32)),
                    ("l6_s3", (128, 8)), ("l7_bout", (128, 8))]:
        sc[nm] = nc.dram_tensor(nm, list(shp), F32, kind="ExternalInput")
    xn = nc.dram_tensor("xn_scr", [D, TOK], BF16, kind="Internal").ap()
    U = nc.dram_tensor("u_scr", [D, TOK], F32, kind="Internal").ap()
    taps = [nc.dram_tensor(f"tap{i}", [D, TOK], F32, kind="Internal").ap()
            for i in range(1, 8)]
    rscr = nc.dram_tensor("r_scr", [D, TOK], BF16, kind="Internal").ap()
    cch = {}
    for nm in ("l1wq", "l1wk", "l1wv", "l1wo", "l3wq", "l3wk", "l3wv",
               "l3wo", "l5wq", "l5wk", "l5wv", "l7wq", "l7wk", "l7wv",
               "l7wo"):
        cch[nm] = nc.dram_tensor(nm + "B", [1024, 1024], BF16,
                                 kind="Internal").ap()
    cch["a2"] = nc.dram_tensor("a2B", [16 * 128, 3968], BF16,
                               kind="Internal").ap()
    cch["dc"] = nc.dram_tensor("dcB", [16 * 128, 3968], BF16,
                               kind="Internal").ap()
    cch["pkv"] = nc.dram_tensor("pkvB", [128, 1024], BF16,
                                kind="Internal").ap()
    cch["fc1"] = nc.dram_tensor("fc1B", [32 * 128, 1024], BF16,
                                kind="Internal").ap()
    cch["fc2"] = nc.dram_tensor("fc2B", [8 * 128, 4096], BF16,
                                kind="Internal").ap()
    for nm, shp in [("w1", (32 * 128, 1024)), ("w2", (32 * 128, 1024)),
                    ("h1", (32 * 128, 4096)), ("h2", (32 * 128, 4096)),
                    ("w3", (8 * 128, 4096))]:
        cch[nm] = nc.dram_tensor(nm + "B", list(shp), BF16,
                                 kind="Internal").ap()

    ctx = Ctx(b, ntmp=2, with_prep_b=False)
    alloc_fused(ctx)
    ctx.init_const_tiles(idn)
    ctx.load_consts({nm: (sc[nm].ap()[:, :], list(sc[nm].shape), F32)
                     for nm in sc})
    b.wait("vector", "ld", b.val("ld"))
    t_ = b.sbuf("c_l2_b2h", (128, 8), F32)
    ctx.consts["l2_b2h"] = t_
    b.emit("vector", lambda E, d=t_, s=ctx.consts["l2_b2"]:
           E.tensor_scalar_mul(d[:], s[:], 0.5), inc="v")
    for eng in ("scalar", "tensor"):
        b.wait(eng, "v", b.val("v"))

    for nm in ("l1wq", "l1wk", "l1wv", "l1wo", "l3wq", "l3wk", "l3wv",
               "l3wo", "l5wq", "l5wk", "l5wv", "l7wq", "l7wk", "l7wv",
               "l7wo"):
        pd = prep_cast(ctx, ext[nm].ap().rearrange("m p d -> (m p) d"),
                       cch[nm], 1024, 1024)
    pd = prep_cast(ctx, ext["a2f"].ap()[:, :], cch["a2"], 16 * 128, 3968)
    pd = prep_cast(ctx, ext["dcf"].ap()[:, :], cch["dc"], 16 * 128, 3968)
    pd = prep_cast(ctx, ext["pkvf"].ap()[:, :], cch["pkv"], 128, 1024)
    for nm, shp in [("w1", (32, 128, 1024)), ("w2", (32, 128, 1024)),
                    ("h1", (32, 128, 4096)), ("h2", (32, 128, 4096)),
                    ("w3", (8, 128, 4096))]:
        pd = prep_cast(ctx, ext[nm].ap().rearrange("m p d -> (m p) d"),
                       cch[nm], shp[0] * 128, shp[2])
    prep_dequant(ctx, ext["fc1q"].ap().rearrange("m p d -> (m p) d"),
                 ext["fc1s"].ap().rearrange("m p d -> (m p) d"), cch["fc1"],
                 32 * 128, 1024)
    pd = prep_dequant(ctx, ext["fc2q"].ap().rearrange("m p d -> (m p) d"),
                      ext["fc2s"].ap().rearrange("m p d -> (m p) d"),
                      cch["fc2"], 8 * 128, 4096)
    b.wait("sync", "pst", pd)

    Wl1 = {"wq": cch["l1wq"], "wk": cch["l1wk"], "wv": cch["l1wv"],
           "wo": cch["l1wo"]}
    Wl2 = {"fc1": cch["fc1"], "fc2": cch["fc2"]}
    Wl3 = {"wq": cch["l3wq"], "wk": cch["l3wk"], "wv": cch["l3wv"],
           "wo": cch["l3wo"]}
    Wl5 = {"wq": cch["l5wq"], "wk": cch["l5wk"], "wv": cch["l5wv"],
           "pkv": cch["pkv"]}
    Wl6 = {nm: cch[nm] for nm in ("w1", "w2", "h1", "h2", "w3")}
    Wl7 = {"wq": cch["l7wq"], "wk": cch["l7wk"], "wv": cch["l7wv"],
           "wo": cch["l7wo"]}

    set_ctx_chunks(ctx, "tabs")
    ln_st = ln_phase(ctx, x_in, xn, u_dram=U,
                     aff=dict(gamma="l1_g", beta="l1_b", eps2=1e-6))
    l0attna_run(ctx, x_in, xn, U, taps[0], Wl1, ln_st, cch["a2"])
    sublayer_boundary(ctx)

    ln_st = ln_phase(ctx, taps[0], xn, u_dram=U,
                     aff=dict(gamma="l2_g", beta="l2_b", eps2=1e-5))
    l0ffn(ctx, taps[0], xn, U, taps[1], Wl2, ln_st)
    sublayer_boundary(ctx)

    set_ctx_chunks(ctx, "lo")
    ln_st = ln_phase(ctx, taps[1], xn)
    l0attnb_run(ctx, taps[1], xn, taps[2], Wl3, ln_st)
    sublayer_boundary(ctx)

    ln_st = ln_phase(ctx, taps[2], xn, u_dram=U,
                     aff=dict(gamma="l2_g", beta="l2_b", eps2=1e-5))
    l0ffn(ctx, taps[2], xn, U, taps[3], Wl2, ln_st)
    sublayer_boundary(ctx)

    ln_st = ln_phase(ctx, taps[3], xn)
    l1aa_run(ctx, taps[3], xn, taps[4], Wl5, ln_st, cch["dc"], rscr)
    sublayer_boundary(ctx)

    ln_st = ln_phase(ctx, taps[4], xn)
    l1ffn(ctx, taps[4], xn, taps[5], Wl6, ln_st)
    sublayer_boundary(ctx)

    set_ctx_chunks(ctx, "kf")
    ln_st = ln_phase(ctx, taps[5], xn)
    l1ab_run(ctx, taps[5], xn, taps[6], Wl7, ln_st)
    sublayer_boundary(ctx)

    ln_st = ln_phase(ctx, taps[6], xn)
    l1ffn(ctx, taps[6], xn, y_out, Wl6, ln_st)
    sublayer_boundary(ctx)
    b.finish()
    return nc




_prog_cache = {}


def _get_prog():
    if "fused" not in _prog_cache:
        nc = bass.Bass("TRN2", target_bir_lowering=False)
        _prog_cache["fused"] = build_fused(nc)
    return _prog_cache["fused"]


def kernel(**inputs):
    x = inputs["L0_attn_a_hidden_states"]
    idn = np.eye(128, dtype=np.float32)
    xs = [np.ascontiguousarray(x[2 * c:2 * c + 2].reshape(TOK, D).T)
          for c in range(8)]

    past_kv = inputs["L1_attn_a_past_kv"]
    pkvf = np.zeros((128, 1024), np.float32)
    for h in range(16):
        pkvf[0:64, h * 64:(h + 1) * 64] = past_kv[h]
        pkvf[64:128, h * 64:(h + 1) * 64] = past_kv[h]
    gn_w, gn_b = inputs["L1_attn_a_gn_weight"], inputs["L1_attn_a_gn_bias"]
    gnc = np.zeros((128, 8), np.float32)
    gnbc = np.zeros((128, 8), np.float32)
    for c in range(8):
        gnc[0:64, c] = gn_w[2 * c]
        gnc[64:128, c] = gn_w[2 * c + 1]
        gnbc[0:64, c] = gn_b[2 * c]
        gnbc[64:128, c] = gn_b[2 * c + 1]
    Wqkv3 = inputs["L0_attn_b_W_qkv"]
    bq3 = inputs["L0_attn_b_b_qkv"]
    Wqkv5 = inputs["L1_attn_a_W_qkv"]
    Wqkv7 = inputs["L1_attn_b_W_qkv"]
    win = {
        "idn": idn,
        "l1wqP": lhsT_pack(inputs["L0_attn_a_Wq"]),
        "l1wkP": lhsT_pack(inputs["L0_attn_a_Wk"]),
        "l1wvP": lhsT_pack(inputs["L0_attn_a_Wv"]),
        "l1woP": lhsT_pack(inputs["L0_attn_a_Wo"]),
        "a2f": build_a2(inputs["L0_attn_a_rel_bias"]),
        "l1_g": col_pack(inputs["L0_attn_a_ln_g"]),
        "l1_b": col_pack(inputs["L0_attn_a_ln_b"]),
        "fc1q": lhsT_pack(inputs["L0_ffn_fc1_q"]),
        "fc2q": lhsT_pack(inputs["L0_ffn_fc2_q"]),
        "fc1s": lhsT_pack(np.repeat(inputs["L0_ffn_fc1_scales"], 128, axis=0)),
        "fc2s": lhsT_pack(np.repeat(inputs["L0_ffn_fc2_scales"], 128, axis=0)),
        "l2_g": col_pack(inputs["L0_ffn_ln_gamma"]),
        "l2_b": col_pack(inputs["L0_ffn_ln_beta"]),
        "l2_b1": col_pack(inputs["L0_ffn_fc1_bias"]),
        "l2_b2": col_pack(inputs["L0_ffn_fc2_bias"]),
        "l3wqP": lhsT_pack(Wqkv3[:, 0:1024]),
        "l3wkP": lhsT_pack(Wqkv3[:, 1024:2048]),
        "l3wvP": lhsT_pack(Wqkv3[:, 2048:3072]),
        "l3woP": lhsT_pack(inputs["L0_attn_b_W_proj"]),
        "l3_bqkv": np.concatenate([col_pack(bq3[0:1024]),
                                   col_pack(bq3[1024:2048]),
                                   col_pack(bq3[2048:3072])], axis=1),
        "l3_qkn": np.concatenate([inputs["L0_attn_b_qn_w"],
                                  inputs["L0_attn_b_kn_w"]]).reshape(128, 1),
        "l3_bproj": col_pack(inputs["L0_attn_b_b_proj"]),
        "l5wqP": lhsT_pack(Wqkv5[:, 0:1024]),
        "l5wkP": lhsT_pack(Wqkv5[:, 1024:2048]),
        "l5wvP": lhsT_pack(Wqkv5[:, 2048:3072]),
        "dcf": build_decay(), "pkvf": pkvf,
        "l5_gnc": gnc, "l5_gnbc": gnbc,
        "w1P": lhsT_pack(inputs["L1_ffn_w1_w"].T),
        "w2P": lhsT_pack(inputs["L1_ffn_w2_w"].T),
        "h1P": lhsT_pack(inputs["L1_ffn_h1_w"].T),
        "h2P": lhsT_pack(inputs["L1_ffn_h2_w"].T),
        "w3P": lhsT_pack(inputs["L1_ffn_w3_w"].T),
        "l6_s1": col_pack(inputs["L1_ffn_w1_s"]),
        "l6_s2": col_pack(inputs["L1_ffn_w2_s"]),
        "l6_s1h": col_pack(inputs["L1_ffn_h1_s"]),
        "l6_s2h": col_pack(inputs["L1_ffn_h2_s"]),
        "l6_s3": col_pack(inputs["L1_ffn_w3_s"]),
        "l7wqP": lhsT_pack(Wqkv7[0:1024].T),
        "l7wkP": lhsT_pack(Wqkv7[1024:2048].T),
        "l7wvP": lhsT_pack(Wqkv7[2048:3072].T),
        "l7woP": lhsT_pack(inputs["L1_attn_b_W_out"].T),
        "l7_bout": col_pack(inputs["L1_attn_b_b_out"]),
    }
    win = {k: np.ascontiguousarray(v) for k, v in win.items()}

    prog = _get_prog()
    inms = [dict(win, x=xs[c]) for c in range(8)]
    res = run_bass_kernel_spmd(prog, inms, core_ids=list(range(8)))
    out = np.empty((16, 2048, 1024), np.float32)
    for c in range(8):
        out[2 * c:2 * c + 2] = np.asarray(res.results[c]["y"]).T.reshape(2, SEQ, D)
    return out


# version 03 (final)
# speedup vs baseline: 1.3515x; 1.2010x over previous
import numpy as np
import contextlib
import concourse.bass as bass
import concourse.mybir as mybir

F32 = mybir.dt.float32
BF16 = mybir.dt.bfloat16
I32 = mybir.dt.int32
AF = mybir.ActivationFunctionType
ALU = mybir.AluOpType
AX = mybir.AxisListType

ENGINES = ("sync", "vector", "scalar", "tensor", "gpsimd")

D = 1024
NCH = 8
SEQ = 2048
NSEQ = 2
TOK = SEQ * NSEQ
NH = 16
DK = 64
TB = 512
NB = TOK // TB



def lhsT_pack(W):
    din, dout = W.shape
    k, m = din // 128, dout // 128
    A = W.reshape(k, 128, m, 128)
    A = np.ascontiguousarray(A.transpose(2, 1, 0, 3))
    return A.reshape(m, 128, din)


def col_pack(v):
    n = v.shape[0]
    return np.ascontiguousarray(v.reshape(n // 128, 128).T)



class Builder:
    def __init__(self, nc, name="prog"):
        self.nc = nc
        self.name = name
        self.ops = {e: [] for e in ENGINES}
        self.sem_handles = {}
        self.sem_cnt = {}
        self.es = contextlib.ExitStack()

    def sem(self, name):
        if name not in self.sem_handles:
            self.sem_handles[name] = self.es.enter_context(self.nc.semaphore(name))
            self.sem_cnt[name] = 0
        return self.sem_handles[name]

    def bump(self, name, n):
        self.sem(name)
        self.sem_cnt[name] += n
        assert self.sem_cnt[name] < 2000000000, f"sem {name} overflow"
        return self.sem_cnt[name]

    def val(self, name):
        self.sem(name)
        return self.sem_cnt[name]

    def emit(self, eng, maker, inc=None, n=1):
        if inc is not None:
            v = self.bump(inc, n)
            h = self.sem_handles[inc]
            def run(E, maker=maker, h=h, n=n):
                maker(E).then_inc(h, n)
            self.ops[eng].append(run)
            return v
        self.ops[eng].append(lambda E, maker=maker: maker(E))
        return None

    def wait(self, eng, sem, v=None):
        if v is None:
            v = self.val(sem)
        if v <= 0:
            return
        h = self.sem(sem)
        self.ops[eng].append(lambda E, h=h, v=v: E.wait_ge(h, v))

    def drain(self, eng):
        self.ops[eng].append(lambda E: E.drain())

    def grace(self, eng="tensor", cyc=3000):
        self.ops[eng].append(lambda E: E.nop(cycle_cnt=cyc))

    def sbuf(self, name, shape, dtype):
        return self.es.enter_context(self.nc.sbuf_tensor(name, list(shape), dtype))

    def psum(self, name, shape=(128, 512), dtype=F32):
        return self.es.enter_context(self.nc.psum_tensor(name, list(shape), dtype))

    def finish(self):
        block = self.es.enter_context(self.nc.Block())
        for e in ENGINES:
            ops = self.ops[e]
            getattr(block, e)(lambda E, ops=ops: [f(E) for f in ops])
        self.es.close()
        return self.nc


class Guard:
    def __init__(self, b):
        self.b = b
        self.d = {}

    def wait(self, eng, *key):
        e = self.d.get(key)
        if e:
            self.b.wait(eng, e[0], e[1])

    def set(self, key, sem, val):
        self.d[key] = (sem, val)



class Ctx:
    def __init__(self, b: Builder, ntmp=4, with_prep_b=True):
        self.b = b
        self.g = Guard(b)
        self.NTMP = ntmp
        self.ident = b.sbuf("ident", (128, 128), BF16)
        self.identf = b.sbuf("identf", (128, 128), F32)
        self.onesb = b.sbuf("onesb", (128, 1), BF16)
        self.onesrow = b.sbuf("onesrow", (1, 128), F32)
        self.PSALL = b.psum("PSALL", (128, 4096))
        self.P = [self.PSALL[:, i * 512:(i + 1) * 512] for i in range(8)]
        self.xf = [[b.sbuf(f"xf{s}_{c}", (128, TB), F32) for c in range(NCH)]
                   for s in range(2)]
        self.xb = [b.sbuf(f"xbc{c}", (128, TB), BF16) for c in range(NCH)]
        self.x2 = [b.sbuf(f"x2_{i}", (128, TB), BF16) for i in range(2)]
        self.tmp = [b.sbuf(f"tmp{c}", (128, TB), F32) for c in range(ntmp)]
        self.rows = b.sbuf("rows", (1, 4 * TB), F32)
        self.dmy = b.sbuf("dmy", (128, 64), BF16)
        if with_prep_b:
            self.prep_f = [b.sbuf(f"prf{i}", (128, 512), F32)
                           for i in range(2)]
            self.prep_b = [b.sbuf(f"prb{i}", (128, 512), BF16)
                           for i in range(2)]
        self.consts = {}

    def alloc_xnq(self, nsets=2):
        b = self.b
        self.ffn_xnq = [[b.sbuf(f"fxn{s}_{c}", (128, TB), BF16) for c in range(NCH)]
                        for s in range(nsets)]

    def alloc_ffn(self):
        b = self.b
        self.alloc_xnq()
        self.ffn_hA = [b.sbuf(f"fhA{m}", (128, TB), BF16) for m in range(32)]
        self.ffn_hB = [b.sbuf(f"fhB{m}", (128, TB), BF16) for m in range(32)]
        self.ffn_wbuf = [b.sbuf(f"fw{i}", (128, 1024), BF16) for i in range(8)]
        self.ffn_hwbuf = [b.sbuf(f"fhw{i}", (128, 4096), BF16) for i in range(4)]
        self.ffn_t12 = [b.sbuf(f"ft12_{i}", (128, TB), F32) for i in range(2)]
        self.ffn_t1b = [b.sbuf(f"ft1b_{i}", (128, TB), BF16) for i in range(2)]

    def load_consts(self, spec):
        b = self.b
        for name, (src, shape, dtype) in spec.items():
            t = b.sbuf("c_" + name, shape, dtype)
            self.consts[name] = t
            b.emit("sync", lambda E, t=t, src=src:
                   E.dma_start(out=t[:], in_=src), inc="ld", n=16)

    def init_const_tiles(self, idn_ap):
        b = self.b
        b.emit("sync", lambda E, s=idn_ap, t=self.identf:
               E.dma_start(out=t[:], in_=s[:, :]), inc="ld", n=16)
        b.wait("vector", "ld", b.val("ld"))
        b.emit("vector", lambda E, t=self.ident, s=self.identf:
               E.tensor_copy(t[:], s[:]))
        b.emit("vector", lambda E, t=self.onesb: E.memset(t[:], 1.0))
        b.emit("vector", lambda E, t=self.onesrow: E.memset(t[:], 1.0), inc="v")
        b.drain("vector")


def dummy_loads(ctx, src2d, n=2):
    b = ctx.b
    for _ in range(n):
        b.emit("sync", lambda E, d=ctx.dmy, s=src2d:
               E.dma_start(out=d[:, 0:64], in_=s[0:128, 0:64]), inc="ld", n=16)
    return b.val("ld")


def use_bank(ctx, i):
    ctx.g.wait("tensor", "bank", i)


def free_bank(ctx, i, sem, val):
    ctx.g.set(("bank", i), sem, val)


BG_SEMS = frozenset(("pld", "pv", "pst"))


def barrier(ctx):
    b = ctx.b
    for eng in ENGINES:
        for s in list(b.sem_cnt.keys()):
            if s in BG_SEMS:
                continue
            b.wait(eng, s, b.val(s))


def sublayer_boundary(ctx):
    barrier(ctx)
    ctx.g.d.clear()
    ctx.b.grace()



def ln_phase(ctx, x_dram, xn_dram, u_dram=None, aff=None):
    b = ctx.b
    g = ctx.g
    EPS1 = 1e-5
    st_after = {}
    for j in range(NB):
        buf = ctx.xf[j % 2]
        for c in range(NCH):
            g.wait("sync", "xf", j % 2, c)
            b.emit("sync", lambda E, d=buf[c], s_=x_dram, c=c, j=j:
                   E.dma_start(out=d[:], in_=s_[c * 128:(c + 1) * 128,
                                               j * TB:(j + 1) * TB]),
                   inc="ld", n=16)
        ld_done = b.val("ld")
        b.wait("vector", "ld", ld_done)
        cast_done = []
        for c in range(NCH):
            g.wait("vector", "xb", c)
            cast_done.append(b.emit("vector", lambda E, d=ctx.xb[c], s_=buf[c]:
                                    E.tensor_copy(d[:], s_[:]), inc="v"))
        use_bank(ctx, 0)
        use_bank(ctx, 1)
        for c in range(NCH):
            b.wait("scalar", "v", cast_done[c])
            g.wait("scalar", "x2", c % 2)
            sq = b.emit("scalar", lambda E, d=ctx.x2[c % 2], s_=ctx.xb[c]:
                        E.activation(d[:], s_[:], AF.Square), inc="s")
            b.wait("tensor", "v", cast_done[c])
            b.emit("tensor", lambda E, p=ctx.P[0], s_=ctx.xb[c], f=(c == 0),
                   l=(c == NCH - 1), o=ctx.onesb:
                   E.matmul(p[0:1, :], o[:], s_[:], start=f, stop=l))
            b.wait("tensor", "s", sq)
            tv = b.emit("tensor", lambda E, p=ctx.P[1], s_=ctx.x2[c % 2], f=(c == 0),
                        l=(c == NCH - 1), o=ctx.onesb:
                        E.matmul(p[0:1, :], o[:], s_[:], start=f, stop=l),
                        inc="t")
            g.set(("x2", c % 2), "t", tv)
        sumx_t = b.val("t")
        sumsq_t = b.val("t")
        R = ctx.rows
        mu = R[0:1, 0:TB]
        s2r = R[0:1, TB:2 * TB]
        rst = R[0:1, 2 * TB:3 * TB]
        w1 = R[0:1, 3 * TB:4 * TB]
        b.wait("vector", "t", sumsq_t)
        b.emit("vector", lambda E, mu=mu, p=ctx.P[0]:
               E.tensor_scalar_mul(mu[:, :], p[0:1, :], 1.0 / D))
        b.drain("vector")
        b.emit("vector", lambda E, s2r=s2r, mu=mu:
               E.tensor_tensor(s2r[:, :], mu[:, :], mu[:, :], op=ALU.mult))
        b.drain("vector")
        b.emit("vector", lambda E, w1=w1, p=ctx.P[1], s2r=s2r:
               E.scalar_tensor_tensor(w1[:, :], p[0:1, :], 1.0 / D, s2r[:, :],
                                      op0=ALU.mult, op1=ALU.subtract))
        b.drain("vector")
        vrow = b.emit("vector", lambda E, rst=rst, w1=w1:
                      E.tensor_scalar_add(rst[:, :], w1[:, :], EPS1), inc="v")
        free_bank(ctx, 0, "v", vrow)
        free_bank(ctx, 1, "v", vrow)
        if aff is not None:
            b.wait("vector", "v", vrow)
            g.wait("vector", "tmp", 0)
            t0row = ctx.tmp[0][0:1, 0:TB]
            b.emit("vector", lambda E, t0=t0row, rst=rst:
                   E.reciprocal(t0[:, :], rst[:, :]))
            b.drain("vector")
            b.emit("vector", lambda E, s2r=s2r, w1=w1, t0=t0row:
                   E.tensor_tensor(s2r[:, :], w1[:, :], t0[:, :],
                                   op=ALU.mult))
            b.drain("vector")
            v2 = b.emit("vector", lambda E, s2r=s2r, e=aff["eps2"]:
                        E.tensor_scalar_add(s2r[:, :], s2r[:, :], e), inc="v")
        b.wait("scalar", "v", b.val("v"))
        b.emit("scalar", lambda E, rst=rst:
               E.activation(rst[:, :], rst[:, :], AF.Ln))
        b.drain("scalar")
        rst_s = b.emit("scalar", lambda E, rst=rst:
                       E.activation(rst[:, :], rst[:, :], AF.Exp, scale=-0.5),
                       inc="s")
        if aff is not None:
            b.emit("scalar", lambda E, s2r=s2r:
                   E.activation(s2r[:, :], s2r[:, :], AF.Ln))
            b.drain("scalar")
            s2s = b.emit("scalar", lambda E, s2r=s2r:
                         E.activation(s2r[:, :], s2r[:, :], AF.Exp, scale=-0.5),
                         inc="s")
            b.wait("vector", "s", s2s)
            rows_v = b.emit("vector", lambda E, s2r=s2r, rst=rst:
                            E.tensor_tensor(s2r[:, :], s2r[:, :], rst[:, :],
                                            op=ALU.mult), inc="v")
            rows_sem, rows_val = "v", rows_v
        else:
            rows_sem, rows_val = "s", rst_s
        b.wait("tensor", rows_sem, rows_val)
        nbk = 3 if u_dram is not None else 2
        rowsrc = [mu, (s2r if aff is not None else rst), rst]
        for bi in range(nbk):
            use_bank(ctx, 2 + bi)
        for bi in range(nbk):
            b.emit("tensor", lambda E, p=ctx.P[2 + bi], o=ctx.onesrow,
                   r=rowsrc[bi]:
                   E.matmul(p[:, :], o[:, :], r[:, :], start=True, stop=True),
                   inc=("t" if bi == nbk - 1 else None))
        bc_done = b.val("t")
        b.wait("vector", "t", bc_done)
        if aff is not None:
            b.wait("scalar", "t", bc_done)
        for c in range(NCH):
            g.wait("vector", "tmp", c % ctx.NTMP)
            b.emit("vector", lambda E, d=ctx.tmp[c % ctx.NTMP], x=buf[c], p=ctx.P[2]:
                   E.tensor_tensor(d[:], x[:], p[:, :], op=ALU.subtract))
            b.drain("vector")
            if u_dram is not None:
                b.emit("vector", lambda E, d=buf[c], tt=ctx.tmp[c % ctx.NTMP], p=ctx.P[4]:
                       E.tensor_tensor(d[:], tt[:], p[:, :], op=ALU.mult),
                       inc="v")
                b.drain("vector")
            if aff is not None:
                gam = ctx.consts[aff["gamma"]]
                bet = ctx.consts[aff["beta"]]
                b.emit("vector", lambda E, d=ctx.tmp[c % ctx.NTMP], p=ctx.P[3]:
                       E.tensor_tensor(d[:], d[:], p[:, :], op=ALU.mult))
                b.drain("vector")
                sv = b.emit("vector", lambda E, d=ctx.xb[c], tt=ctx.tmp[c % ctx.NTMP],
                            gam=gam, bet=bet, c=c:
                            E.tensor_scalar(d[:], tt[:], gam[:, c:c + 1],
                                            bet[:, c:c + 1], op0=ALU.mult,
                                            op1=ALU.add), inc="v")
                g.set(("tmp", c % ctx.NTMP), "v", sv)
            else:
                av = b.emit("vector", lambda E, d=ctx.xb[c], tt=ctx.tmp[c % ctx.NTMP],
                            p=ctx.P[3]:
                            E.tensor_tensor(d[:], tt[:], p[:, :], op=ALU.mult),
                            inc="v")
                g.set(("tmp", c % ctx.NTMP), "v", av)
        apply_v, apply_s = b.val("v"), b.val("s")
        free_bank(ctx, 2, "v", apply_v)
        if aff is not None:
            free_bank(ctx, 3, "v", apply_v)
        else:
            free_bank(ctx, 3, "v", apply_v)
        if u_dram is not None:
            free_bank(ctx, 4, "v", apply_v)
        b.wait("scalar", "v", apply_v)
        if aff is not None:
            b.wait("scalar", "s", apply_s)
        for c in range(NCH):
            b.emit("scalar", lambda E, s_=ctx.xb[c], d=xn_dram, c=c, j=j:
                   E.dma_start(out=d[c * 128:(c + 1) * 128, j * TB:(j + 1) * TB],
                               in_=s_[:]), inc="st", n=16)
            if u_dram is not None:
                b.emit("scalar", lambda E, s_=buf[c], d=u_dram, c=c, j=j:
                       E.dma_start(out=d[c * 128:(c + 1) * 128,
                                        j * TB:(j + 1) * TB], in_=s_[:]),
                       inc="st", n=16)
        st_after[j] = b.val("st")
        for c in range(NCH):
            g.set(("xb", c), "st", st_after[j])
            g.set(("xf", j % 2, c), "st", st_after[j])
    return st_after



PREP_CW = 256


def prep_cast(ctx, src2d, dst2d, nrows, ncols, scale=None):
    b = ctx.b
    g = getattr(ctx, "gp", ctx.g)
    cw = getattr(ctx, "prep_cw", 512)
    nt = nrows // 128
    ns = (ncols + cw - 1) // cw
    for i in range(nt):
        for cs in range(ns):
            c0 = cs * cw
            n = min(cw, ncols - c0)
            p = (i * ns + cs) % 2
            g.wait("gpsimd", "prep", p)
            b.emit("gpsimd", lambda E, d=ctx.prep_f[p], s=src2d, i=i, c0=c0, n=n:
                   E.dma_start(out=d[:, 0:n],
                               in_=s[i * 128:(i + 1) * 128, c0:c0 + n]),
                   inc="pld", n=16)
            b.wait("gpsimd", "pld", b.val("pld"))
            cv = b.emit("gpsimd", lambda E, d=ctx.prep_b[p], s=ctx.prep_f[p], n=n:
                        E.tensor_copy(d[:, 0:n], s[:, 0:n]), inc="pv")
            b.wait("gpsimd", "pv", cv)
            b.emit("gpsimd", lambda E, s=ctx.prep_b[p], d=dst2d, i=i, c0=c0, n=n:
                   E.dma_start(out=d[i * 128:(i + 1) * 128, c0:c0 + n],
                               in_=s[:, 0:n]),
                   inc="pst", n=16)
            g.set(("prep", p), "pst", b.val("pst"))
    return b.val("pst")



def _hid_mm_phase(ctx, Wdram, hin, hout, sc, in_ready, nkin, act=AF.Gelu):
    b = ctx.b
    g = ctx.g
    hwbuf = ctx.ffn_hwbuf
    wlds = {}
    def _ld(m):
        p = m % 4
        g.wait("sync", "hw", p)
        b.emit("sync", lambda E, d=hwbuf[p], s=Wdram, m=m, n=nkin * 128:
               E.dma_start(out=d[:, 0:n], in_=s[m * 128:(m + 1) * 128, :]),
               inc="ld", n=16)
        wlds[m] = b.val("ld")
    for m in range(3):
        _ld(m)
    for m in range(32):
        p = m % 4
        if m + 3 < 32:
            _ld(m + 3)
        elif m + 3 == 32:
            wlds[32] = dummy_loads(ctx, Wdram, 1)
            wlds[33] = dummy_loads(ctx, Wdram, 1)
        b.wait("tensor", "ld", wlds[min(m + 2, 33)])
        if m == 0:
            b.grace()
            b.wait("tensor", in_ready[0], in_ready[1])
        bank = 4 + m % 2
        use_bank(ctx, bank)
        for k in range(nkin):
            b.emit("tensor", lambda E, P=ctx.P[bank], w=hwbuf[p], x=hin[k], k=k:
                   E.matmul(P[:, :], w[:, k * 128:(k + 1) * 128], x[:],
                            start=(k == 0), stop=(k == nkin - 1)),
                   inc=("t" if k == nkin - 1 else None))
        mmv = b.val("t")
        g.set(("hw", p), "t", mmv)
        if m == 0:
            g.wait("scalar", "hfree", id(hout))
            g.wait("scalar", "hfreeS", id(hout))
        b.wait("scalar", "t", mmv)
        gv = b.emit("scalar", lambda E, d=hout[m], P=ctx.P[bank], sc=sc, m=m:
                    E.activation(d[:], P[:, :], act, scale=sc[:, m:m + 1]),
                    inc="s")
        free_bank(ctx, bank, "s", gv)
    g.set(("hfree", id(hin)), "t", b.val("t"))
    return ("s", b.val("s"))


def l1ffn(ctx, x_dram, xn_dram, xout_dram, W, ln_st):
    b = ctx.b
    g = ctx.g
    s1 = ctx.consts["l6_s1"]; s2 = ctx.consts["l6_s2"]
    s1h = ctx.consts["l6_s1h"]; s2h = ctx.consts["l6_s2h"]
    s3 = ctx.consts["l6_s3"]
    wbuf, hwbuf = ctx.ffn_wbuf, ctx.ffn_hwbuf
    t12, t1b = ctx.ffn_t12, ctx.ffn_t1b
    hA, hB = ctx.ffn_hA, ctx.ffn_hB
    for j in range(NB):
        xnq = ctx.ffn_xnq[j % len(ctx.ffn_xnq)]
        b.wait("sync", "st", ln_st[j])
        for c in range(NCH):
            g.wait("sync", "xnq", j % len(ctx.ffn_xnq), c)
            b.emit("sync", lambda E, d=xnq[c], s=xn_dram, c=c, j=j:
                   E.dma_start(out=d[:], in_=s[c * 128:(c + 1) * 128,
                                              j * TB:(j + 1) * TB]),
                   inc="ld", n=16)
        wlds = {}
        def _ld_w12(m):
            p = m % 4
            g.wait("sync", "wb", p)
            b.emit("sync", lambda E, d=wbuf[2 * p], s=W["w1"], m=m:
                   E.dma_start(out=d[:], in_=s[m * 128:(m + 1) * 128, :]),
                   inc="ld", n=16)
            b.emit("sync", lambda E, d=wbuf[2 * p + 1], s=W["w2"], m=m:
                   E.dma_start(out=d[:], in_=s[m * 128:(m + 1) * 128, :]),
                   inc="ld", n=16)
            wlds[m] = b.val("ld")
        for m in range(3):
            _ld_w12(m)
        first_write = True
        for m in range(32):
            p = m % 4
            if m + 3 < 32:
                _ld_w12(m + 3)
            elif m + 3 == 32:
                wlds[32] = dummy_loads(ctx, W["w1"], 1)
                wlds[33] = dummy_loads(ctx, W["w1"], 1)
            b.wait("tensor", "ld", wlds[min(m + 2, 33)])
            if m == 0:
                b.grace()
            bank1, bank2 = m % 2, 2 + m % 2
            use_bank(ctx, bank1)
            for k in range(NCH):
                b.emit("tensor", lambda E, P=ctx.P[bank1], w=wbuf[2 * p],
                       x=xnq[k], k=k:
                       E.matmul(P[:, :], w[:, k * 128:(k + 1) * 128], x[:],
                                start=(k == 0), stop=(k == NCH - 1)),
                       inc=("t" if k == NCH - 1 else None))
            mm1 = b.val("t")
            use_bank(ctx, bank2)
            for k in range(NCH):
                b.emit("tensor", lambda E, P=ctx.P[bank2], w=wbuf[2 * p + 1],
                       x=xnq[k], k=k:
                       E.matmul(P[:, :], w[:, k * 128:(k + 1) * 128], x[:],
                                start=(k == 0), stop=(k == NCH - 1)),
                       inc=("t" if k == NCH - 1 else None))
            mm2 = b.val("t")
            g.set(("wb", p), "t", mm2)
            b.wait("scalar", "t", mm1)
            g.wait("scalar", "t1b", m % 2)
            if first_write:
                g.wait("scalar", "hfree", id(hA))
                g.wait("scalar", "hfreeS", id(hA))
            sv = b.emit("scalar", lambda E, d=t1b[m % 2], P=ctx.P[bank1],
                        s1=s1, m=m:
                        E.activation(d[:], P[:, :], AF.Silu,
                                     scale=s1[:, m:m + 1]), inc="s")
            free_bank(ctx, bank1, "s", sv)
            b.wait("vector", "t", mm2)
            if first_write:
                g.wait("vector", "hfree", id(hA))
                g.wait("vector", "hfreeS", id(hA))
                first_write = False
            b.emit("vector", lambda E, d=t12[m % 2], P=ctx.P[bank2], s2=s2, m=m:
                   E.tensor_scalar_mul(d[:], P[:, :], s2[:, m:m + 1]))
            b.drain("vector")
            b.wait("vector", "s", sv)
            hv = b.emit("vector", lambda E, d=hA[m], a=t1b[m % 2], c_=t12[m % 2]:
                        E.tensor_tensor(d[:], a[:], c_[:], op=ALU.mult),
                        inc="v")
            free_bank(ctx, bank2, "v", hv)
            g.set(("t1b", m % 2), "v", hv)
        for c in range(NCH):
            g.set(("xnq", j % len(ctx.ffn_xnq), c), "t", b.val("t"))
        r1 = _hid_mm_phase(ctx, W["h1"], hA, hB, s1h, ("v", b.val("v")), 32)
        r2 = _hid_mm_phase(ctx, W["h2"], hB, hA, s2h, r1, 32)
        xbuf = ctx.xf[j % 2]
        for c in range(NCH):
            g.wait("sync", "xf", j % 2, c)
            b.emit("sync", lambda E, d=xbuf[c], s=x_dram, c=c, j=j:
                   E.dma_start(out=d[:], in_=s[c * 128:(c + 1) * 128,
                                              j * TB:(j + 1) * TB]),
                   inc="ld", n=16)
        x_ld = b.val("ld")
        wlds3 = {}
        def _ld_w3(m):
            p = m % 4
            g.wait("sync", "hw", p)
            b.emit("sync", lambda E, d=hwbuf[p], s=W["w3"], m=m:
                   E.dma_start(out=d[:], in_=s[m * 128:(m + 1) * 128, :]),
                   inc="ld", n=16)
            wlds3[m] = b.val("ld")
        for m in range(3):
            _ld_w3(m)
        for m in range(NCH):
            p = m % 4
            if m + 3 < NCH:
                _ld_w3(m + 3)
            elif m + 3 == NCH:
                wlds3[NCH] = dummy_loads(ctx, W["w3"], 1)
                wlds3[NCH + 1] = dummy_loads(ctx, W["w3"], 1)
            b.wait("tensor", "ld", wlds3[min(m + 2, NCH + 1)])
            if m == 0:
                b.grace()
                b.wait("tensor", r2[0], r2[1])
            bank = 4 + m % 2
            use_bank(ctx, bank)
            for k in range(32):
                b.emit("tensor", lambda E, P=ctx.P[bank], w=hwbuf[p],
                       x=hA[k], k=k:
                       E.matmul(P[:, :], w[:, k * 128:(k + 1) * 128], x[:],
                                start=(k == 0), stop=(k == 31)),
                       inc=("t" if k == 31 else None))
            mmv = b.val("t")
            g.set(("hw", p), "t", mmv)
            b.wait("scalar", "t", mmv)
            g.wait("scalar", "tmp", m % ctx.NTMP)
            ov = b.emit("scalar", lambda E, d=ctx.tmp[m % ctx.NTMP], P=ctx.P[bank],
                        s3=s3, m=m:
                        E.activation(d[:], P[:, :], AF.Copy,
                                     scale=s3[:, m:m + 1]), inc="s")
            free_bank(ctx, bank, "s", ov)
            b.wait("vector", "s", ov)
            b.wait("vector", "ld", x_ld)
            rv = b.emit("vector", lambda E, d=xbuf[m], t=ctx.tmp[m % ctx.NTMP]:
                        E.tensor_tensor(d[:], d[:], t[:], op=ALU.add), inc="v")
            g.set(("tmp", m % ctx.NTMP), "v", rv)
            b.wait("scalar", "v", rv)
            stv = b.emit("scalar", lambda E, s=xbuf[m], d=xout_dram, m=m, j=j:
                         E.dma_start(out=d[m * 128:(m + 1) * 128,
                                          j * TB:(j + 1) * TB], in_=s[:]),
                         inc="st", n=16)
            g.set(("xf", j % 2, m), "st", stv)
        g.set(("hfree", id(hA)), "t", b.val("t"))
    return b.val("st")


def prep_dequant(ctx, q2d, scale2d, dst2d, nrows, ncols):
    b = ctx.b
    g = getattr(ctx, "gp", ctx.g)
    cw = getattr(ctx, "prep_cw", 512)
    nt = nrows // 128
    ns = (ncols + cw - 1) // cw
    for i in range(nt):
        for cs in range(ns):
            c0 = cs * cw
            n = min(cw, ncols - c0)
            p = (i * ns + cs) % 2
            g.wait("gpsimd", "prep", p)
            b.emit("gpsimd", lambda E, d=ctx.prep_q[p], s=q2d, i=i, c0=c0, n=n:
                   E.dma_start(out=d[:, 0:n],
                               in_=s[i * 128:(i + 1) * 128, c0:c0 + n]),
                   inc="pld", n=16)
            b.emit("gpsimd", lambda E, d=ctx.prep_f[p], s=scale2d, i=i, c0=c0, n=n:
                   E.dma_start(out=d[:, 0:n],
                               in_=s[i * 128:(i + 1) * 128, c0:c0 + n]),
                   inc="pld", n=16)
            b.wait("gpsimd", "pld", b.val("pld"))
            b.emit("gpsimd", lambda E, d=ctx.prep_f2[p], s=ctx.prep_q[p], n=n:
                   E.tensor_copy(d[:, 0:n], s[:, 0:n]))
            b.ops["gpsimd"].append(lambda E: E.drain())
            b.emit("gpsimd", lambda E, d=ctx.prep_f2[p], n=n:
                   E.tensor_scalar_sub(d[:, 0:n], d[:, 0:n], 128.0))
            b.ops["gpsimd"].append(lambda E: E.drain())
            cv = b.emit("gpsimd", lambda E, d=ctx.prep_b[p], t=ctx.prep_f2[p],
                        sc=ctx.prep_f[p], n=n:
                        E.tensor_mul(d[:, 0:n], t[:, 0:n], sc[:, 0:n]),
                        inc="pv")
            b.wait("gpsimd", "pv", cv)
            b.emit("gpsimd", lambda E, s=ctx.prep_b[p], d=dst2d, i=i, c0=c0, n=n:
                   E.dma_start(out=d[i * 128:(i + 1) * 128, c0:c0 + n],
                               in_=s[:, 0:n]),
                   inc="pst", n=16)
            g.set(("prep", p), "pst", b.val("pst"))
    return b.val("pst")



DEBUG_SERIALIZE = 0
DEBUG_PE_DELAY = 0

def l0ffn(ctx, x_dram, xn_dram, u_dram, xout_dram, W, ln_st):
    b = ctx.b
    g = ctx.g
    b1 = ctx.consts["l2_b1"]
    b2h = ctx.consts["l2_b2h"]
    wbuf, hwbuf = ctx.ffn_wbuf, ctx.ffn_hwbuf
    for j in range(NB):
        hA = ctx.ffn_hA if j % 2 == 0 else ctx.ffn_hB
        xnq = ctx.ffn_xnq[j % len(ctx.ffn_xnq)]
        b.wait("sync", "st", ln_st[j])
        for c in range(NCH):
            g.wait("sync", "xnq", j % len(ctx.ffn_xnq), c)
            b.emit("sync", lambda E, d=xnq[c], s=xn_dram, c=c, j=j:
                   E.dma_start(out=d[:], in_=s[c * 128:(c + 1) * 128,
                                              j * TB:(j + 1) * TB]),
                   inc="ld", n=16)
        xn_ld = b.val("ld")
        b.wait("tensor", "ld", xn_ld)
        b.grace()
        first_write = True
        wlds = {}
        def _ld_fc1(m):
            p = m % 4
            g.wait("sync", "wb", p)
            b.emit("sync", lambda E, d=wbuf[p], s=W["fc1"], m=m:
                   E.dma_start(out=d[:], in_=s[m * 128:(m + 1) * 128, :]),
                   inc="ld", n=16)
            wlds[m] = b.val("ld")
        for m in range(4):
            _ld_fc1(m)
        for m in range(32):
            p = m % 4
            b.wait("tensor", "ld", wlds.get(m + 2, 0) or wlds[m])
            bank = m % 2
            use_bank(ctx, bank)
            for k in range(NCH):
                b.emit("tensor", lambda E, P=ctx.P[bank], w=wbuf[p],
                       x=xnq[k], k=k:
                       E.matmul(P[:, :], w[:, k * 128:(k + 1) * 128], x[:],
                                start=(k == 0), stop=(k == NCH - 1)),
                       inc=("t" if k == NCH - 1 else None))
            mmv = b.val("t")
            g.set(("wb", p), "t", mmv)
            if m + 4 < 32:
                _ld_fc1(m + 4)
            elif m == 28:
                wlds[32] = dummy_loads(ctx, W["fc1"], 1)
                wlds[33] = dummy_loads(ctx, W["fc1"], 1)
            b.wait("scalar", "t", mmv)
            if first_write:
                g.wait("scalar", "hfree", id(hA))
                g.wait("scalar", "hfreeS", id(hA))
                first_write = False
            sv = b.emit("scalar", lambda E, d=hA[m], P=ctx.P[bank], b1=b1, m=m:
                        E.activation(d[:], P[:, :], AF.Silu,
                                     bias=b1[:, m:m + 1]), inc="s")
            free_bank(ctx, bank, "s", sv)
        for c in range(NCH):
            g.set(("xnq", j % len(ctx.ffn_xnq), c), "t", b.val("t"))
        h_ready = ("s", b.val("s"))
        if getattr(ctx, "dbg_h", None) is not None:
            b.wait("gpsimd", "s", h_ready[1])
            for m in range(32):
                b.emit("gpsimd", lambda E, s=hA[m], d=ctx.dbg_h, m=m, j=j:
                       E.dma_start(out=d[(j * 32 + m) * 128:(j * 32 + m + 1) * 128, :],
                                   in_=s[:]), inc="st", n=16)
            g.set(("hfreeS", id(hA)), "st", b.val("st"))
        xbuf = ctx.xf[j % 2]
        ubuf = ctx.xf[(j % 2) ^ 1]
        for c in range(NCH):
            g.wait("sync", "xf", j % 2, c)
            b.emit("sync", lambda E, d=xbuf[c], s=x_dram, c=c, j=j:
                   E.dma_start(out=d[:], in_=s[c * 128:(c + 1) * 128,
                                              j * TB:(j + 1) * TB]),
                   inc="ld", n=16)
            g.wait("sync", "xf", (j % 2) ^ 1, c)
            b.emit("sync", lambda E, d=ubuf[c], s=u_dram, c=c, j=j:
                   E.dma_start(out=d[:], in_=s[c * 128:(c + 1) * 128,
                                              j * TB:(j + 1) * TB]),
                   inc="ld", n=16)
        x_ld = b.val("ld")
        wlds2 = {}
        def _ld_fc2(m):
            p = m % 4
            g.wait("sync", "hw", p)
            b.emit("sync", lambda E, d=hwbuf[p], s=W["fc2"], m=m:
                   E.dma_start(out=d[:], in_=s[m * 128:(m + 1) * 128, :]),
                   inc="ld", n=16)
            wlds2[m] = b.val("ld")
        for m in range(3):
            _ld_fc2(m)
        for m in range(NCH):
            p = m % 4
            if m + 3 < NCH:
                _ld_fc2(m + 3)
            elif m + 3 == NCH:
                wlds2[NCH] = dummy_loads(ctx, W["fc2"], 1)
                wlds2[NCH + 1] = dummy_loads(ctx, W["fc2"], 1)
            b.wait("tensor", "ld", wlds2[min(m + 2, NCH + 1)])
            if m == 0:
                b.grace()
            if m == 0:
                b.wait("tensor", h_ready[0], h_ready[1])
            bank = 4 + m % 2
            use_bank(ctx, bank)
            for k in range(32):
                b.emit("tensor", lambda E, P=ctx.P[bank], w=hwbuf[p],
                       x=hA[k], k=k:
                       E.matmul(P[:, :], w[:, k * 128:(k + 1) * 128], x[:],
                                start=(k == 0), stop=(k == 31)),
                       inc=("t" if k == 31 else None))
            mmv = b.val("t")
            g.set(("hw", p), "t", mmv)
            b.wait("vector", "t", mmv)
            g.wait("vector", "tmp", m % ctx.NTMP)
            ov = b.emit("vector", lambda E, d=ctx.tmp[m % ctx.NTMP], P=ctx.P[bank],
                        bb=b2h, m=m:
                        E.tensor_scalar(d[:], P[:, :], 0.5, bb[:, m:m + 1],
                                        op0=ALU.mult, op1=ALU.add), inc="v")
            free_bank(ctx, bank, "v", ov)
            b.drain("vector")
            b.wait("vector", "ld", x_ld)
            b.emit("vector", lambda E, d=xbuf[m], u=ubuf[m]:
                   E.tensor_tensor(d[:], d[:], u[:], op=ALU.add))
            b.drain("vector")
            rv = b.emit("vector", lambda E, d=xbuf[m], t=ctx.tmp[m % ctx.NTMP]:
                        E.tensor_tensor(d[:], d[:], t[:], op=ALU.add), inc="v")
            g.set(("tmp", m % ctx.NTMP), "v", rv)
            b.wait("scalar", "v", rv)
            stv = b.emit("scalar", lambda E, s=xbuf[m], d=xout_dram, m=m, j=j:
                         E.dma_start(out=d[m * 128:(m + 1) * 128,
                                          j * TB:(j + 1) * TB], in_=s[:]),
                         inc="st", n=16)
            g.set(("xf", j % 2, m), "st", stv)
            g.set(("xf", (j % 2) ^ 1, m), "st", stv)
        g.set(("hfree", id(hA)), "t", b.val("t"))
    return b.val("st")


def alloc_attn(ctx, tables=False, need_ctx=True, n_aw=6):
    b = ctx.b
    ctx.at_qF = [b.sbuf(f"aqF{c}", (128, SEQ), BF16) for c in range(NCH)]
    ctx.at_kF = [b.sbuf(f"akF{c}", (128, SEQ), BF16) for c in range(NCH)]
    ctx.at_vT = [b.sbuf(f"avT{i}", (128, D), BF16) for i in range(SEQ // 128)]
    ctx.at_w = [b.sbuf(f"aw{i}", (128, 1024), BF16) for i in range(n_aw)]
    ctx.at_vw = [b.sbuf(f"avw{i}", (128, 1024), BF16) for i in range(4)]
    ctx.at_vf = ctx.prep_f[0]
    ctx.at_pkv = None
    ctx.at_rb = None
    ctx.at_q2 = ctx.x2
    ctx.at_rows = ctx.rows
    ctx.at_mq = b.sbuf("amq", (128, 32), F32)
    ctx.at_sc = b.sbuf("asc", (128, 4), F32)
    ctx.at_negM = b.sbuf("anegM", (128, NH), F32)
    ctx.at_P = ctx.x2
    if need_ctx:
        ctx.at_ctx = [[b.sbuf(f"actx{c}", (128, TB), BF16) for c in range(NCH)]] * 2
    if tables:
        ctx.at_tab = [b.sbuf(f"atab{i}", (128, 2432), BF16) for i in range(2)]
        ctx.at_bmax = b.sbuf("abmax", (1, NH), F32)


def alloc_l1aa(ctx):
    b = ctx.b
    ctx.at_pkv = b.sbuf("apkv", (128, 1024), BF16)
    ctx.at_rb = b.sbuf("arb", (128, TB), BF16)
    ctx.at_rl = [b.sbuf(f"arl{i}", (128, TB), BF16) for i in range(2)]
    ctx.at_zero = b.sbuf("azero", (128, TB), BF16)


def _attn_proj_phase(ctx, xn_dram, W, ln_st, s, rms, bias_c):
    b = ctx.b
    g = ctx.g
    qF, kF, vT, aw, vw = ctx.at_qF, ctx.at_kF, ctx.at_vT, ctx.at_w, ctx.at_vw
    EPS = 1e-6
    for jb in range(SEQ // TB):
        j = s * (SEQ // TB) + jb
        nset = len(ctx.ffn_xnq)
        xnq = ctx.ffn_xnq[j % nset]
        b.wait("sync", "st", ln_st[j])
        for c in range(NCH):
            g.wait("sync", "xnq", j % nset, c)
            b.emit("sync", lambda E, d=xnq[c], s_=xn_dram, c=c, j=j:
                   E.dma_start(out=d[:], in_=s_[c * 128:(c + 1) * 128,
                                               j * TB:(j + 1) * TB]),
                   inc="ld", n=16)
        lds = {}
        def _ldqk(c):
            pq, pk = c % 3, c % 3
            g.wait("sync", "aw", pq)
            b.emit("sync", lambda E, d=aw[pq], s_=W["wq"], c=c:
                   E.dma_start(out=d[:], in_=s_[c * 128:(c + 1) * 128, :]),
                   inc="ld", n=16)
            g.wait("sync", "avw", pk)
            b.emit("sync", lambda E, d=vw[pk], s_=W["wk"], c=c:
                   E.dma_start(out=d[:], in_=s_[c * 128:(c + 1) * 128, :]),
                   inc="ld", n=16)
            lds[c] = b.val("ld")
        vlds = {}
        def _ldv(c2):
            p = c2 % 4
            g.wait("sync", "avw", p)
            b.emit("sync", lambda E, d=vw[p], s_=W["wv"], c2=c2:
                   E.dma_start(out=d[:], in_=s_[c2 * 128:(c2 + 1) * 128, :]),
                   inc="ld", n=16)
            vlds[c2] = b.val("ld")
        for c2 in range(3):
            _ldv(c2)
        for c2 in range(NCH):
            if c2 + 3 < NCH:
                _ldv(c2 + 3)
            elif c2 + 3 == NCH:
                vlds[NCH] = dummy_loads(ctx, W["wv"], 1)
                vlds[NCH + 1] = dummy_loads(ctx, W["wv"], 1)
            b.wait("tensor", "ld", vlds[min(c2 + 2, NCH + 1)])
            if c2 == 0:
                b.grace()
            p = c2 % 4
            bank = c2 % 2
            use_bank(ctx, bank)
            for k in range(NCH):
                b.emit("tensor", lambda E, P=ctx.P[bank], w=vw[p], x=xnq[k], k=k:
                       E.matmul(P[:, :], w[:, k * 128:(k + 1) * 128], x[:],
                                start=(k == 0), stop=(k == NCH - 1)),
                       inc=("t" if k == NCH - 1 else None))
            vmm = b.val("t")
            g.set(("avw", p), "t", vmm)
            b.wait("vector", "t", vmm)
            g.wait("vector", "vfs", 0)
            g.wait("vector", "prep", 0)
            ev = b.emit("vector", lambda E, d=ctx.at_vf, P=ctx.P[bank]:
                        E.tensor_copy(d[:], P[:, :]), inc="v")
            free_bank(ctx, bank, "v", ev)
            b.wait("tensor", "v", ev)
            for ts in range(4):
                bank2 = 2 + ts % 2
                use_bank(ctx, bank2)
                tt = b.emit("tensor", lambda E, P=ctx.P[bank2], s_=ctx.at_vf,
                            idn=ctx.identf, ts=ts:
                            E.transpose(P[:, 0:128], s_[:, ts * 128:(ts + 1) * 128],
                                        idn[:]), inc="t")
                b.wait("vector", "t", tt)
                g.wait("vector", "avT", jb * 4 + ts)
                tv = b.emit("vector", lambda E, d=vT[jb * 4 + ts], P=ctx.P[bank2],
                            c2=c2, ts=ts:
                            E.tensor_copy(d[:, c2 * 128:(c2 + 1) * 128],
                                          P[:, 0:128]), inc="v")
                free_bank(ctx, bank2, "v", tv)
            g.set(("vfs", 0), "t", b.val("t"))
            g.set(("prep", 0), "t", b.val("t"))
        _ldqk(0)
        _ldqk(1)
        for c in range(NCH):
            if c + 2 < NCH:
                _ldqk(c + 2)
            else:
                lds[c + 2] = dummy_loads(ctx, W["wq"], 2)
            b.wait("tensor", "ld", lds[c + 2])
            if c == 0:
                b.grace()
            pq, pk = c % 3, c % 3
            bq, bk = 2, 3
            use_bank(ctx, bq)
            for k in range(NCH):
                b.emit("tensor", lambda E, P=ctx.P[bq], w=aw[pq], x=xnq[k], k=k:
                       E.matmul(P[:, :], w[:, k * 128:(k + 1) * 128], x[:],
                                start=(k == 0), stop=(k == NCH - 1)),
                       inc=("t" if k == NCH - 1 else None))
            mmq = b.val("t")
            g.set(("aw", pq), "t", mmq)
            use_bank(ctx, bk)
            for k in range(NCH):
                b.emit("tensor", lambda E, P=ctx.P[bk], w=vw[pk], x=xnq[k], k=k:
                       E.matmul(P[:, :], w[:, k * 128:(k + 1) * 128], x[:],
                                start=(k == 0), stop=(k == NCH - 1)),
                       inc=("t" if k == NCH - 1 else None))
            mmk = b.val("t")
            g.set(("avw", pk), "t", mmk)
            b.wait("vector", "t", mmk)
            g.wait("vector", "aqk", c)
            if bias_c is not None:
                bqkv = ctx.consts[bias_c]
                vq = b.emit("vector", lambda E, d=qF[c], P=ctx.P[bq], bb=bqkv,
                            c=c, jb=jb:
                            E.tensor_scalar_add(d[:, jb * TB:(jb + 1) * TB],
                                                P[:, :], bb[:, c:c + 1]),
                            inc="v")
                b.drain("vector")
                vk = b.emit("vector", lambda E, d=kF[c], P=ctx.P[bk], bb=bqkv,
                            c=c, jb=jb:
                            E.tensor_scalar_add(d[:, jb * TB:(jb + 1) * TB],
                                                P[:, :], bb[:, 8 + c:9 + c]),
                            inc="v")
            else:
                vq = b.emit("vector", lambda E, d=qF[c], P=ctx.P[bq], c=c, jb=jb:
                            E.tensor_copy(d[:, jb * TB:(jb + 1) * TB], P[:, :]),
                            inc="v")
                b.drain("vector")
                vk = b.emit("vector", lambda E, d=kF[c], P=ctx.P[bk], c=c, jb=jb:
                            E.tensor_copy(d[:, jb * TB:(jb + 1) * TB], P[:, :]),
                            inc="v")
            free_bank(ctx, bq, "v", vq)
            free_bank(ctx, bk, "v", vk)
            if rms:
                _rmsnorm_qk(ctx, c, jb, vq, vk)
            else:
                g.set(("aqk", c), "v", vk)
        for c2 in range(NCH):
            g.set(("xnq", j % nset, c2), "t", b.val("t"))
    return


def _rmsnorm_qk(ctx, c, jb, vq, vk):
    b = ctx.b
    g = ctx.g
    EPS = 1e-6
    qF, kF = ctx.at_qF, ctx.at_kF
    R = ctx.at_rows
    w2dd = ctx.consts["l3_w2dd"]
    b.wait("scalar", "v", vk)
    g.wait("scalar", "x2", 0)
    sq = b.emit("scalar", lambda E, d=ctx.at_q2[0], s_=qF[c], jb=jb:
                E.activation(d[:], s_[:, jb * TB:(jb + 1) * TB], AF.Square),
                inc="s")
    g.wait("scalar", "x2", 1)
    sk = b.emit("scalar", lambda E, d=ctx.at_q2[1], s_=kF[c], jb=jb:
                E.activation(d[:], s_[:, jb * TB:(jb + 1) * TB], AF.Square),
                inc="s")
    for bi in range(4, 8):
        use_bank(ctx, bi)
    b.wait("tensor", "s", sq)
    b.emit("tensor", lambda E, P=ctx.P[4], o=ctx.onesb, x=ctx.at_q2[0]:
           E.matmul(P[0:1, 0:TB], o[0:64, :], x[0:64, :], start=True,
                    stop=True, tile_position=(0, 0)))
    tq = b.emit("tensor", lambda E, P=ctx.P[5], o=ctx.onesb, x=ctx.at_q2[0]:
                E.matmul(P[0:1, 0:TB], o[64:128, :], x[64:128, :], start=True,
                         stop=True, tile_position=(64, 0)), inc="t")
    g.set(("x2", 0), "t", tq)
    b.wait("tensor", "s", sk)
    b.emit("tensor", lambda E, P=ctx.P[6], o=ctx.onesb, x=ctx.at_q2[1]:
           E.matmul(P[0:1, 0:TB], o[0:64, :], x[0:64, :], start=True,
                    stop=True, tile_position=(0, 0)))
    tk = b.emit("tensor", lambda E, P=ctx.P[7], o=ctx.onesb, x=ctx.at_q2[1]:
                E.matmul(P[0:1, 0:TB], o[64:128, :], x[64:128, :], start=True,
                         stop=True, tile_position=(64, 0)), inc="t")
    g.set(("x2", 1), "t", tk)
    b.wait("vector", "t", tk)
    for ri in range(4):
        b.emit("vector", lambda E, R=R, P=ctx.P[4 + ri], ri=ri:
               E.tensor_scalar(R[0:1, ri * TB:(ri + 1) * TB], P[0:1, 0:TB],
                               1.0 / 64.0, EPS, op0=ALU.mult, op1=ALU.add))
    b.drain("vector")
    rv = b.emit("vector", lambda E: E.engine_nop(), inc="v")
    b.wait("scalar", "v", rv)
    b.emit("scalar", lambda E, R=R:
           E.activation(R[0:1, 0:4 * TB], R[0:1, 0:4 * TB], AF.Ln))
    b.drain("scalar")
    rs = b.emit("scalar", lambda E, R=R:
                E.activation(R[0:1, 0:4 * TB], R[0:1, 0:4 * TB], AF.Exp,
                             scale=-0.5), inc="s")
    for bi in range(4, 8):
        free_bank(ctx, bi, "s", rs)
    b.wait("tensor", "s", rs)
    use_bank(ctx, 4)
    use_bank(ctx, 5)
    b.emit("tensor", lambda E, P=ctx.P[4], o=ctx.onesrow, R=R:
           E.matmul(P[0:64, :], o[0:1, 0:64], R[0:1, 0:TB], start=True,
                    stop=True, tile_position=(0, 0)))
    b.emit("tensor", lambda E, P=ctx.P[4], o=ctx.onesrow, R=R:
           E.matmul(P[64:128, :], o[0:1, 0:64], R[0:1, TB:2 * TB], start=True,
                    stop=True, tile_position=(0, 64)))
    b.emit("tensor", lambda E, P=ctx.P[5], o=ctx.onesrow, R=R:
           E.matmul(P[0:64, :], o[0:1, 0:64], R[0:1, 2 * TB:3 * TB], start=True,
                    stop=True, tile_position=(0, 0)))
    tb = b.emit("tensor", lambda E, P=ctx.P[5], o=ctx.onesrow, R=R:
                E.matmul(P[64:128, :], o[0:1, 0:64], R[0:1, 3 * TB:4 * TB],
                         start=True, stop=True, tile_position=(0, 64)), inc="t")
    b.wait("vector", "t", tb)
    b.emit("vector", lambda E, d=qF[c], P=ctx.P[4], jb=jb:
           E.scalar_tensor_tensor(d[:, jb * TB:(jb + 1) * TB],
                                  d[:, jb * TB:(jb + 1) * TB], 1.0, P[:, :],
                                  op0=ALU.mult, op1=ALU.mult))
    b.drain("vector")
    ak = b.emit("vector", lambda E, d=kF[c], P=ctx.P[5], w=w2dd, jb=jb:
                E.scalar_tensor_tensor(d[:, jb * TB:(jb + 1) * TB],
                                       d[:, jb * TB:(jb + 1) * TB], w[:, 0:1],
                                       P[:, :], op0=ALU.mult, op1=ALU.mult),
                inc="v")
    b.drain("vector")
    free_bank(ctx, 4, "v", ak)
    free_bank(ctx, 5, "v", ak)
    g.set(("aqk", c), "v", ak)

def _attn_passA(ctx, s, tables=False):
    b = ctx.b
    g = ctx.g
    qF, kF = ctx.at_qF, ctx.at_kF
    mq, sc, R = ctx.at_mq, ctx.at_sc, ctx.at_rows
    b.wait("tensor", "v", b.val("v"))
    for c in range(NCH):
        hA, hB = 2 * c, 2 * c + 1
        for qt in range(SEQ // 128):
            for bi in range(8):
                use_bank(ctx, bi)
            for kb in range(4):
                b.emit("tensor", lambda E, P=ctx.P[kb], q=qF[c], k=kF[c],
                       qt=qt, kb=kb:
                       E.matmul(P[:, :], q[0:64, qt * 128:(qt + 1) * 128],
                                k[0:64, kb * TB:(kb + 1) * TB],
                                start=True, stop=True, tile_position=(0, 0)))
                last = (kb == 3)
                b.emit("tensor", lambda E, P=ctx.P[4 + kb], q=qF[c], k=kF[c],
                       qt=qt, kb=kb:
                       E.matmul(P[:, :], q[64:128, qt * 128:(qt + 1) * 128],
                                k[64:128, kb * TB:(kb + 1) * TB],
                                start=True, stop=True, tile_position=(64, 0)),
                       inc=("t" if last else None))
            tmm = b.val("t")
            b.wait("vector", "t", tmm)
            b.emit("vector", lambda E, m=mq, PA=ctx.PSALL, qt=qt:
                   E.reduce_max(m[:, qt:qt + 1], PA[:, 0:SEQ], axis=AX.X))
            b.drain("vector")
            rm = b.emit("vector", lambda E, m=mq, PA=ctx.PSALL, qt=qt:
                        E.reduce_max(m[:, 16 + qt:17 + qt], PA[:, SEQ:2 * SEQ],
                                     axis=AX.X), inc="v")
            b.drain("vector")
            for bi in range(8):
                free_bank(ctx, bi, "v", rm)
        for hi, h in enumerate((hA, hB)):
            b.emit("vector", lambda E, sc=sc, m=mq, hi=hi:
                   E.reduce_max(sc[:, hi:hi + 1], m[:, hi * 16:(hi + 1) * 16],
                                axis=AX.X))
        b.drain("vector")
        fm = b.emit("vector", lambda E: E.engine_nop(), inc="v")
        for hi, h in enumerate((hA, hB)):
            b.wait("tensor", "v", fm if hi == 0 else b.val("v"))
            use_bank(ctx, 6)
            tb = b.emit("tensor", lambda E, P=ctx.P[6], sc=sc, idn=ctx.identf,
                        hi=hi:
                        E.transpose(P[0:1, 0:128], sc[:, hi:hi + 1], idn[:]),
                        inc="t")
            b.wait("vector", "t", tb)
            rv = b.emit("vector", lambda E, R=R, P=ctx.P[6], h=h:
                        E.reduce_max(R[0:1, h:h + 1], P[0:1, 0:128], axis=AX.X),
                        inc="v")
            b.drain("vector")
            free_bank(ctx, 6, "v", rv)
    return


def _attn_finalizeM(ctx, tables=False):
    b = ctx.b
    g = ctx.g
    R = ctx.at_rows
    if tables:
        b.emit("vector", lambda E, R=R, bm=ctx.at_bmax:
               E.tensor_tensor(R[0:1, 0:NH], R[0:1, 0:NH], bm[0:1, 0:NH],
                               op=ALU.add))
        b.drain("vector")
    nm = b.emit("vector", lambda E, R=R:
                E.tensor_scalar_mul(R[0:1, NH:2 * NH], R[0:1, 0:NH], -1.0),
                inc="v")
    b.wait("tensor", "v", nm)
    use_bank(ctx, 6)
    tb = b.emit("tensor", lambda E, P=ctx.P[6], o=ctx.onesrow, R=R:
                E.matmul(P[:, 0:NH], o[0:1, 0:128], R[0:1, NH:2 * NH],
                         start=True, stop=True), inc="t")
    b.wait("vector", "t", tb)
    ngm = b.emit("vector", lambda E, d=ctx.at_negM, P=ctx.P[6]:
                 E.tensor_copy(d[:, :], P[:, 0:NH]), inc="v")
    free_bank(ctx, 6, "v", ngm)
    return ngm

def _attn_passB_out(ctx, s, x_dram, xout_dram, W, bias_c, bproj_c,
                    tables=None, u_dram=None):
    b = ctx.b
    g = ctx.g
    qF, kF, vT, aw = ctx.at_qF, ctx.at_kF, ctx.at_vT, ctx.at_w
    negM = ctx.at_negM
    R = ctx.at_rows
    for qb in range(SEQ // TB):
        j = s * (SEQ // TB) + qb
        cbuf = ctx.at_ctx[qb % 2]
        xbuf = ctx.xf[j % 2]
        for c in range(NCH):
            g.wait("sync", "xf", j % 2, c)
            b.emit("sync", lambda E, d=xbuf[c], s_=x_dram, c=c, j=j:
                   E.dma_start(out=d[:], in_=s_[c * 128:(c + 1) * 128,
                                               j * TB:(j + 1) * TB]),
                   inc="ld", n=16)
            if u_dram is not None:
                g.wait("sync", "xf", (j % 2) ^ 1, c)
                b.emit("sync", lambda E, d=ctx.xf[(j % 2) ^ 1][c], s_=u_dram,
                       c=c, j=j:
                       E.dma_start(out=d[:], in_=s_[c * 128:(c + 1) * 128,
                                                   j * TB:(j + 1) * TB]),
                       inc="ld", n=16)
        x_ld = b.val("ld")
        for c in range(NCH):
            hA, hB = 2 * c, 2 * c + 1
            if tables is not None:
                for hi, h in enumerate((hA, hB)):
                    g.wait("sync", "atab", hi)
                    b.emit("sync", lambda E, d=ctx.at_tab[hi], s_=tables["a2"],
                           h=h, qb=qb:
                           E.dma_start(out=d[:],
                                       in_=s_[h * 128:(h + 1) * 128,
                                              qb * TB:qb * TB + 2432]),
                           inc="ld", n=16)
                tab_ld = dummy_loads(ctx, tables["a2"], 2)
                b.wait("tensor", "ld", tab_ld)
                b.grace()
            use_bank(ctx, 4)
            use_bank(ctx, 5)
            use_bank(ctx, 6)
            ngm_v = ctx._negM_ready
            b.wait("scalar", "v", ngm_v)
            for kt in range(SEQ // 128):
                bk = kt % 2
                bk2 = 2 + kt % 2
                use_bank(ctx, bk)
                mm1 = b.emit("tensor", lambda E, P=ctx.P[bk], k=kF[c], q=qF[c],
                             kt=kt, qb=qb:
                             E.matmul(P[:, :], k[0:64, kt * 128:(kt + 1) * 128],
                                      q[0:64, qb * TB:(qb + 1) * TB],
                                      start=True, stop=(tables is None),
                                      tile_position=(0, 0)),
                             inc=("t" if tables is None else None))
                if tables is not None:
                    n0 = 1920 - 128 * kt
                    mm1 = b.emit("tensor", lambda E, P=ctx.P[bk], idn=ctx.ident,
                                 tb_=ctx.at_tab[0], n0=n0:
                                 E.matmul(P[:, :], idn[:], tb_[:, n0:n0 + TB],
                                          start=False, stop=True), inc="t")
                use_bank(ctx, bk2)
                mm2 = b.emit("tensor", lambda E, P=ctx.P[bk2], k=kF[c], q=qF[c],
                             kt=kt, qb=qb:
                             E.matmul(P[:, :], k[64:128, kt * 128:(kt + 1) * 128],
                                      q[64:128, qb * TB:(qb + 1) * TB],
                                      start=True, stop=(tables is None),
                                      tile_position=(64, 0)),
                             inc=("t" if tables is None else None))
                if tables is not None:
                    n0 = 1920 - 128 * kt
                    mm2 = b.emit("tensor", lambda E, P=ctx.P[bk2], idn=ctx.ident,
                                 tb_=ctx.at_tab[1], n0=n0:
                                 E.matmul(P[:, :], idn[:], tb_[:, n0:n0 + TB],
                                          start=False, stop=True), inc="t")
                b.wait("scalar", "t", mm2)
                g.wait("scalar", "aPt", 0)
                e1 = b.emit("scalar", lambda E, d=ctx.at_P[0], P=ctx.P[bk],
                            nm=negM, h=hA:
                            E.activation(d[:], P[:, :], AF.Exp,
                                         bias=nm[:, h:h + 1]), inc="s")
                free_bank(ctx, bk, "s", e1)
                g.wait("scalar", "aPt", 1)
                e2 = b.emit("scalar", lambda E, d=ctx.at_P[1], P=ctx.P[bk2],
                            nm=negM, h=hB:
                            E.activation(d[:], P[:, :], AF.Exp,
                                         bias=nm[:, h:h + 1]), inc="s")
                free_bank(ctx, bk2, "s", e2)
                b.wait("tensor", "s", e2)
                b.emit("tensor", lambda E, P=ctx.P[4], o=ctx.onesb,
                       x=ctx.at_P[0], kt=kt:
                       E.matmul(P[0:1, 0:TB], o[:, :], x[:, :],
                                start=(kt == 0), stop=(kt == 15)))
                b.emit("tensor", lambda E, P=ctx.P[5], o=ctx.onesb,
                       x=ctx.at_P[1], kt=kt:
                       E.matmul(P[0:1, 0:TB], o[:, :], x[:, :],
                                start=(kt == 0), stop=(kt == 15)))
                b.emit("tensor", lambda E, P=ctx.P[6], v=vT[kt], x=ctx.at_P[0],
                       kt=kt, hA=hA:
                       E.matmul(P[0:64, :], v[:, hA * 64:(hA + 1) * 64], x[:, :],
                                start=(kt == 0), stop=(kt == 15),
                                tile_position=(0, 0)))
                av = b.emit("tensor", lambda E, P=ctx.P[6], v=vT[kt],
                            x=ctx.at_P[1], kt=kt, hB=hB:
                            E.matmul(P[64:128, :], v[:, hB * 64:(hB + 1) * 64],
                                     x[:, :], start=(kt == 0), stop=(kt == 15),
                                     tile_position=(0, 64)), inc="t")
                g.set(("aPt", 0), "t", av)
                g.set(("aPt", 1), "t", av)
            av_done = b.val("t")
            if tables is not None:
                g.set(("atab", 0), "t", av_done)
                g.set(("atab", 1), "t", av_done)
            b.wait("vector", "t", av_done)
            b.emit("vector", lambda E, R=R, P=ctx.P[4]:
                   E.reciprocal(R[0:1, 2 * TB:3 * TB], P[0:1, 0:TB]))
            b.drain("vector")
            rc = b.emit("vector", lambda E, R=R, P=ctx.P[5]:
                        E.reciprocal(R[0:1, 3 * TB:4 * TB], P[0:1, 0:TB]),
                        inc="v")
            b.drain("vector")
            free_bank(ctx, 4, "v", rc)
            free_bank(ctx, 5, "v", rc)
            b.wait("tensor", "v", rc)
            use_bank(ctx, 7)
            b.emit("tensor", lambda E, P=ctx.P[7], o=ctx.onesrow, R=R:
                   E.matmul(P[0:64, :], o[0:1, 0:64], R[0:1, 2 * TB:3 * TB],
                            start=True, stop=True, tile_position=(0, 0)))
            cb = b.emit("tensor", lambda E, P=ctx.P[7], o=ctx.onesrow, R=R:
                        E.matmul(P[64:128, :], o[0:1, 0:64],
                                 R[0:1, 3 * TB:4 * TB], start=True, stop=True,
                                 tile_position=(0, 64)), inc="t")
            b.wait("vector", "t", cb)
            g.wait("vector", "actx", qb % 2, c)
            b.emit("vector", lambda E, d=cbuf[c], P=ctx.P[6]:
                   E.tensor_copy(d[:], P[:, :]))
            b.drain("vector")
            b.emit("vector", lambda E, d=cbuf[c], P=ctx.P[7]:
                   E.scalar_tensor_tensor(d[:], d[:], 1.0, P[:, :],
                                          op0=ALU.mult, op1=ALU.mult))
            if bias_c is not None:
                b.drain("vector")
                cv = b.emit("vector", lambda E, d=cbuf[c], bb=ctx.consts[bias_c],
                            c=c:
                            E.tensor_scalar_add(d[:], d[:], bb[:, 16 + c:17 + c]),
                            inc="v")
            else:
                cv = b.emit("vector", lambda E: E.engine_nop(), inc="v")
            free_bank(ctx, 6, "v", cv)
            free_bank(ctx, 7, "v", cv)
            ctx._ctx_ready = cv
        lds = {}
        def _ldo(ms):
            p = ms % 3
            g.wait("sync", "aw", p)
            b.emit("sync", lambda E, d=aw[p], s_=W["wo"], ms=ms:
                   E.dma_start(out=d[:], in_=s_[ms * 128:(ms + 1) * 128, :]),
                   inc="ld", n=16)
            lds[ms] = b.val("ld")
        _ldo(0)
        _ldo(1)
        bpr = ctx.consts[bproj_c] if bproj_c is not None else None
        for ms in range(NCH):
            if ms + 2 < NCH:
                _ldo(ms + 2)
            else:
                lds[ms + 2] = dummy_loads(ctx, W["wo"], 2)
            b.wait("tensor", "ld", lds[ms + 2])
            if ms == 0:
                b.grace()
                b.wait("tensor", "v", ctx._ctx_ready)
            p = ms % 3
            bank = ms % 2
            use_bank(ctx, bank)
            for kc in range(NCH):
                b.emit("tensor", lambda E, P=ctx.P[bank], w=aw[p], x=cbuf[kc],
                       kc=kc:
                       E.matmul(P[:, :], w[:, kc * 128:(kc + 1) * 128], x[:],
                                start=(kc == 0), stop=(kc == NCH - 1)),
                       inc=("t" if kc == NCH - 1 else None))
            mmo = b.val("t")
            g.set(("aw", p), "t", mmo)
            b.wait("vector", "t", mmo)
            g.wait("vector", "tmp", ms % ctx.NTMP)
            if bpr is not None:
                ov = b.emit("vector", lambda E, d=ctx.tmp[ms % ctx.NTMP], P=ctx.P[bank],
                            bb=bpr, ms=ms:
                            E.tensor_scalar_add(d[:], P[:, :], bb[:, ms:ms + 1]),
                            inc="v")
            else:
                ov = b.emit("vector", lambda E, d=ctx.tmp[ms % ctx.NTMP], P=ctx.P[bank]:
                            E.tensor_copy(d[:], P[:, :]), inc="v")
            free_bank(ctx, bank, "v", ov)
            b.drain("vector")
            b.wait("vector", "ld", x_ld)
            if u_dram is not None:
                b.emit("vector", lambda E, d=xbuf[ms], u=ctx.xf[(j % 2) ^ 1][ms]:
                       E.tensor_tensor(d[:], d[:], u[:], op=ALU.add))
                b.drain("vector")
            rv = b.emit("vector", lambda E, d=xbuf[ms], t=ctx.tmp[ms % ctx.NTMP]:
                        E.tensor_tensor(d[:], d[:], t[:], op=ALU.add), inc="v")
            g.set(("tmp", ms % ctx.NTMP), "v", rv)
            b.wait("scalar", "v", rv)
            stv = b.emit("scalar", lambda E, s_=xbuf[ms], d=xout_dram, ms=ms,
                         j=j:
                         E.dma_start(out=d[ms * 128:(ms + 1) * 128,
                                          j * TB:(j + 1) * TB], in_=s_[:]),
                         inc="st", n=16)
            g.set(("xf", j % 2, ms), "st", stv)
            if u_dram is not None:
                g.set(("xf", (j % 2) ^ 1, ms), "st", stv)
        for c in range(NCH):
            g.set(("actx", qb % 2, c), "t", b.val("t"))
    tend = b.val("t")
    for c in range(NCH):
        g.set(("aqk", c), "t", tend)
    for i in range(SEQ // 128):
        g.set(("avT", i), "t", tend)
    for i in range(NCH):
        g.set(("avw", i), "t", tend)


def l0attnb_run(ctx, x_dram, xn_dram, xout_dram, W, ln_st):
    b = ctx.b
    qkn = ctx.consts["l3_qkn"]
    if "l3_w2dd" not in ctx.consts:
        w2dd = b.sbuf("c_l3_w2dd", (128, 1), F32)
        ctx.consts["l3_w2dd"] = w2dd
        b.wait("vector", "ld", b.val("ld"))
        b.emit("vector", lambda E, d=ctx.at_sc, s=qkn:
               E.tensor_copy(d[0:64, 3:4], s[64:128, :]))
        b.drain("vector")
        b.emit("vector", lambda E, d=w2dd, s=qkn, t=ctx.at_sc:
               E.scalar_tensor_tensor(d[0:64, :], s[0:64, :], 0.125,
                                      t[0:64, 3:4], op0=ALU.mult, op1=ALU.mult))
        b.drain("vector")
        b.emit("vector", lambda E, d=w2dd:
               E.tensor_copy(d[64:128, :], d[0:64, :]), inc="v")
        b.drain("vector")
    for s in range(NSEQ):
        _attn_proj_phase(ctx, xn_dram, W, ln_st, s, rms=True, bias_c="l3_bqkv")
        _attn_passA(ctx, s)
        ctx._negM_ready = _attn_finalizeM(ctx)
        _attn_passB_out(ctx, s, x_dram, xout_dram, W, bias_c="l3_bqkv",
                        bproj_c="l3_bproj")
    return b.val("st")

def l0attna_run(ctx, x_dram, xn_dram, u_dram, xout_dram, W, ln_st, a2_cache):
    b = ctx.b
    g = ctx.g
    for h in range(NH):
        hi = h % 2
        g.wait("sync", "atab", hi)
        b.emit("sync", lambda E, d=ctx.at_tab[hi], s_=a2_cache, h=h:
               E.dma_start(out=d[:], in_=s_[h * 128:(h + 1) * 128, 0:2432]),
               inc="ld", n=16)
        b.emit("sync", lambda E, d=ctx.at_tab[hi], s_=a2_cache, h=h:
               E.dma_start(out=d[:, 0:1536],
                           in_=s_[h * 128:(h + 1) * 128, 2432:3968]),
               inc="ld", n=16) if False else None
        tl = dummy_loads(ctx, a2_cache, 2)
        b.wait("vector", "ld", tl)
        b.emit("vector", lambda E, d=ctx.at_sc, s_=ctx.at_tab[hi]:
               E.reduce_max(d[:, 2:3], s_[:], axis=AX.X))
        b.drain("vector")
        bm = b.emit("vector", lambda E: E.engine_nop(), inc="v")
        b.wait("tensor", "v", bm)
        use_bank(ctx, 6)
        tb = b.emit("tensor", lambda E, P=ctx.P[6], sc=ctx.at_sc, idn=ctx.identf:
                    E.transpose(P[0:1, 0:128], sc[:, 2:3], idn[:]), inc="t")
        b.wait("vector", "t", tb)
        rv = b.emit("vector", lambda E, d=ctx.at_bmax, P=ctx.P[6], h=h:
                    E.reduce_max(d[0:1, h:h + 1], P[0:1, 0:128], axis=AX.X),
                    inc="v")
        b.drain("vector")
        free_bank(ctx, 6, "v", rv)
        g.set(("atab", hi), "v", rv)
    for s in range(NSEQ):
        _attn_proj_phase(ctx, xn_dram, W, ln_st, s, rms=False, bias_c=None)
        _attn_passA(ctx, s)
        ctx._negM_ready = _attn_finalizeM(ctx, tables=True)
        _attn_passB_out(ctx, s, x_dram, xout_dram, W, bias_c=None,
                        bproj_c=None, tables=dict(a2=a2_cache), u_dram=u_dram)
    return b.val("st")


def l1aa_run(ctx, x_dram, xn_dram, xout_dram, W, ln_st, decay_cache, r_dram):
    b = ctx.b
    g = ctx.g
    qF, kF, vT = ctx.at_qF, ctx.at_kF, ctx.at_vT
    R = ctx.at_rows
    gnc = ctx.consts["l5_gnc"]
    gnbc = ctx.consts["l5_gnbc"]
    CNT = float(NH * DK * SEQ)
    b.emit("vector", lambda E, z=ctx.at_zero: E.memset(z[:], 0.0), inc="v")
    b.drain("vector")
    for s in range(NSEQ):
        _attn_proj_phase(ctx, xn_dram, W, ln_st, s, rms=False, bias_c=None)
        b.wait("tensor", "v", b.val("v"))
        g.wait("sync", "pkvb", 0)
        b.emit("sync", lambda E, d=ctx.at_pkv, s_=W["pkv"]:
               E.dma_start(out=d[:], in_=s_[0:128, :]), inc="ld", n=16)
        pkv_ld = dummy_loads(ctx, W["pkv"], 2)
        b.wait("tensor", "ld", pkv_ld)
        first_stat = True
        for qb in range(SEQ // TB):
            j = s * (SEQ // TB) + qb
            for c in range(NCH):
                hA, hB = 2 * c, 2 * c + 1
                for hi, h in enumerate((hA, hB)):
                    g.wait("sync", "atab", hi)
                    b.emit("sync", lambda E, d=ctx.at_tab[hi], s_=decay_cache,
                           h=h, qb=qb:
                           E.dma_start(out=d[:],
                                       in_=s_[h * 128:(h + 1) * 128,
                                              qb * TB:qb * TB + 2432]),
                           inc="ld", n=16)
                tab_ld = dummy_loads(ctx, decay_cache, 2)
                b.wait("tensor", "ld", tab_ld)
                b.grace()
                use_bank(ctx, 6)
                use_bank(ctx, 7)
                b.emit("tensor", lambda E, P=ctx.P[7], pk=ctx.at_pkv, q=qF[c],
                       hA=hA, qb=qb:
                       E.matmul(P[0:64, :], pk[0:64, hA * 64:(hA + 1) * 64],
                                q[0:64, qb * TB:(qb + 1) * TB], start=True,
                                stop=True, tile_position=(0, 0)))
                cr = b.emit("tensor", lambda E, P=ctx.P[7], pk=ctx.at_pkv,
                            q=qF[c], hB=hB, qb=qb:
                            E.matmul(P[64:128, :],
                                     pk[64:128, hB * 64:(hB + 1) * 64],
                                     q[64:128, qb * TB:(qb + 1) * TB],
                                     start=True, stop=True,
                                     tile_position=(64, 64)), inc="t")
                nkt = 4 * qb + 4
                for kt in range(nkt):
                    bk = kt % 2
                    bk2 = 2 + kt % 2
                    use_bank(ctx, bk)
                    mm1 = b.emit("tensor", lambda E, P=ctx.P[bk], k=kF[c],
                                 q=qF[c], kt=kt, qb=qb:
                                 E.matmul(P[:, :],
                                          k[0:64, kt * 128:(kt + 1) * 128],
                                          q[0:64, qb * TB:(qb + 1) * TB],
                                          start=True, stop=True,
                                          tile_position=(0, 0)), inc="t")
                    use_bank(ctx, bk2)
                    mm2 = b.emit("tensor", lambda E, P=ctx.P[bk2], k=kF[c],
                                 q=qF[c], kt=kt, qb=qb:
                                 E.matmul(P[:, :],
                                          k[64:128, kt * 128:(kt + 1) * 128],
                                          q[64:128, qb * TB:(qb + 1) * TB],
                                          start=True, stop=True,
                                          tile_position=(64, 0)), inc="t")
                    n0 = 1920 - 128 * kt
                    b.wait("vector", "t", mm2)
                    g.wait("vector", "aPt", 0)
                    e1 = b.emit("vector", lambda E, d=ctx.at_P[0], P=ctx.P[bk],
                                tb_=ctx.at_tab[0], n0=n0:
                                E.tensor_tensor(d[:], P[:, :],
                                                tb_[:, n0:n0 + TB],
                                                op=ALU.mult), inc="v")
                    free_bank(ctx, bk, "v", e1)
                    b.drain("vector")
                    g.wait("vector", "aPt", 1)
                    e2 = b.emit("vector", lambda E, d=ctx.at_P[1], P=ctx.P[bk2],
                                tb_=ctx.at_tab[1], n0=n0:
                                E.tensor_tensor(d[:], P[:, :],
                                                tb_[:, n0:n0 + TB],
                                                op=ALU.mult), inc="v")
                    free_bank(ctx, bk2, "v", e2)
                    b.wait("tensor", "v", e2)
                    av = b.emit("tensor", lambda E, P=ctx.P[6], v=vT[kt],
                                x=ctx.at_P[0], kt=kt, hA=hA, first=(kt == 0),
                                last=(kt == nkt - 1):
                                E.matmul(P[0:64, :],
                                         v[:, hA * 64:(hA + 1) * 64], x[:, :],
                                         start=first, stop=last,
                                         tile_position=(0, 0)))
                    av = b.emit("tensor", lambda E, P=ctx.P[6], v=vT[kt],
                                x=ctx.at_P[1], kt=kt, hB=hB, first=(kt == 0),
                                last=(kt == nkt - 1):
                                E.matmul(P[64:128, :],
                                         v[:, hB * 64:(hB + 1) * 64], x[:, :],
                                         start=first, stop=last,
                                         tile_position=(0, 64)), inc="t")
                    g.set(("aPt", 0), "t", b.val("t"))
                    g.set(("aPt", 1), "t", b.val("t"))
                g.set(("atab", 0), "t", b.val("t"))
                g.set(("atab", 1), "t", b.val("t"))
                av_done = b.val("t")
                b.wait("vector", "t", av_done)
                g.wait("vector", "arb", 0)
                b.emit("vector", lambda E, d=ctx.at_vf, P=ctx.P[6]:
                       E.tensor_copy(d[:], P[:, :]))
                b.drain("vector")
                rv = b.emit("vector", lambda E, d=ctx.at_rb, a=ctx.at_vf,
                            P=ctx.P[7]:
                            E.scalar_tensor_tensor(d[:], P[:, :], 1024.0, a[:],
                                                   op0=ALU.mult, op1=ALU.add),
                            inc="v")
                free_bank(ctx, 6, "v", rv)
                free_bank(ctx, 7, "v", rv)
                b.wait("scalar", "v", rv)
                g.wait("scalar", "x2", 0)
                sq = b.emit("scalar", lambda E, d=ctx.x2[0], s_=ctx.at_rb:
                            E.activation(d[:], s_[:], AF.Square), inc="s")
                if first_stat:
                    use_bank(ctx, 4)
                    use_bank(ctx, 5)
                b.wait("tensor", "v", rv)
                b.emit("tensor", lambda E, P=ctx.P[4], o=ctx.onesb, x=ctx.at_rb,
                       f=first_stat:
                       E.matmul(P[0:1, 0:TB], o[:, :], x[:, :], start=f,
                                stop=False))
                b.wait("tensor", "s", sq)
                st_mm = b.emit("tensor", lambda E, P=ctx.P[5], o=ctx.onesb,
                               x=ctx.x2[0], f=first_stat:
                               E.matmul(P[0:1, 0:TB], o[:, :], x[:, :], start=f,
                                        stop=False), inc="t")
                g.set(("x2", 0), "t", st_mm)
                g.set(("aPt", 0), "t", st_mm)
                first_stat = False
                b.wait("scalar", "v", rv)
                sr = b.emit("scalar", lambda E, s_=ctx.at_rb, d=r_dram, c=c, j=j:
                            E.dma_start(out=d[c * 128:(c + 1) * 128,
                                             j * TB:(j + 1) * TB], in_=s_[:]),
                            inc="st", n=16)
                g.set(("arb", 0), "st", sr)
        b.emit("tensor", lambda E, P=ctx.P[4], o=ctx.onesb, z=ctx.at_zero:
               E.matmul(P[0:1, 0:TB], o[:, :], z[:, :], start=False, stop=True))
        scl = b.emit("tensor", lambda E, P=ctx.P[5], o=ctx.onesb, z=ctx.at_zero:
                     E.matmul(P[0:1, 0:TB], o[:, :], z[:, :], start=False,
                              stop=True), inc="t")
        b.wait("vector", "t", scl)
        b.emit("vector", lambda E, R=R, P=ctx.P[4]:
               E.reduce_sum(R[0:1, 0:1], P[0:1, 0:TB], axis=AX.X))
        b.drain("vector")
        b.emit("vector", lambda E, R=R, P=ctx.P[5]:
               E.reduce_sum(R[0:1, 1:2], P[0:1, 0:TB], axis=AX.X))
        b.drain("vector")
        b.emit("vector", lambda E, R=R, CNT=CNT:
               E.tensor_scalar_mul(R[0:1, 2:3], R[0:1, 0:1], 1.0 / CNT))
        b.drain("vector")
        b.emit("vector", lambda E, R=R:
               E.tensor_tensor(R[0:1, 3:4], R[0:1, 2:3], R[0:1, 2:3],
                               op=ALU.mult))
        b.drain("vector")
        b.emit("vector", lambda E, R=R, CNT=CNT:
               E.scalar_tensor_tensor(R[0:1, 4:5], R[0:1, 1:2], 1.0 / CNT,
                                      R[0:1, 3:4], op0=ALU.mult,
                                      op1=ALU.subtract))
        b.drain("vector")
        sv2 = b.emit("vector", lambda E, R=R:
                     E.tensor_scalar_add(R[0:1, 5:6], R[0:1, 4:5], 1e-5),
                     inc="v")
        free_bank(ctx, 4, "v", sv2)
        free_bank(ctx, 5, "v", sv2)
        b.wait("scalar", "v", sv2)
        b.emit("scalar", lambda E, R=R:
               E.activation(R[0:1, 6:7], R[0:1, 5:6], AF.Ln))
        b.drain("scalar")
        rss = b.emit("scalar", lambda E, R=R:
                     E.activation(R[0:1, 6:7], R[0:1, 6:7], AF.Exp, scale=-0.5),
                     inc="s")
        b.wait("tensor", "s", rss)
        use_bank(ctx, 4)
        bc = b.emit("tensor", lambda E, P=ctx.P[4], o=ctx.onesrow, R=R:
                    E.matmul(P[:, 0:1], o[0:1, 0:128], R[0:1, 2:3],
                             start=True, stop=True), inc="t")
        use_bank(ctx, 5)
        bc2 = b.emit("tensor", lambda E, P=ctx.P[5], o=ctx.onesrow, R=R:
                     E.matmul(P[:, 0:1], o[0:1, 0:128], R[0:1, 6:7],
                              start=True, stop=True), inc="t")
        b.wait("vector", "t", bc2)
        b.emit("vector", lambda E, d=ctx.at_sc, P=ctx.P[4]:
               E.tensor_copy(d[:, 0:1], P[:, 0:1]))
        b.drain("vector")
        mb = b.emit("vector", lambda E, d=ctx.at_sc, P=ctx.P[5]:
                    E.tensor_copy(d[:, 1:2], P[:, 0:1]), inc="v")
        free_bank(ctx, 4, "v", mb)
        free_bank(ctx, 5, "v", mb)
        b.drain("vector")
        for qb in range(SEQ // TB):
            j = s * (SEQ // TB) + qb
            xbuf = ctx.xf[j % 2]
            for c in range(NCH):
                g.wait("sync", "xf", j % 2, c)
                b.emit("sync", lambda E, d=xbuf[c], s_=x_dram, c=c, j=j:
                       E.dma_start(out=d[:], in_=s_[c * 128:(c + 1) * 128,
                                                   j * TB:(j + 1) * TB]),
                       inc="ld", n=16)
            rb_lds = {}
            def _ldrl(c):
                g.wait("sync", "arl", c % 2)
                b.emit("sync", lambda E, d=ctx.at_rl[c % 2], s_=r_dram, c=c, j=j:
                       E.dma_start(out=d[:], in_=s_[c * 128:(c + 1) * 128,
                                                   j * TB:(j + 1) * TB]),
                       inc="ld", n=16)
                rb_lds[c] = b.val("ld")
            _ldrl(0)
            _ldrl(1)
            x_ld = b.val("ld")
            for c in range(NCH):
                b.wait("vector", "ld", rb_lds[min(c + 1, NCH + 1)])
                if c == 0:
                    b.wait("vector", "ld", x_ld)
                b.emit("vector", lambda E, d=ctx.at_sc, gn=gnc, c=c:
                       E.tensor_tensor(d[:, 2:3], d[:, 1:2], gn[:, c:c + 1],
                                       op=ALU.mult))
                b.drain("vector")
                b.emit("vector", lambda E, d=ctx.at_sc, gb=gnbc, c=c:
                       E.scalar_tensor_tensor(d[:, 3:4], d[:, 2:3], -1.0,
                                              d[:, 0:1], op0=ALU.mult,
                                              op1=ALU.mult))
                b.drain("vector")
                b.emit("vector", lambda E, d=ctx.at_sc, gb=gnbc, c=c:
                       E.tensor_tensor(d[:, 3:4], d[:, 3:4], gb[:, c:c + 1],
                                       op=ALU.add))
                b.drain("vector")
                g.wait("vector", "tmp", c % ctx.NTMP)
                b.emit("vector", lambda E, d=ctx.tmp[c % ctx.NTMP], s_=ctx.at_rl[c % 2],
                       sc=ctx.at_sc:
                       E.tensor_scalar(d[:], s_[:], sc[:, 2:3], sc[:, 3:4],
                                       op0=ALU.mult, op1=ALU.add))
                b.drain("vector")
                rv2 = b.emit("vector", lambda E, d=xbuf[c], t=ctx.tmp[c % ctx.NTMP]:
                             E.tensor_tensor(d[:], d[:], t[:], op=ALU.add),
                             inc="v")
                g.set(("tmp", c % ctx.NTMP), "v", rv2)
                g.set(("arl", c % 2), "v", rv2)
                if c + 2 < NCH:
                    _ldrl(c + 2)
                elif c + 2 == NCH:
                    rb_lds[NCH] = dummy_loads(ctx, r_dram, 1)
                    rb_lds[NCH + 1] = dummy_loads(ctx, r_dram, 1)
                b.wait("scalar", "v", rv2)
                sv3 = b.emit("scalar", lambda E, s_=xbuf[c], d=xout_dram,
                             c=c, j=j:
                             E.dma_start(out=d[c * 128:(c + 1) * 128,
                                              j * TB:(j + 1) * TB], in_=s_[:]),
                             inc="st", n=16)
                g.set(("xf", j % 2, c), "st", sv3)
        tend = b.val("t")
        for c in range(NCH):
            g.set(("aqk", c), "t", tend)
        for i in range(SEQ // 128):
            g.set(("avT", i), "t", tend)
        for i in range(NCH):
            g.set(("avw", i), "t", tend)
    return b.val("st")


def alloc_l1ab(ctx):
    b = ctx.b
    ctx.lb_kv = [b.sbuf(f"lkv{i}", (128, 1024), BF16) for i in range(NCH)]
    ctx.lb_kc = b.sbuf("lkc", (128, NCH), BF16)


def l1ab_run(ctx, x_dram, xn_dram, xout_dram, W, ln_st):
    b = ctx.b
    g = ctx.g
    EPS = 1e-6
    qF, aw, vw = ctx.at_qF, ctx.at_w, ctx.at_vw
    ktm = ctx.at_vT
    vtm = [ctx.at_kF[i // 2][:, (i % 2) * 1024:(i % 2 + 1) * 1024]
           for i in range(16)]
    for s in range(NSEQ):
        for jb in range(SEQ // TB):
            j = s * (SEQ // TB) + jb
            nset = len(ctx.ffn_xnq)
            xnq = ctx.ffn_xnq[j % nset]
            b.wait("sync", "st", ln_st[j])
            for c in range(NCH):
                g.wait("sync", "xnq", j % nset, c)
                b.emit("sync", lambda E, d=xnq[c], s_=xn_dram, c=c, j=j:
                       E.dma_start(out=d[:], in_=s_[c * 128:(c + 1) * 128,
                                                   j * TB:(j + 1) * TB]),
                       inc="ld", n=16)
            for wname, dst, gname, relu in (("wv", vtm, "akT", False),
                                            ("wk", ktm, "avT", True)):
                vlds = {}
                def _ldv(c2, wname=wname):
                    p = c2 % 4
                    g.wait("sync", "avw", p)
                    b.emit("sync", lambda E, d=vw[p], s_=W[wname], c2=c2:
                           E.dma_start(out=d[:],
                                       in_=s_[c2 * 128:(c2 + 1) * 128, :]),
                           inc="ld", n=16)
                    vlds[c2] = b.val("ld")
                for c2 in range(3):
                    _ldv(c2)
                for c2 in range(NCH):
                    if c2 + 3 < NCH:
                        _ldv(c2 + 3)
                    elif c2 + 3 == NCH:
                        vlds[NCH] = dummy_loads(ctx, W[wname], 1)
                        vlds[NCH + 1] = dummy_loads(ctx, W[wname], 1)
                    b.wait("tensor", "ld", vlds[min(c2 + 2, NCH + 1)])
                    if c2 == 0:
                        b.grace()
                    p = c2 % 4
                    bank = c2 % 2
                    use_bank(ctx, bank)
                    for k in range(NCH):
                        b.emit("tensor", lambda E, P=ctx.P[bank], w=vw[p],
                               x=xnq[k], k=k:
                               E.matmul(P[:, :], w[:, k * 128:(k + 1) * 128],
                                        x[:], start=(k == 0),
                                        stop=(k == NCH - 1)),
                               inc=("t" if k == NCH - 1 else None))
                    vmm = b.val("t")
                    g.set(("avw", p), "t", vmm)
                    b.wait("vector", "t", vmm)
                    g.wait("vector", "vfs", 0)
                    g.wait("vector", "prep", 0)
                    if relu:
                        ev = b.emit("vector", lambda E, d=ctx.at_vf,
                                    P=ctx.P[bank]:
                                    E.tensor_scalar(d[:], P[:, :], 0.0, EPS,
                                                    op0=ALU.max, op1=ALU.add),
                                    inc="v")
                    else:
                        ev = b.emit("vector", lambda E, d=ctx.at_vf,
                                    P=ctx.P[bank]:
                                    E.tensor_copy(d[:], P[:, :]), inc="v")
                    free_bank(ctx, bank, "v", ev)
                    b.wait("tensor", "v", ev)
                    for ts in range(4):
                        bank2 = 2 + ts % 2
                        use_bank(ctx, bank2)
                        tt = b.emit("tensor", lambda E, P=ctx.P[bank2],
                                    s_=ctx.at_vf, idn=ctx.identf, ts=ts:
                                    E.transpose(P[:, 0:128],
                                                s_[:, ts * 128:(ts + 1) * 128],
                                                idn[:]), inc="t")
                        b.wait("vector", "t", tt)
                        g.wait("vector", gname, jb * 4 + ts)
                        tv = b.emit("vector", lambda E, d=dst[jb * 4 + ts],
                                    P=ctx.P[bank2], c2=c2:
                                    E.tensor_copy(d[:, c2 * 128:(c2 + 1) * 128],
                                                  P[:, 0:128]), inc="v")
                        free_bank(ctx, bank2, "v", tv)
                    g.set(("vfs", 0), "t", b.val("t"))
                    g.set(("prep", 0), "t", b.val("t"))
            lds = {}
            def _ldq(c):
                p = c % 3
                g.wait("sync", "aw", p)
                b.emit("sync", lambda E, d=aw[p], s_=W["wq"], c=c:
                       E.dma_start(out=d[:], in_=s_[c * 128:(c + 1) * 128, :]),
                       inc="ld", n=16)
                lds[c] = b.val("ld")
            _ldq(0)
            _ldq(1)
            for c in range(NCH):
                if c + 2 < NCH:
                    _ldq(c + 2)
                else:
                    lds[c + 2] = dummy_loads(ctx, W["wq"], 2)
                b.wait("tensor", "ld", lds[c + 2])
                if c == 0:
                    b.grace()
                p = c % 3
                bank = c % 2
                use_bank(ctx, bank)
                for k in range(NCH):
                    b.emit("tensor", lambda E, P=ctx.P[bank], w=aw[p],
                           x=xnq[k], k=k:
                           E.matmul(P[:, :], w[:, k * 128:(k + 1) * 128], x[:],
                                    start=(k == 0), stop=(k == NCH - 1)),
                           inc=("t" if k == NCH - 1 else None))
                mmq = b.val("t")
                g.set(("aw", p), "t", mmq)
                b.wait("vector", "t", mmq)
                g.wait("vector", "aqk", c)
                vq = b.emit("vector", lambda E, d=qF[c], P=ctx.P[bank], c=c,
                            jb=jb:
                            E.tensor_scalar(d[:, jb * TB:(jb + 1) * TB],
                                            P[:, :], 0.0, EPS, op0=ALU.max,
                                            op1=ALU.add), inc="v")
                free_bank(ctx, bank, "v", vq)
                g.set(("aqk", c), "v", vq)
            for c2 in range(NCH):
                g.set(("xnq", j % nset, c2), "t", b.val("t"))
        b.wait("tensor", "v", b.val("v"))
        use_bank(ctx, 6)
        for ds in range(NCH):
            bank = (ds % 2) * 2
            use_bank(ctx, bank)
            use_bank(ctx, bank + 1)
            for kt in range(16):
                for half in range(2):
                    b.emit("tensor", lambda E, PA=ctx.PSALL, x=ktm[kt],
                           v=vtm[kt], ds=ds, half=half, bank=bank, kt=kt:
                           E.matmul(PA[:, (bank + half) * 512:
                                        (bank + half + 1) * 512],
                                    x[:, ds * 128:(ds + 1) * 128],
                                    v[:, half * 512:(half + 1) * 512],
                                    start=(kt == 0), stop=(kt == 15)))
                b.emit("tensor", lambda E, P=ctx.P[6], x=ktm[kt],
                       o=ctx.onesb, ds=ds, kt=kt:
                       E.matmul(P[:, ds:ds + 1], x[:, ds * 128:(ds + 1) * 128],
                                o[:, :], start=(kt == 0), stop=(kt == 15)),
                       inc=("t" if kt == 15 else None))
            kvm = b.val("t")
            b.wait("vector", "t", kvm)
            g.wait("vector", "lkv", ds)
            kve = b.emit("vector", lambda E, d=ctx.lb_kv[ds], PA=ctx.PSALL,
                         bank=bank:
                         E.tensor_copy(d[:, :],
                                       PA[:, bank * 512:(bank + 2) * 512]),
                         inc="v")
            free_bank(ctx, bank, "v", kve)
            free_bank(ctx, bank + 1, "v", kve)
        g.wait("vector", "lkc", 0)
        kce = b.emit("vector", lambda E, d=ctx.lb_kc, P=ctx.P[6]:
                     E.tensor_copy(d[:, :], P[:, 0:NCH]), inc="v")
        free_bank(ctx, 6, "v", kce)
        for jb in range(SEQ // TB):
            j = s * (SEQ // TB) + jb
            cbuf = ctx.at_ctx[jb % 2]
            xbuf = ctx.xf[j % 2]
            for c in range(NCH):
                g.wait("sync", "xf", j % 2, c)
                b.emit("sync", lambda E, d=xbuf[c], s_=x_dram, c=c, j=j:
                       E.dma_start(out=d[:], in_=s_[c * 128:(c + 1) * 128,
                                                   j * TB:(j + 1) * TB]),
                       inc="ld", n=16)
            x_ld = b.val("ld")
            b.wait("tensor", "v", kce)
            use_bank(ctx, 7)
            for dc in range(NCH):
                b.emit("tensor", lambda E, P=ctx.P[7], kc=ctx.lb_kc, q=qF[dc],
                       dc=dc, jb=jb:
                       E.matmul(P[0:1, 0:TB], kc[:, dc:dc + 1],
                                q[:, jb * TB:(jb + 1) * TB],
                                start=(dc == 0), stop=(dc == NCH - 1)),
                       inc=("t" if dc == NCH - 1 else None))
            zmm = b.val("t")
            R = ctx.at_rows
            b.wait("vector", "t", zmm)
            b.emit("vector", lambda E, R=R, P=ctx.P[7]:
                   E.tensor_scalar_add(R[0:1, 0:TB], P[0:1, 0:TB], EPS))
            b.drain("vector")
            zv = b.emit("vector", lambda E, R=R:
                        E.reciprocal(R[0:1, TB:2 * TB], R[0:1, 0:TB]),
                        inc="v")
            free_bank(ctx, 7, "v", zv)
            b.wait("tensor", "v", zv)
            use_bank(ctx, 6)
            b.emit("tensor", lambda E, P=ctx.P[6], o=ctx.onesrow, R=R:
                   E.matmul(P[0:64, :], o[0:1, 0:64], R[0:1, TB:2 * TB],
                            start=True, stop=True, tile_position=(0, 0)))
            zb = b.emit("tensor", lambda E, P=ctx.P[6], o=ctx.onesrow, R=R:
                        E.matmul(P[64:128, :], o[0:1, 0:64], R[0:1, TB:2 * TB],
                                 start=True, stop=True, tile_position=(0, 64)),
                        inc="t")
            for es in range(NCH):
                bank = es % 2
                use_bank(ctx, bank)
                for dc in range(NCH):
                    b.emit("tensor", lambda E, P=ctx.P[bank], kv=ctx.lb_kv[dc],
                           q=qF[dc], es=es, dc=dc, jb=jb:
                           E.matmul(P[:, :], kv[:, es * 128:(es + 1) * 128],
                                    q[:, jb * TB:(jb + 1) * TB],
                                    start=(dc == 0), stop=(dc == NCH - 1)),
                           inc=("t" if dc == NCH - 1 else None))
                amm = b.val("t")
                b.wait("vector", "t", amm)
                g.wait("vector", "actx", jb % 2, es)
                b.emit("vector", lambda E, d=cbuf[es], P=ctx.P[bank]:
                       E.tensor_copy(d[:], P[:, :]))
                b.drain("vector")
                av = b.emit("vector", lambda E, d=cbuf[es], P=ctx.P[6]:
                            E.scalar_tensor_tensor(d[:], d[:], 1.0, P[:, :],
                                                   op0=ALU.mult, op1=ALU.mult),
                            inc="v")
                free_bank(ctx, bank, "v", av)
            ctx_ready = b.val("v")
            free_bank(ctx, 6, "v", ctx_ready)
            lds = {}
            def _ldo(ms):
                p = ms % 3
                g.wait("sync", "aw", p)
                b.emit("sync", lambda E, d=aw[p], s_=W["wo"], ms=ms:
                       E.dma_start(out=d[:], in_=s_[ms * 128:(ms + 1) * 128, :]),
                       inc="ld", n=16)
                lds[ms] = b.val("ld")
            _ldo(0)
            _ldo(1)
            bpr = ctx.consts["l7_bout"]
            for ms in range(NCH):
                if ms + 2 < NCH:
                    _ldo(ms + 2)
                else:
                    lds[ms + 2] = dummy_loads(ctx, W["wo"], 2)
                b.wait("tensor", "ld", lds[ms + 2])
                if ms == 0:
                    b.grace()
                    b.wait("tensor", "v", ctx_ready)
                p = ms % 3
                bank = ms % 2
                use_bank(ctx, bank)
                for kc in range(NCH):
                    b.emit("tensor", lambda E, P=ctx.P[bank], w=aw[p],
                           x=cbuf[kc], kc=kc:
                           E.matmul(P[:, :], w[:, kc * 128:(kc + 1) * 128],
                                    x[:], start=(kc == 0),
                                    stop=(kc == NCH - 1)),
                           inc=("t" if kc == NCH - 1 else None))
                mmo = b.val("t")
                g.set(("aw", p), "t", mmo)
                b.wait("vector", "t", mmo)
                g.wait("vector", "tmp", ms % ctx.NTMP)
                ov = b.emit("vector", lambda E, d=ctx.tmp[ms % ctx.NTMP],
                            P=ctx.P[bank], bb=bpr, ms=ms:
                            E.tensor_scalar_add(d[:], P[:, :], bb[:, ms:ms + 1]),
                            inc="v")
                free_bank(ctx, bank, "v", ov)
                b.drain("vector")
                b.wait("vector", "ld", x_ld)
                rv = b.emit("vector", lambda E, d=xbuf[ms], t=ctx.tmp[ms % ctx.NTMP]:
                            E.tensor_tensor(d[:], d[:], t[:], op=ALU.add),
                            inc="v")
                g.set(("tmp", ms % ctx.NTMP), "v", rv)
                b.wait("scalar", "v", rv)
                stv = b.emit("scalar", lambda E, s_=xbuf[ms], d=xout_dram,
                             ms=ms, j=j:
                             E.dma_start(out=d[ms * 128:(ms + 1) * 128,
                                              j * TB:(j + 1) * TB], in_=s_[:]),
                             inc="st", n=16)
                g.set(("xf", j % 2, ms), "st", stv)
            for es in range(NCH):
                g.set(("actx", jb % 2, es), "t", b.val("t"))
        tend = b.val("t")
        for c in range(NCH):
            g.set(("aqk", c), "t", tend)
        for i in range(SEQ // 128):
            g.set(("avT", i), "t", tend)
            g.set(("akT", i), "t", tend)
        for i in range(4):
            g.set(("avw", i), "t", tend)
        for ds in range(NCH):
            g.set(("lkv", ds), "t", tend)
        g.set(("lkc", 0), "t", tend)
    return b.val("st")


def alloc_fused(ctx):
    b = ctx.b
    ctx.ffn_t12 = [b.sbuf(f"ft12_{i}", (128, TB), F32) for i in range(2)]
    ctx.ffn_t1b = [b.sbuf(f"ft1b_{i}", (128, TB), BF16) for i in range(2)]
    ctx.prep_f = ctx.ffn_t12
    ctx.prep_b = ctx.ffn_t1b
    A = [b.sbuf(f"plA{i}", (128, SEQ), BF16) for i in range(NCH)]
    B = [b.sbuf(f"plB{i}", (128, SEQ), BF16) for i in range(NCH)]
    C = [b.sbuf(f"plC{i}", (128, 4096), BF16) for i in range(4)]
    Dp = [b.sbuf(f"plD{i}", (128, 1024), BF16) for i in range(8)]
    E = b.sbuf("plE", (128, 8960), BF16)
    ctx.plE = E
    ctx.at_qF = A
    ctx.at_kF = B
    ctx.at_vT = [C[i // 4][:, (i % 4) * 1024:(i % 4 + 1) * 1024]
                 for i in range(16)]
    ctx.at_w = Dp[0:3]
    ctx.at_vw = Dp[3:7]
    ctx.at_vf = ctx.prep_f[0]
    ctx.at_q2 = ctx.x2
    ctx.at_rows = ctx.rows
    ctx.at_mq = b.sbuf("amq", (128, 32), F32)
    ctx.at_sc = b.sbuf("asc", (128, 4), F32)
    ctx.at_negM = b.sbuf("anegM", (128, NH), F32)
    ctx.at_P = ctx.x2
    ctx.at_tab = [E[:, 0:2432], E[:, 2432:4864]]
    ctx.at_bmax = b.sbuf("abmax", (1, NH), F32)
    ctx.at_pkv = E[:, 4864:5888]
    ctx.at_rb = E[:, 5888:6400]
    ctx.at_rl = [E[:, 6400:6912], E[:, 6912:7424]]
    ctx.at_zero = E[:, 7424:7936]
    ctx.lb_kv = [E[:, i * 1024:(i + 1) * 1024] for i in range(NCH)]
    ctx.lb_kc = b.sbuf("lkc", (128, NCH), BF16)
    ctx.alloc_xnq(nsets=1)
    ctx.ffn_hA = [A[m // 4][:, (m % 4) * TB:(m % 4 + 1) * TB]
                  for m in range(32)]
    ctx.ffn_hB = [B[m // 4][:, (m % 4) * TB:(m % 4 + 1) * TB]
                  for m in range(32)]
    ctx.ffn_wbuf = Dp[0:8]
    ctx.ffn_hwbuf = C
    ctx.prep_q = [b.sbuf(f"prq{i}", (128, 256), I32) for i in range(2)]
    ctx.prep_f2 = [ctx.xf[0][0], ctx.xf[0][1]]


def set_ctx_chunks(ctx, mode):
    E = ctx.plE
    if mode == "tabs":
        cb = [E[:, 4864 + c * TB:4864 + (c + 1) * TB] for c in range(NCH)]
    elif mode == "lo":
        cb = [E[:, c * TB:(c + 1) * TB] for c in range(NCH)]
    else:
        cb = [ctx.at_kF[c][:, 0:TB] for c in range(NCH)]
    ctx.at_ctx = [cb, cb]


from concourse.bass_utils import run_bass_kernel_spmd

NB_BUCKETS, MAX_DIST = 32, 128


def bucket_of(rel):
    nb = NB_BUCKETS // 2
    ret = (rel > 0).astype(np.int64) * nb
    n = np.abs(rel)
    max_exact = nb // 2
    is_small = n < max_exact
    n_safe = np.maximum(n, 1).astype(np.float64)
    val_large = max_exact + (np.log(n_safe / max_exact) /
                             np.log(MAX_DIST / max_exact)
                             * (nb - max_exact)).astype(np.int64)
    val_large = np.minimum(val_large, nb - 1)
    return ret + np.where(is_small, n, val_large)


def build_a2(rel_bias):
    idx = np.arange(4095) - 2047
    buck = bucket_of(idx)
    g = rel_bias[buck, :]
    p = np.arange(128)[:, None]
    n = np.arange(3968)[None, :]
    gather = 3967 + p - n
    A2 = g[gather]
    A2 = np.ascontiguousarray(A2.transpose(2, 0, 1)).reshape(16 * 128, 3968)
    return A2.astype(np.float32)


def build_decay():
    gamma = 1.0 - 2.0 ** (-5.0 - np.arange(16))
    i = np.arange(4095)
    out = np.zeros((16, 4095), np.float64)
    for h in range(16):
        rel = i - 2047.0
        out[h] = np.where(rel <= 0, gamma[h] ** np.maximum(-rel, 0), 0.0)
    p = np.arange(128)[:, None]
    n = np.arange(3968)[None, :]
    gather = 3967 + p - n
    A = out[:, gather]
    return np.ascontiguousarray(A.reshape(16 * 128, 3968)).astype(np.float32)


def build_fused(nc):
    b = Builder(nc)
    x_in = nc.dram_tensor("x", [D, TOK], F32, kind="ExternalInput").ap()
    idn = nc.dram_tensor("idn", [128, 128], F32, kind="ExternalInput").ap()
    y_out = nc.dram_tensor("y", [D, TOK], F32, kind="ExternalOutput").ap()
    ext = {}
    for nm in ("l1wq", "l1wk", "l1wv", "l1wo", "l3wq", "l3wk", "l3wv",
               "l3wo", "l5wq", "l5wk", "l5wv", "l7wq", "l7wk", "l7wv",
               "l7wo"):
        ext[nm] = nc.dram_tensor(nm + "P", [8, 128, 1024], F32,
                                 kind="ExternalInput")
    ext["a2f"] = nc.dram_tensor("a2f", [16 * 128, 3968], F32,
                                kind="ExternalInput")
    ext["dcf"] = nc.dram_tensor("dcf", [16 * 128, 3968], F32,
                                kind="ExternalInput")
    ext["pkvf"] = nc.dram_tensor("pkvf", [128, 1024], F32,
                                 kind="ExternalInput")
    ext["fc1q"] = nc.dram_tensor("fc1q", [32, 128, 1024], I32,
                                 kind="ExternalInput")
    ext["fc2q"] = nc.dram_tensor("fc2q", [8, 128, 4096], I32,
                                 kind="ExternalInput")
    ext["fc1s"] = nc.dram_tensor("fc1s", [32, 128, 1024], F32,
                                 kind="ExternalInput")
    ext["fc2s"] = nc.dram_tensor("fc2s", [8, 128, 4096], F32,
                                 kind="ExternalInput")
    for nm, shp in [("w1", (32, 128, 1024)), ("w2", (32, 128, 1024)),
                    ("h1", (32, 128, 4096)), ("h2", (32, 128, 4096)),
                    ("w3", (8, 128, 4096))]:
        ext[nm] = nc.dram_tensor(nm + "P", list(shp), F32,
                                 kind="ExternalInput")
    sc = {}
    for nm, shp in [("l1_g", (128, 8)), ("l1_b", (128, 8)),
                    ("l2_g", (128, 8)), ("l2_b", (128, 8)),
                    ("l2_b1", (128, 32)), ("l2_b2", (128, 8)),
                    ("l3_bqkv", (128, 24)), ("l3_qkn", (128, 1)),
                    ("l3_bproj", (128, 8)),
                    ("l5_gnc", (128, 8)), ("l5_gnbc", (128, 8)),
                    ("l6_s1", (128, 32)), ("l6_s2", (128, 32)),
                    ("l6_s1h", (128, 32)), ("l6_s2h", (128, 32)),
                    ("l6_s3", (128, 8)), ("l7_bout", (128, 8))]:
        sc[nm] = nc.dram_tensor(nm, list(shp), F32, kind="ExternalInput")
    xn = nc.dram_tensor("xn_scr", [D, TOK], BF16, kind="Internal").ap()
    U = nc.dram_tensor("u_scr", [D, TOK], F32, kind="Internal").ap()
    taps = [nc.dram_tensor(f"tap{i}", [D, TOK], F32, kind="Internal").ap()
            for i in range(1, 8)]
    rscr = nc.dram_tensor("r_scr", [D, TOK], BF16, kind="Internal").ap()
    cch = {}
    for nm in ("l1wq", "l1wk", "l1wv", "l1wo", "l3wq", "l3wk", "l3wv",
               "l3wo", "l5wq", "l5wk", "l5wv", "l7wq", "l7wk", "l7wv",
               "l7wo"):
        cch[nm] = nc.dram_tensor(nm + "B", [1024, 1024], BF16,
                                 kind="Internal").ap()
    cch["a2"] = nc.dram_tensor("a2B", [16 * 128, 3968], BF16,
                               kind="Internal").ap()
    cch["dc"] = nc.dram_tensor("dcB", [16 * 128, 3968], BF16,
                               kind="Internal").ap()
    cch["pkv"] = nc.dram_tensor("pkvB", [128, 1024], BF16,
                                kind="Internal").ap()
    cch["fc1"] = nc.dram_tensor("fc1B", [32 * 128, 1024], BF16,
                                kind="Internal").ap()
    cch["fc2"] = nc.dram_tensor("fc2B", [8 * 128, 4096], BF16,
                                kind="Internal").ap()
    for nm, shp in [("w1", (32 * 128, 1024)), ("w2", (32 * 128, 1024)),
                    ("h1", (32 * 128, 4096)), ("h2", (32 * 128, 4096)),
                    ("w3", (8 * 128, 4096))]:
        cch[nm] = nc.dram_tensor(nm + "B", list(shp), BF16,
                                 kind="Internal").ap()

    ctx = Ctx(b, ntmp=2, with_prep_b=False)
    alloc_fused(ctx)
    ctx.init_const_tiles(idn)
    ctx.load_consts({nm: (sc[nm].ap()[:, :], list(sc[nm].shape), F32)
                     for nm in sc})
    b.wait("vector", "ld", b.val("ld"))
    t_ = b.sbuf("c_l2_b2h", (128, 8), F32)
    ctx.consts["l2_b2h"] = t_
    b.emit("vector", lambda E, d=t_, s=ctx.consts["l2_b2"]:
           E.tensor_scalar_mul(d[:], s[:], 0.5), inc="v")
    for eng in ("scalar", "tensor"):
        b.wait(eng, "v", b.val("v"))

    ctx.prep_cw = PREP_CW
    ctx.prep_f = [b.sbuf(f"bgpf{i}", (128, PREP_CW), F32) for i in range(2)]
    ctx.prep_b = [b.sbuf(f"bgpb{i}", (128, PREP_CW), BF16) for i in range(2)]
    ctx.prep_f2 = [b.sbuf(f"bgpf2_{i}", (128, PREP_CW), F32)
                   for i in range(2)]
    ctx.gp = Guard(b)
    mst = {}
    for nm in ("l1wq", "l1wk", "l1wv", "l1wo"):
        prep_cast(ctx, ext[nm].ap().rearrange("m p d -> (m p) d"),
                  cch[nm], 1024, 1024)
    prep_cast(ctx, ext["a2f"].ap()[:, :], cch["a2"], 16 * 128, 3968)
    mst["sl1"] = b.val("pst")
    prep_dequant(ctx, ext["fc1q"].ap().rearrange("m p d -> (m p) d"),
                 ext["fc1s"].ap().rearrange("m p d -> (m p) d"), cch["fc1"],
                 32 * 128, 1024)
    prep_dequant(ctx, ext["fc2q"].ap().rearrange("m p d -> (m p) d"),
                 ext["fc2s"].ap().rearrange("m p d -> (m p) d"),
                 cch["fc2"], 8 * 128, 4096)
    mst["sl2"] = b.val("pst")
    for nm in ("l3wq", "l3wk", "l3wv", "l3wo"):
        prep_cast(ctx, ext[nm].ap().rearrange("m p d -> (m p) d"),
                  cch[nm], 1024, 1024)
    mst["sl3"] = b.val("pst")
    for nm in ("l5wq", "l5wk", "l5wv"):
        prep_cast(ctx, ext[nm].ap().rearrange("m p d -> (m p) d"),
                  cch[nm], 1024, 1024)
    prep_cast(ctx, ext["pkvf"].ap()[:, :], cch["pkv"], 128, 1024)
    prep_cast(ctx, ext["dcf"].ap()[:, :], cch["dc"], 16 * 128, 3968)
    mst["sl5"] = b.val("pst")
    for nm, shp in [("w1", (32, 128, 1024)), ("w2", (32, 128, 1024)),
                    ("h1", (32, 128, 4096)), ("h2", (32, 128, 4096)),
                    ("w3", (8, 128, 4096))]:
        prep_cast(ctx, ext[nm].ap().rearrange("m p d -> (m p) d"),
                  cch[nm], shp[0] * 128, shp[2])
    mst["sl6"] = b.val("pst")
    for nm in ("l7wq", "l7wk", "l7wv", "l7wo"):
        prep_cast(ctx, ext[nm].ap().rearrange("m p d -> (m p) d"),
                  cch[nm], 1024, 1024)
    mst["sl7"] = b.val("pst")

    Wl1 = {"wq": cch["l1wq"], "wk": cch["l1wk"], "wv": cch["l1wv"],
           "wo": cch["l1wo"]}
    Wl2 = {"fc1": cch["fc1"], "fc2": cch["fc2"]}
    Wl3 = {"wq": cch["l3wq"], "wk": cch["l3wk"], "wv": cch["l3wv"],
           "wo": cch["l3wo"]}
    Wl5 = {"wq": cch["l5wq"], "wk": cch["l5wk"], "wv": cch["l5wv"],
           "pkv": cch["pkv"]}
    Wl6 = {nm: cch[nm] for nm in ("w1", "w2", "h1", "h2", "w3")}
    Wl7 = {"wq": cch["l7wq"], "wk": cch["l7wk"], "wv": cch["l7wv"],
           "wo": cch["l7wo"]}

    set_ctx_chunks(ctx, "tabs")
    ln_st = ln_phase(ctx, x_in, xn, u_dram=U,
                     aff=dict(gamma="l1_g", beta="l1_b", eps2=1e-6))
    b.wait("sync", "pst", mst["sl1"])
    l0attna_run(ctx, x_in, xn, U, taps[0], Wl1, ln_st, cch["a2"])
    sublayer_boundary(ctx)

    ln_st = ln_phase(ctx, taps[0], xn, u_dram=U,
                     aff=dict(gamma="l2_g", beta="l2_b", eps2=1e-5))
    b.wait("sync", "pst", mst["sl2"])
    l0ffn(ctx, taps[0], xn, U, taps[1], Wl2, ln_st)
    sublayer_boundary(ctx)

    set_ctx_chunks(ctx, "lo")
    ln_st = ln_phase(ctx, taps[1], xn)
    b.wait("sync", "pst", mst["sl3"])
    l0attnb_run(ctx, taps[1], xn, taps[2], Wl3, ln_st)
    sublayer_boundary(ctx)

    ln_st = ln_phase(ctx, taps[2], xn, u_dram=U,
                     aff=dict(gamma="l2_g", beta="l2_b", eps2=1e-5))
    l0ffn(ctx, taps[2], xn, U, taps[3], Wl2, ln_st)
    sublayer_boundary(ctx)

    ln_st = ln_phase(ctx, taps[3], xn)
    b.wait("sync", "pst", mst["sl5"])
    l1aa_run(ctx, taps[3], xn, taps[4], Wl5, ln_st, cch["dc"], rscr)
    sublayer_boundary(ctx)

    ln_st = ln_phase(ctx, taps[4], xn)
    b.wait("sync", "pst", mst["sl6"])
    l1ffn(ctx, taps[4], xn, taps[5], Wl6, ln_st)
    sublayer_boundary(ctx)

    set_ctx_chunks(ctx, "kf")
    ln_st = ln_phase(ctx, taps[5], xn)
    b.wait("sync", "pst", mst["sl7"])
    l1ab_run(ctx, taps[5], xn, taps[6], Wl7, ln_st)
    sublayer_boundary(ctx)

    ln_st = ln_phase(ctx, taps[6], xn)
    l1ffn(ctx, taps[6], xn, y_out, Wl6, ln_st)
    sublayer_boundary(ctx)
    b.finish()
    return nc




_prog_cache = {}


def _get_prog():
    if "fused" not in _prog_cache:
        nc = bass.Bass("TRN2", target_bir_lowering=False)
        _prog_cache["fused"] = build_fused(nc)
    return _prog_cache["fused"]


def kernel(**inputs):
    x = inputs["L0_attn_a_hidden_states"]
    idn = np.eye(128, dtype=np.float32)
    xs = [np.ascontiguousarray(x[2 * c:2 * c + 2].reshape(TOK, D).T)
          for c in range(8)]

    past_kv = inputs["L1_attn_a_past_kv"]
    pkvf = np.zeros((128, 1024), np.float32)
    for h in range(16):
        pkvf[0:64, h * 64:(h + 1) * 64] = past_kv[h]
        pkvf[64:128, h * 64:(h + 1) * 64] = past_kv[h]
    gn_w, gn_b = inputs["L1_attn_a_gn_weight"], inputs["L1_attn_a_gn_bias"]
    gnc = np.zeros((128, 8), np.float32)
    gnbc = np.zeros((128, 8), np.float32)
    for c in range(8):
        gnc[0:64, c] = gn_w[2 * c]
        gnc[64:128, c] = gn_w[2 * c + 1]
        gnbc[0:64, c] = gn_b[2 * c]
        gnbc[64:128, c] = gn_b[2 * c + 1]
    Wqkv3 = inputs["L0_attn_b_W_qkv"]
    bq3 = inputs["L0_attn_b_b_qkv"]
    Wqkv5 = inputs["L1_attn_a_W_qkv"]
    Wqkv7 = inputs["L1_attn_b_W_qkv"]
    win = {
        "idn": idn,
        "l1wqP": lhsT_pack(inputs["L0_attn_a_Wq"]),
        "l1wkP": lhsT_pack(inputs["L0_attn_a_Wk"]),
        "l1wvP": lhsT_pack(inputs["L0_attn_a_Wv"]),
        "l1woP": lhsT_pack(inputs["L0_attn_a_Wo"]),
        "a2f": build_a2(inputs["L0_attn_a_rel_bias"]),
        "l1_g": col_pack(inputs["L0_attn_a_ln_g"]),
        "l1_b": col_pack(inputs["L0_attn_a_ln_b"]),
        "fc1q": lhsT_pack(inputs["L0_ffn_fc1_q"]),
        "fc2q": lhsT_pack(inputs["L0_ffn_fc2_q"]),
        "fc1s": lhsT_pack(np.repeat(inputs["L0_ffn_fc1_scales"], 128, axis=0)),
        "fc2s": lhsT_pack(np.repeat(inputs["L0_ffn_fc2_scales"], 128, axis=0)),
        "l2_g": col_pack(inputs["L0_ffn_ln_gamma"]),
        "l2_b": col_pack(inputs["L0_ffn_ln_beta"]),
        "l2_b1": col_pack(inputs["L0_ffn_fc1_bias"]),
        "l2_b2": col_pack(inputs["L0_ffn_fc2_bias"]),
        "l3wqP": lhsT_pack(Wqkv3[:, 0:1024]),
        "l3wkP": lhsT_pack(Wqkv3[:, 1024:2048]),
        "l3wvP": lhsT_pack(Wqkv3[:, 2048:3072]),
        "l3woP": lhsT_pack(inputs["L0_attn_b_W_proj"]),
        "l3_bqkv": np.concatenate([col_pack(bq3[0:1024]),
                                   col_pack(bq3[1024:2048]),
                                   col_pack(bq3[2048:3072])], axis=1),
        "l3_qkn": np.concatenate([inputs["L0_attn_b_qn_w"],
                                  inputs["L0_attn_b_kn_w"]]).reshape(128, 1),
        "l3_bproj": col_pack(inputs["L0_attn_b_b_proj"]),
        "l5wqP": lhsT_pack(Wqkv5[:, 0:1024]),
        "l5wkP": lhsT_pack(Wqkv5[:, 1024:2048]),
        "l5wvP": lhsT_pack(Wqkv5[:, 2048:3072]),
        "dcf": build_decay(), "pkvf": pkvf,
        "l5_gnc": gnc, "l5_gnbc": gnbc,
        "w1P": lhsT_pack(inputs["L1_ffn_w1_w"].T),
        "w2P": lhsT_pack(inputs["L1_ffn_w2_w"].T),
        "h1P": lhsT_pack(inputs["L1_ffn_h1_w"].T),
        "h2P": lhsT_pack(inputs["L1_ffn_h2_w"].T),
        "w3P": lhsT_pack(inputs["L1_ffn_w3_w"].T),
        "l6_s1": col_pack(inputs["L1_ffn_w1_s"]),
        "l6_s2": col_pack(inputs["L1_ffn_w2_s"]),
        "l6_s1h": col_pack(inputs["L1_ffn_h1_s"]),
        "l6_s2h": col_pack(inputs["L1_ffn_h2_s"]),
        "l6_s3": col_pack(inputs["L1_ffn_w3_s"]),
        "l7wqP": lhsT_pack(Wqkv7[0:1024].T),
        "l7wkP": lhsT_pack(Wqkv7[1024:2048].T),
        "l7wvP": lhsT_pack(Wqkv7[2048:3072].T),
        "l7woP": lhsT_pack(inputs["L1_attn_b_W_out"].T),
        "l7_bout": col_pack(inputs["L1_attn_b_b_out"]),
    }
    win = {k: np.ascontiguousarray(v) for k, v in win.items()}

    prog = _get_prog()
    inms = [dict(win, x=xs[c]) for c in range(8)]
    res = run_bass_kernel_spmd(prog, inms, core_ids=list(range(8)))
    out = np.empty((16, 2048, 1024), np.float32)
    for c in range(8):
        out[2 * c:2 * c + 2] = np.asarray(res.results[c]["y"]).T.reshape(2, SEQ, D)
    return out
